# version 14
# baseline (speedup 1.0000x reference)
import os
import math
import contextlib
import numpy as np
import ml_dtypes
import concourse.bass as bass
import concourse.mybir as mybir
from concourse.bass_utils import run_bass_kernel_spmd

F32 = mybir.dt.float32
BF16 = mybir.dt.bfloat16
AF = mybir.ActivationFunctionType
ALU = mybir.AluOpType
AX = mybir.AxisListType

T = 4096
D = 1024
COLS = 4104
NDT = 8
HEADS = 8
DH = 64
WIN = 512
NW = T // WIN
INNER = 512
HID = 2730
HIDP = 2816
NJ = HIDP // 128
NF = 640
NFR = 513
S_ST = 16
QCH = 128
NCH = T // QCH
UPAD = 256
ULEN = UPAD + T

ENGS = ("PE", "ACT", "DVE", "POOL", "SP")
SECT = {"PE": "tensor", "ACT": "scalar", "DVE": "vector", "POOL": "gpsimd", "SP": "sync"}



class Trk:

    def __init__(self, nc, stack):
        self.nc = nc
        self.idx = {e: 0 for e in ENGS}
        self.qcnt = {e: 0 for e in ENGS}
        self.buf = {e: [] for e in ENGS}
        self.pubs = {e: [] for e in ENGS}
        self.pubsem = {e: stack.enter_context(nc.semaphore(f"pub{e}")) for e in ENGS}
        self.QK = 8
        self.qsems = {e: [stack.enter_context(nc.semaphore(f"q{e}{i}")) for i in range(self.QK)]
                      for e in ENGS}
        self.clock = {e: {} for e in ENGS}
        self.hist = {e: [(-1, {})] for e in ENGS}
        self.qsnap = {e: [] for e in ENGS}
        self.wr = {}
        self.rd = {}
        self.space = {}
        self.names = {}
        self.attach = {}
        self.nwaits = 0

    def reg(self, buf, name, space):
        self.space[id(buf)] = space
        self.names[id(buf)] = name
        return buf

    def _snap(self, e, opi):
        h = self.hist[e]
        for k in range(len(h) - 1, -1, -1):
            if h[k][0] <= opi:
                return h[k][1]
        return h[0][1]

    def _learn(self, e, d):
        c = self.clock[e]
        ch = False
        for k, v in d.items():
            if isinstance(v, dict):
                cur = c.get(k)
                if cur is None:
                    c[k] = dict(v)
                    ch = True
                else:
                    for s, pv in v.items():
                        if cur.get(s, -1) < pv:
                            cur[s] = pv
                            ch = True
            elif c.get(k, -1) < v:
                c[k] = v
                ch = True
        if ch:
            self.hist[e].append((self.idx[e], self._copyclock(c)))

    def _copyclock(self, c):
        return {k: (dict(v) if isinstance(v, dict) else v) for k, v in c.items()}

    def _learn_slot(self, e, key, slot, pidx):
        c = self.clock[e]
        cur = c.setdefault(key, {})
        if cur.get(slot, -1) < pidx:
            cur[slot] = pidx
            self.hist[e].append((self.idx[e], self._copyclock(c)))

    def _ensure(self, cons, key, pidx):
        if key[0] == "Q":
            slotv = self.clock[cons].get(key)
            slot = (pidx - 1) % self.QK
            if slotv is not None and slotv.get(slot, -1) >= pidx:
                return
        elif self.clock[cons].get(key, -1) >= pidx:
            return
        if key[0] == "Q":
            qe = key[1:]
            slot = (pidx - 1) % self.QK
            lap = (pidx - 1) // self.QK + 1
            self.buf[cons].append(("wait", self.qsems[qe][slot], 16 * lap))
            self.nwaits += 1
            snap = self.qsnap[qe][pidx - 1]
            self._learn(cons, dict(snap))
            self._learn_slot(cons, key, slot, pidx)
        else:
            pubs = self.pubs[key]
            pick = None
            for (pi, cum) in pubs:
                if pi >= pidx:
                    pick = (pi, cum)
                    break
            if pick is None:
                assert pidx < self.idx[key], (key, pidx, self.idx[key])
                cum = (pubs[-1][1] + 1) if pubs else 1
                pubs.append((pidx, cum))
                self.attach[(key, pidx)] = (self.pubsem[key], 1)
                pick = (pidx, cum)
            self.buf[cons].append(("wait", self.pubsem[key], pick[1]))
            self.nwaits += 1
            snap = self._snap(key, pick[0])
            self._learn(cons, dict(snap, **{key: pick[0]}))

    def _pre(self, eng, reads, writes):
        for b in reads:
            w = self.wr.get(id(b))
            if w and w != (eng, self.idx[eng]):
                self._ensure(eng, w[0], w[1])
            if self.space.get(id(b)) == "P":
                for rk, ri in list(self.rd.get(id(b), {}).items()):
                    if rk != eng:
                        self._ensure(eng, rk, ri)
        for b in writes:
            w = self.wr.get(id(b))
            if w:
                self._ensure(eng, w[0], w[1])
            for rk, ri in list(self.rd.get(id(b), {}).items()):
                if not (rk == eng and ri == self.idx[eng]):
                    self._ensure(eng, rk, ri)

    def _post_reads(self, key, opi, reads):
        for b in reads:
            d = self.rd.setdefault(id(b), {})
            if d.get(key, -1) < opi:
                d[key] = opi

    def _post_writes(self, key, opi, writes):
        for b in writes:
            self.wr[id(b)] = (key, opi)
            self.rd[id(b)] = {}

    def op(self, eng, fn, reads=(), writes=()):
        self._pre(eng, reads, writes)
        opi = self.idx[eng]
        self.buf[eng].append(("op", fn, opi))
        self.idx[eng] = opi + 1
        self._post_reads(eng, opi, reads)
        self._post_writes(eng, opi, writes)
        return opi

    def group(self, eng, fns, reads=(), writes=()):
        self._pre(eng, reads, writes)
        base = self.idx[eng]
        n = len(fns)
        for k, fn in enumerate(fns):
            self.buf[eng].append(("gop", fn, base, k == n - 1))
        self.idx[eng] = base + 1
        self._post_reads(eng, base, reads)
        self._post_writes(eng, base, writes)
        return base

    def dma(self, eng, fn, reads=(), writes=()):
        self._pre(eng, reads, writes)
        self.qcnt[eng] += 1
        qi = self.qcnt[eng]
        slot = (qi - 1) % self.QK
        if qi > self.QK:
            prev = qi - self.QK
            lapp = (prev - 1) // self.QK + 1
            self.buf[eng].append(("wait", self.qsems[eng][slot], 16 * lapp))
            snapp = self.qsnap[eng][prev - 1]
            self._learn(eng, dict(snapp))
            self._learn_slot(eng, "Q" + eng, slot, prev)
        self.qsnap[eng].append(self._copyclock(self.clock[eng]))
        self.buf[eng].append(("dma", fn, qi, slot))
        key = "Q" + eng
        self._post_reads(key, qi, reads)
        self._post_writes(key, qi, writes)
        return qi

    def wait_all_of(self, cons, key):
        if key[0] == "Q":
            qe = key[1:]
            n = self.qcnt[qe]
            for pidx in range(max(1, n - self.QK + 1), n + 1):
                self._ensure(cons, key, pidx)
        else:
            if self.idx[key] > 0:
                self._ensure(cons, key, self.idx[key] - 1)

    def finish(self, eng="SP"):
        for e in ENGS:
            if self.qcnt[e] > 0:
                self.wait_all_of(eng, "Q" + e)
        for e in ENGS:
            if e != eng and self.idx[e] > 0:
                self._ensure(eng, e, self.idx[e] - 1)

    def flush(self, block):
        for e in ENGS:
            entries = self.buf[e]
            if not entries:
                continue
            trk = self

            def run(h, _e=e, _entries=entries):
                for ent in _entries:
                    kind = ent[0]
                    if kind == "wait":
                        h.wait_ge(ent[1], ent[2])
                    elif kind == "op":
                        ins = ent[1](h)
                        a = trk.attach.get((_e, ent[2]))
                        if a is not None:
                            ins.then_inc(a[0], a[1])
                    elif kind == "gop":
                        ins = ent[1](h)
                        if ent[3]:
                            a = trk.attach.get((_e, ent[2]))
                            if a is not None:
                                ins.then_inc(a[0], a[1])
                    elif kind == "dma":
                        ins = ent[1](h)
                        ins.then_inc(trk.qsems[_e][ent[3]], 16)

            getattr(block, SECT[e])(run)



def build_constants():
    c = {}
    ee = np.arange(D, dtype=np.float64)
    ff = np.arange(NFR, dtype=np.float64)
    ang = 2.0 * np.pi * np.outer(ee, ff) / D
    c["dft_cr"] = np.cos(ang).astype(np.float32)
    c["dft_ci"] = (-np.sin(ang)).astype(np.float32)
    cf = np.full(NFR, 2.0 / D)
    cf[0] = 1.0 / D
    cf[NFR - 1] = 1.0 / D
    br = (cf[:, None] * np.cos(ang.T)).astype(np.float32)
    bi = (-cf[:, None] * np.sin(ang.T)).astype(np.float32)
    B = np.zeros((2 * NF, D), np.float32)
    B[0:NFR] = br
    B[NF:NF + NFR] = bi
    c["ifft_b"] = B

    hi = np.arange(0, DH, 2, dtype=np.float64)
    inv_freq = 1.0 / 10000.0 ** (hi / DH)
    t = np.arange(T, dtype=np.float64)
    freqs = t[:, None] * inv_freq[None, :]
    freqs = np.concatenate([freqs, freqs], -1)
    base = (hi + 0.4 * DH) / (1.4 * DH)
    power = (t - T // 2) / float(WIN // 2)
    scale = base[None, :] ** power[:, None]
    scale = np.concatenate([scale, scale], -1)
    cosf, sinf = np.cos(freqs), np.sin(freqs)
    for nm, arr in (("cs_q", cosf * scale), ("ss_q", sinf * scale),
                    ("cs_k", cosf / scale), ("ss_k", sinf / scale)):
        a = arr.T.astype(np.float32)
        c[nm] = np.concatenate([a, a], 0)

    R = np.zeros((128, 128), np.float32)
    for h0 in (0, 64):
        for f in range(32):
            R[h0 + f + 32, h0 + f] = -1.0
            R[h0 + f, h0 + f + 32] = 1.0
    c["rotm"] = R

    c["ident"] = np.eye(128, dtype=np.float32)
    c["l1"] = np.triu(np.ones((128, 128), np.float32))
    c["ones_col"] = np.ones((128, 1), np.float32)
    c["ones_row"] = np.ones((1, 128), np.float32)
    c["negbig"] = np.concatenate([np.full((128, 512), -1.0e38, np.float32),
                                  np.zeros((128, 512), np.float32)], axis=1)
    c["ones128"] = np.ones((128, 128), np.float32)
    c["jrev_row"] = np.arange(127, -1, -1, dtype=np.float32).reshape(1, 128)
    c["eps_ln"] = np.full((128, 1), 1e-5, np.float32)
    c["eps_hrr"] = np.full((128, 1), 1e-8, np.float32)

    NEG = np.float32(-1.0e38)
    mm = np.zeros((128, NF), np.float32)
    for oi in range(128):
        mm[oi, :oi] = NEG
        mm[oi, oi + 513:] = NEG
    c["mask_mid"] = mm
    for it in range(4):
        m0 = mm.copy()
        cut = 512 - it * 128
        m0[:, :cut] = NEG
        c[f"mask_w0_{it}"] = m0
    for jt in range(8):
        mt = np.zeros((128, WIN), np.float32)
        for p in range(128):
            jl = jt * 128 + p
            lo = max(0, jl - 512)
            hi_ = min(WIN, jl + 1)
            mt[p, :lo] = NEG
            mt[p, hi_:] = NEG
        c[f"maskT_{jt}"] = mt

    for i in range(4):
        sq = np.zeros((128, 16), np.float32)
        sq[0:64, 2 * i] = 1.0
        sq[64:128, 2 * i + 1] = 1.0
        c[f"sel_q{i}"] = sq
        sk = np.zeros((128, 16), np.float32)
        sk[0:64, 8 + 2 * i] = 1.0
        sk[64:128, 8 + 2 * i + 1] = 1.0
        c[f"sel_k{i}"] = sk
        rq = np.zeros((16, 128), np.float32)
        rq[2 * i, 0:64] = 1.0
        rq[2 * i + 1, 64:128] = 1.0
        c[f"rsel_q{i}"] = rq
        rk = np.zeros((16, 128), np.float32)
        rk[8 + 2 * i, 0:64] = 1.0
        rk[8 + 2 * i + 1, 64:128] = 1.0
        c[f"rsel_k{i}"] = rk
    return c


def host_layout(inputs):
    g = {}
    f32 = lambda a: np.ascontiguousarray(np.asarray(a, np.float32))
    wqkv = f32(inputs["L0_ffn_wqkv"])
    wo = f32(inputs["L0_ffn_wo"])
    w1 = f32(inputs["L0_ffn_w1"])
    w2 = f32(inputs["L0_ffn_w2"])
    b1 = f32(inputs["L0_ffn_b1"])
    b2 = f32(inputs["L0_ffn_b2"])
    lng = f32(inputs["L0_ffn_ln_g"])
    lnb = f32(inputs["L0_ffn_ln_b"])
    qs = f32(inputs["L0_ffn_q_scale"])
    ks = f32(inputs["L0_ffn_k_scale"])
    w1p = np.zeros((2, D, 2 * HIDP), np.float32)
    w1p[:, :, 0:HID] = w1[:, :, 0:HID]
    w1p[:, :, HIDP:HIDP + HID] = w1[:, :, HID:2 * HID]
    b1p = np.zeros((2, 2 * HIDP), np.float32)
    b1p[:, 0:HID] = b1[:, 0:HID]
    b1p[:, HIDP:HIDP + HID] = b1[:, HID:2 * HID]
    w2p = np.zeros((2, HIDP, D), np.float32)
    w2p[:, 0:HID, :] = w2
    g["wqkv"] = wqkv
    g["wo"] = wo
    g["w1dev"] = np.ascontiguousarray(
        w1p.reshape(2, NDT, 128, 2, NJ, 128).transpose(0, 4, 3, 2, 1, 5)
    ).reshape(2, NJ, 2, 128, 1024)
    g["w2p"] = w2p
    g["b1a_col"] = np.ascontiguousarray(b1p[:, :HIDP].reshape(2, NJ, 128).transpose(0, 2, 1))
    g["b1g_col"] = np.ascontiguousarray(b1p[:, HIDP:].reshape(2, NJ, 128).transpose(0, 2, 1))
    g["lng_col"] = np.ascontiguousarray(lng.reshape(2, NDT, 128).transpose(0, 2, 1))
    g["lnb_col"] = np.ascontiguousarray(lnb.reshape(2, NDT, 128).transpose(0, 2, 1))
    g["b2_col"] = np.ascontiguousarray(b2.reshape(2, NDT, 128).transpose(0, 2, 1))

    def gcol(a):
        return f32(np.concatenate([a, a])[:, None])

    qsr = np.concatenate([qs[:, 32:64], qs[:, 0:32]], axis=1)
    ksr = np.concatenate([ks[:, 32:64], ks[:, 0:32]], axis=1)
    g["qs_col"] = np.stack([gcol(qs[l]) for l in range(2)])
    g["qsr_col"] = np.stack([gcol(qsr[l]) for l in range(2)])
    g["ks_col"] = np.stack([gcol(ks[l]) for l in range(2)])
    g["ksr_col"] = np.stack([gcol(ksr[l]) for l in range(2)])
    g["Wq"] = f32(inputs["L0_attn_a_Wq"])
    g["Wk"] = f32(inputs["L0_attn_a_Wk"])
    g["Wv"] = f32(inputs["L0_attn_a_Wv"])
    g["bq"] = f32(inputs["L0_attn_a_bq"]).reshape(1, D)
    g["bk"] = f32(inputs["L0_attn_a_bk"]).reshape(1, D)
    g["bv"] = f32(inputs["L0_attn_a_bv"]).reshape(1, D)
    g["logA"] = f32(inputs["L0_attn_b_log_A_diag"])
    g["BwT"] = np.ascontiguousarray(f32(inputs["L0_attn_b_B_w"]).T)
    g["Cw"] = f32(inputs["L0_attn_b_C_w"])
    g["Dvec_col"] = np.ascontiguousarray(f32(inputs["L0_attn_b_D_vec"]).reshape(NDT, 128).T)
    return g



class MK:
    def __init__(self, consts, stages="afbf", dbg_taps=()):
        self.stages = [s for s in stages if s in "afb"]
        self.dbg_taps = list(dbg_taps)
        self.consts = consts
        self.stack = contextlib.ExitStack()
        nc = bass.Bass("TRN2", target_bir_lowering=False, debug=False)
        self.nc = nc
        self.t = Trk(nc, self.stack)
        self.din = {}
        self.dbg_out = {}
        self._declare_io()
        self._alloc()

    def _declare_io(self):
        nc = self.nc
        A = lambda name, shape: nc.dram_tensor(name, list(shape), F32, kind="ExternalInput")
        self.xin = A("xin", (D, T))
        for nm, shape in [
            ("wqkv", (2, D, 1536)), ("wo", (2, INNER, D)), ("w1dev", (2, NJ, 2, 128, 1024)),
            ("w2p", (2, HIDP, D)), ("b1a_col", (2, 128, NJ)), ("b1g_col", (2, 128, NJ)),
            ("lng_col", (2, 128, NDT)), ("lnb_col", (2, 128, NDT)), ("b2_col", (2, 128, NDT)),
            ("qs_col", (2, 128, 1)), ("qsr_col", (2, 128, 1)),
            ("ks_col", (2, 128, 1)), ("ksr_col", (2, 128, 1)),
            ("Wq", (D, D)), ("Wk", (D, D)), ("Wv", (D, D)),
            ("bq", (1, D)), ("bk", (1, D)), ("bv", (1, D)),
            ("logA", (D, S_ST)), ("BwT", (D, S_ST)), ("Cw", (D, S_ST)),
            ("Dvec_col", (128, NDT)),
        ]:
            self.din[nm] = A(nm, shape)
        for nm, arr in self.consts.items():
            self.din[nm] = A("c_" + nm, arr.shape)
        self.xout = nc.dram_tensor("xout", [D, T], F32, kind="ExternalOutput")
        I = lambda name, shape, dt=BF16: nc.dram_tensor(name, list(shape), dt, kind="Internal")
        self.d_xmod = self.t.reg(I("d_xmod", (NDT, 128, COLS)), "d_xmod", "D")
        self.d_ln3 = self.t.reg(I("d_ln3", (NDT, 128, COLS)), "d_ln3", "D")
        self.d_wf = self.t.reg(I("d_wf", (6, NDT, 128, NF)), "d_wf", "D")
        self.d_wT = self.t.reg(I("d_wT", (2, 5, 128, T)), "d_wT", "D")
        self.d_u = self.t.reg(I("d_u", (S_ST, ULEN)), "d_u", "D")
        self.d_k0 = self.t.reg(I("d_k0", (S_ST, 128, D)), "d_k0", "D")
        self.d_y = self.t.reg(I("d_y", (NDT, 128, T)), "d_y", "D")
        self.d_w1b = self.t.reg(I("d_w1b", (2, NJ, 2, 128, 1024)), "d_w1b", "D")
        self.d_w2b = self.t.reg(I("d_w2b", (2, NJ, 2, 128, 512)), "d_w2b", "D")
        self.d_tabsb = self.t.reg(I("d_tabsb", (4, 128, T)), "d_tabsb", "D")

    def _alloc(self):
        nc, t = self.nc, self.t
        SB = lambda nm, shape, dt: t.reg(nc.alloc_sbuf_tensor(nm, list(shape), dt), nm, "S")
        PS = lambda nm, shape, dt: t.reg(nc.alloc_psum_tensor(nm, list(shape), dt), nm, "P")
        self.XM = [SB(f"XM{i}", (128, COLS), BF16) for i in range(NDT)]
        self.XS = [SB(f"XS{i}", (128, COLS), BF16) for i in range(NDT)]
        self.G8 = [SB(f"G8_{i}", (128, 1536), BF16) for i in range(NDT)]
        self.WO8 = [SB(f"WO8_{i}", (64, D), BF16) for i in range(HEADS)]
        self.VT7 = SB("VT7", (128, 520), BF16)
        self.FS = SB("FS", (128, 1024), F32)
        self.WK = [SB(f"WK{i}", (128, T), BF16) for i in range(2)]
        self.SC = [SB(f"SC{i}", (128, 64), F32) for i in range(4)]
        self.RS = [SB(f"RS{i}", (1, 512), F32) for i in range(2)]
        self.NMR = SB("NMR", (1, 512), BF16)
        self.BIASA = SB("BIASA", (128, NJ), F32)
        self.BIASG = SB("BIASG", (128, NJ), F32)
        self.C16B = SB("C16B", (16, 512), BF16)
        self.ident_bf = SB("ident_bf", (128, 128), BF16)
        self.ident_f = SB("ident_f", (128, 128), F32)
        self.l1_bf = SB("l1_bf", (128, 128), BF16)
        self.rotm_bf = SB("rotm_bf", (128, 128), BF16)
        self.ones_col_bf = SB("ones_col_bf", (128, 1), BF16)
        self.ones_row_f = SB("ones_row_f", (1, 128), F32)
        self.ones128_f = SB("ones128_f", (128, 128), F32)
        self.jrev_row = SB("jrev_row", (1, 128), F32)
        self.eps_ln = SB("eps_ln", (128, 1), F32)
        self.eps_hrr = SB("eps_hrr", (128, 1), F32)
        self.masks = {}
        self.masks["mask_mid"] = SB("mask_mid", (128, NF), BF16)
        self.negbig_bf = SB("negbig_bf", (128, 1024), BF16)
        self.BCOLW = SB("BCOLW", (128, 4), BF16)
        self.BCOLM = SB("BCOLM", (128, 4), BF16)
        self.sel = {}
        for i in range(4):
            for p in ("sel_q", "sel_k"):
                self.sel[f"{p}{i}"] = SB(f"{p}{i}", (128, 16), BF16)
            for p in ("rsel_q", "rsel_k"):
                self.sel[f"{p}{i}"] = SB(f"{p}{i}", (16, 128), BF16)
        self.qs_col = SB("sb_qs_col", (128, 1), F32)
        self.qsr_col = SB("sb_qsr_col", (128, 1), F32)
        self.ks_col = SB("sb_ks_col", (128, 1), F32)
        self.ksr_col = SB("sb_ksr_col", (128, 1), F32)
        self.lng_col = SB("sb_lng_col", (128, NDT), F32)
        self.lnb_col = SB("sb_lnb_col", (128, NDT), F32)
        self.lgb_bf = SB("lgb_bf", (128, 16), BF16)
        self.b2_col = SB("sb_b2_col", (128, NDT), F32)
        self.b1a_col = SB("sb_b1a_col", (128, NJ), F32)
        self.b1g_col = SB("sb_b1g_col", (128, NJ), F32)
        self.Dvec_col = SB("sb_Dvec_col", (128, NDT), F32)
        self.bhat_col = SB("bhat_col", (128, 32), F32)
        self.bcol_bf = SB("bcol_bf", (128, 8), BF16)
        self.PB = [PS(f"PB{i}", (128, 512), F32) for i in range(5)]
        self.PW = PS("PW", (128, NF), F32)
        self.PT = PS("PT", (128, 1024), BF16)

    def es_slot(self, it):
        return [(self.XM[4], 0), (self.XM[6], 2048), (self.XM[4], 1024),
                (self.WK[0], 0)][it]

    def qbuf(self, h):
        return self.XM[0], h * 512
    def kbuf(self, h):
        return (self.XM[1] if h % 2 == 0 else self.XM[2]), (h // 2) * 1024
    def vtbuf(self, h):
        if h < 7:
            return self.XM[3], h * 520
        return self.VT7, 0
    def attnT_buf(self):
        return self.XM[4]
    def qraw_buf(self):
        return self.XM[4]
    def ofin_buf(self, h):
        return self.XM[5], h * 512
    def q2buf(self):
        return self.XM[7]
    def tabq(self, k):
        return self.XM[6], k * 512
    def maskT(self, jt):
        if jt < 4:
            return self.masks[f"maskT_{jt}"], 0
        return self.XM[6], 2048 + (jt - 4) * 512
    def mst(self, j):
        return self.XM[j // 8], (j % 8) * 512
    def prbuf(self):
        return self.XM[6]
    def w2buf(self, r):
        return self.XM[7], r * 1024
    def lgra(self):
        return self.XM[3]
    def lgrg(self):
        return self.XM[4]

    def cast_load(self, dram_ap, sb_f32_tmp, out_bf, shape_cols, eng="SP"):
        t = self.t
        t.dma(eng, lambda h: h.dma_start(sb_f32_tmp[:, 0:shape_cols], dram_ap), writes=[sb_f32_tmp])
        t.op("DVE", lambda h: h.tensor_copy(out_bf[:, 0:shape_cols], sb_f32_tmp[:, 0:shape_cols]),
             reads=[sb_f32_tmp], writes=[out_bf])

    def mm(self, psum_ap, pairs, psum_buf, reads, start=True, stop=True):
        t = self.t
        n = len(pairs)
        fns = []
        for k, (lt, rh) in enumerate(pairs):
            fns.append(lambda h, lt=lt, rh=rh, k=k: h.matmul(
                psum_ap, lt, rh, start=(start and k == 0), stop=(stop and k == n - 1)))
        t.group("PE", fns, reads=list(reads), writes=[psum_buf])

    def tap(self, name, bufs, cols=COLS, dt=None):
        nc, t = self.nc, self.t
        n = len(bufs)
        dtt = dt or BF16
        k = 0
        while f"dbg_{name}{k}" in self.dbg_out:
            k += 1
        nm = f"dbg_{name}{k}"
        o = nc.dram_tensor(nm, [n, 128, cols], dtt, kind="ExternalOutput")
        self.dbg_out[nm] = o
        for i, b in enumerate(bufs):
            t.dma("POOL", lambda h, i=i, b=b: h.dma_start(o[i, :, 0:cols], b[:, 0:cols]),
                  reads=[b])

    def build(self):
        t = self.t
        self.ffn_converted = False
        self.load_consts()
        if "f" in self.stages:
            for k, nm in enumerate(("cs_q", "ss_q", "cs_k", "ss_k")):
                t.dma("POOL", lambda h, k=k, nm=nm: h.dma_start(
                    self.d_tabsb[k], self.din[nm][:, :]), writes=[self.d_tabsb])
        self.load_x()
        for st in self.stages:
            if st == "a":
                self.attn_a()
            elif st == "f":
                self.ffn()
            elif st == "b":
                self.attn_b()
        self.write_out()
        t.finish("SP")
        block = self.stack.enter_context(self.nc.Block())
        t.flush(block)
        self.stack.close()
        return self.nc

    def load_consts(self):
        t, din = self.t, self.din

        def ldb(name, dst, rows, cols):
            t.dma("POOL", lambda h: h.dma_start(dst[0:rows, 0:cols], din[name][0:rows, 0:cols]),
                  writes=[dst])

        def ldf(name, dst, rows, cols):
            t.dma("SP", lambda h: h.dma_start(dst[0:rows, 0:cols], din[name][0:rows, 0:cols]),
                  writes=[dst])

        ldb("ident", self.ident_bf, 128, 128)
        ldf("ident", self.ident_f, 128, 128)
        ldb("l1", self.l1_bf, 128, 128)
        ldb("rotm", self.rotm_bf, 128, 128)
        ldb("ones_col", self.ones_col_bf, 128, 1)
        ldf("ones_row", self.ones_row_f, 1, 128)
        ldf("ones128", self.ones128_f, 128, 128)
        ldf("jrev_row", self.jrev_row, 1, 128)
        ldf("eps_ln", self.eps_ln, 128, 1)
        ldf("eps_hrr", self.eps_hrr, 128, 1)
        ldb("mask_mid", self.masks["mask_mid"], 128, NF)
        ldb("negbig", self.negbig_bf, 128, 1024)
        for nm, dst in self.sel.items():
            if nm.startswith("rsel"):
                ldb(nm, dst, 16, 128)
            else:
                ldb(nm, dst, 128, 16)
        t.dma("SP", lambda h: h.dma_start(self.Dvec_col[:, :], din["Dvec_col"][:, :]),
              writes=[self.Dvec_col])

    def load_layer_smalls(self, l):
        t, din = self.t, self.din
        for nm, dst in [("qs_col", self.qs_col), ("qsr_col", self.qsr_col),
                        ("ks_col", self.ks_col), ("ksr_col", self.ksr_col),
                        ("lng_col", self.lng_col), ("lnb_col", self.lnb_col),
                        ("b2_col", self.b2_col), ("b1a_col", self.b1a_col),
                        ("b1g_col", self.b1g_col)]:
            t.dma("SP", lambda h, nm=nm, dst=dst: h.dma_start(
                dst[:, :], din[nm][l]), writes=[dst])
        for dst in (self.qs_col, self.qsr_col):
            t.op("DVE", lambda h, dst=dst: h.tensor_scalar_mul(dst[:, :], dst[:, :], 8.0),
                 reads=[dst], writes=[dst])

    def load_x(self):
        t = self.t
        for dt_ in range(NDT):
            t.dma("POOL", lambda h, dt_=dt_: h.dma_start(
                self.XM[dt_][:, 1:T + 1], self.xin[dt_ * 128:(dt_ + 1) * 128, 0:T]),
                writes=[self.XM[dt_]])
            t.op("DVE", lambda h, dt_=dt_: h.memset(self.XM[dt_][:, 0:1], 0.0),
                 reads=[], writes=[self.XM[dt_]])
            t.op("DVE", lambda h, dt_=dt_: h.memset(self.XM[dt_][:, T + 1:COLS], 0.0),
                 reads=[], writes=[self.XM[dt_]])

    def row_to_col(self, row_f32, col_out, psum, ncols=32):
        t = self.t
        fns = []
        for ch in range(ncols):
            fns.append(lambda h, ch=ch: h.transpose(
                psum[:, ch:ch + 1], row_f32[0:1, ch * 128:(ch + 1) * 128], self.ident_f[:, :]))
        t.group("PE", fns, reads=[row_f32, self.ident_f], writes=[psum])
        t.op("DVE", lambda h: h.tensor_copy(col_out[:, 0:ncols], psum[:, 0:ncols]),
             reads=[psum], writes=[col_out])

    def col_to_row(self, col_f32, row_out, psum, ncols=32):
        t = self.t
        for blk in range(0, ncols, 4):
            nb = min(4, ncols - blk)
            fns = []
            for ch in range(blk, blk + nb):
                fns.append(lambda h, ch=ch: h.transpose(
                    psum[0:1, (ch % 4) * 128:(ch % 4) * 128 + 128],
                    col_f32[:, ch:ch + 1], self.ident_f[:, :]))
            t.group("PE", fns, reads=[col_f32, self.ident_f], writes=[psum])
            t.op("ACT", lambda h, blk=blk, nb=nb: h.copy(
                row_out[0:1, blk * 128:(blk + nb) * 128], psum[0:1, 0:nb * 128]),
                reads=[psum], writes=[row_out])

    def ln_stats(self, XG, shifted=False, sq_scratch=None):
        t = self.t
        PBa = self.PB[0]
        sumc, sqc, muc, rstdc = self.SC[0], self.SC[1], self.SC[2], self.SC[3]
        sq_tiles = sq_scratch

        def src_ap(dt_, c0, n):
            off = -1 if (shifted and dt_ >= 4) else 0
            return XG[dt_][:, c0 + off:c0 + off + n]

        for dt_ in range(NDT):
            t.op("ACT", lambda h, dt_=dt_: h.activation(
                sq_tiles[dt_][:, 0:T], src_ap(dt_, 1, T), AF.Square),
                reads=[XG[dt_]], writes=[sq_tiles[dt_]])
        for tc in range(NW):
            c0 = 1 + tc * 512
            self.mm(PBa[0:1, 0:512],
                    [(self.ones_col_bf[:, :], src_ap(dt_, c0, 512)) for dt_ in range(NDT)],
                    PBa, reads=list(XG) + [self.ones_col_bf])
            t.op("ACT", lambda h: h.copy(self.RS[0][0:1, 0:512], PBa[0:1, 0:512]),
                 reads=[PBa], writes=[self.RS[0]])
            self.row_to_col4(self.RS[0], sumc, tc * 4, self.PB[1])
            self.mm(PBa[0:1, 0:512],
                    [(self.ones_col_bf[:, :], sq_tiles[dt_][:, tc * 512:tc * 512 + 512])
                     for dt_ in range(NDT)],
                    PBa, reads=list(sq_tiles) + [self.ones_col_bf])
            t.op("ACT", lambda h: h.copy(self.RS[0][0:1, 0:512], PBa[0:1, 0:512]),
                 reads=[PBa], writes=[self.RS[0]])
            self.row_to_col4(self.RS[0], sqc, tc * 4, self.PB[1])
        t.op("DVE", lambda h: h.tensor_scalar_mul(muc[:, 0:32], sumc[:, 0:32], 1.0 / D),
             reads=[sumc], writes=[muc])
        t.op("DVE", lambda h: h.tensor_scalar_mul(sqc[:, 0:32], sqc[:, 0:32], 1.0 / D),
             reads=[sqc], writes=[sqc])
        t.op("DVE", lambda h: h.tensor_mul(sumc[:, 0:32], muc[:, 0:32], muc[:, 0:32]),
             reads=[muc], writes=[sumc])
        t.op("DVE", lambda h: h.tensor_sub(sqc[:, 0:32], sqc[:, 0:32], sumc[:, 0:32]),
             reads=[sqc, sumc], writes=[sqc])
        t.op("ACT", lambda h: h.activation(sqc[:, 0:32], sqc[:, 0:32], AF.Ln,
                                           bias=self.eps_ln[:, :]),
             reads=[sqc, self.eps_ln], writes=[sqc])
        t.op("ACT", lambda h: h.activation(rstdc[:, 0:32], sqc[:, 0:32], AF.Exp, scale=-0.5),
             reads=[sqc], writes=[rstdc])
        t.op("DVE", lambda h: h.tensor_mul(muc[:, 0:32], muc[:, 0:32], rstdc[:, 0:32]),
             reads=[muc, rstdc], writes=[muc])

    def row_to_col4(self, row_f32, col_out, cidx, psum):
        t = self.t
        fns = []
        for k in range(4):
            fns.append(lambda h, k=k: h.transpose(
                psum[:, k:k + 1], row_f32[0:1, k * 128:(k + 1) * 128], self.ident_f[0:1, 0:1]))
        t.group("PE", fns, reads=[row_f32, self.ident_f], writes=[psum])
        t.op("DVE", lambda h: h.tensor_copy(col_out[:, cidx:cidx + 4], psum[:, 0:4]),
             reads=[psum], writes=[col_out])

    def ln_apply(self, XG, OUT):
        t = self.t
        arep, brep = self.WK[0], self.WK[1]
        for tc in range(NW):
            c0 = tc * 512
            if tc % 2 == 0:
                ra, roa, rb, rob = self.RS[0], 0, self.RS[1], 0
                pca, pcb, pba, pbb = self.PB[2], self.PB[3], self.PB[0], self.PB[1]
            else:
                ra, roa, rb, rob = self.FS, 0, self.FS, 512
                pca, pcb, pba, pbb = self.PB[4], self.PW, self.PB[4], self.PW
            self.cols_to_row2(self.SC[3], tc * 4, 4, ra, roa, pca)
            self.mm(pba[0:128, 0:512],
                    [(self.ones_row_f[0:1, 0:128], ra[0:1, roa:roa + 512])],
                    pba, reads=[self.ones_row_f, ra])
            t.op("DVE", lambda h, c0=c0, pba=pba: h.tensor_copy(
                arep[:, c0:c0 + 512], pba[:, 0:512]),
                reads=[pba], writes=[arep])
            self.cols_to_row2(self.SC[2], tc * 4, 4, rb, rob, pcb)
            t.op("DVE", lambda h, rb=rb, rob=rob: h.tensor_scalar_mul(
                rb[0:1, rob:rob + 512], rb[0:1, rob:rob + 512], -1.0),
                reads=[rb], writes=[rb])
            self.mm(pbb[0:128, 0:512],
                    [(self.ones_row_f[0:1, 0:128], rb[0:1, rob:rob + 512])],
                    pbb, reads=[self.ones_row_f, rb])
            t.op("DVE", lambda h, c0=c0, pbb=pbb: h.tensor_copy(
                brep[:, c0:c0 + 512], pbb[:, 0:512]),
                reads=[pbb], writes=[brep])
        for dt_ in range(NDT):
            t.op("DVE", lambda h, dt_=dt_: h.tensor_mul(
                OUT[dt_][:, 1:T + 1], XG[dt_][:, 1:T + 1], arep[:, 0:T]),
                reads=[XG[dt_], arep], writes=[OUT[dt_]])
            t.op("DVE", lambda h, dt_=dt_: h.tensor_add(
                OUT[dt_][:, 1:T + 1], OUT[dt_][:, 1:T + 1], brep[:, 0:T]),
                reads=[OUT[dt_], brep], writes=[OUT[dt_]])
            t.op("DVE", lambda h, dt_=dt_: h.memset(OUT[dt_][:, 0:1], 0.0),
                 reads=[], writes=[OUT[dt_]])
            t.op("DVE", lambda h, dt_=dt_: h.memset(OUT[dt_][:, T + 1:COLS], 0.0),
                 reads=[], writes=[OUT[dt_]])

    def spill(self, XG, dram):
        t = self.t
        for dt_ in range(NDT):
            t.dma("ACT", lambda h, dt_=dt_: h.dma_start(
                dram[dt_, :, :], XG[dt_][:, :]), reads=[XG[dt_]], writes=[dram])

    def restore_add(self, XG, dram, OTHER):
        t = self.t
        f = self.WK[1]
        for dt_ in range(NDT):
            t.dma("ACT", lambda h, dt_=dt_: h.dma_start(
                f[:, 0:T], dram[dt_, :, 1:T + 1]), reads=[dram], writes=[f])
            t.op("DVE", lambda h, dt_=dt_: h.tensor_add(
                XG[dt_][:, 1:T + 1], OTHER[dt_][:, 1:T + 1], f[:, 0:T]),
                reads=[OTHER[dt_], f], writes=[XG[dt_]])
            t.op("DVE", lambda h, dt_=dt_: h.memset(XG[dt_][:, 0:1], 0.0),
                 reads=[], writes=[XG[dt_]])
            t.op("DVE", lambda h, dt_=dt_: h.memset(XG[dt_][:, T + 1:COLS], 0.0),
                 reads=[], writes=[XG[dt_]])

    def cols_to_row2(self, colbuf, cidx, n, rowt, rowoff, rowpsum):
        t = self.t
        fns = []
        for k in range(n):
            fns.append(lambda h, k=k: h.transpose(
                rowpsum[0:1, k * 128:(k + 1) * 128], colbuf[:, cidx + k:cidx + k + 1],
                self.ident_f[:, :]))
        t.group("PE", fns, reads=[colbuf, self.ident_f], writes=[rowpsum])
        t.op("ACT", lambda h, n=n: h.copy(rowt[0:1, rowoff:rowoff + n * 128],
                                          rowpsum[0:1, 0:n * 128]),
             reads=[rowpsum], writes=[rowt])

    def cols_to_row(self, colbuf, cidx, n, row_out, rowpsum):
        t = self.t
        fns = []
        for k in range(n):
            fns.append(lambda h, k=k: h.transpose(
                rowpsum[0:1, k * 128:(k + 1) * 128], colbuf[:, cidx + k:cidx + k + 1],
                self.ident_f[:, :]))
        t.group("PE", fns, reads=[colbuf, self.ident_f], writes=[rowpsum])
        t.op("ACT", lambda h, n=n: h.copy(row_out[0:1, 0:n * 128], rowpsum[0:1, 0:n * 128]),
             reads=[rowpsum], writes=[row_out])

    def bcast_row(self, row_f32, ncols, psum_out, po=128):
        self.mm(psum_out[0:po, 0:ncols],
                [(self.ones_row_f[0:1, 0:po], row_f32[0:1, 0:ncols])],
                psum_out, reads=[self.ones_row_f, row_f32])

    def ffn(self, first=True):
        t, din = self.t, self.din
        taps = os.environ.get("MK_TAPS", "")
        if not self.ffn_converted:
            self.ffn_converted = True
            for l in range(2):
                for j in range(NJ):
                    for ag in range(2):
                        t.dma("POOL", lambda h, l=l, j=j, ag=ag: h.dma_start(
                            self.d_w1b[l, j, ag], self.din["w1dev"][l, j, ag]),
                            writes=[self.d_w1b])
                    for grp in range(2):
                        t.dma("POOL", lambda h, l=l, j=j, grp=grp: h.dma_start(
                            self.d_w2b[l, j, grp],
                            self.din["w2p"][l, j * 128:(j + 1) * 128,
                                            grp * 512:(grp + 1) * 512]),
                            writes=[self.d_w2b])
        self.ln_stats(self.XM, shifted=False, sq_scratch=self.XS)
        self.ln_apply(self.XM, self.XS)
        if "ln" in taps:
            self.tap("ln", self.XS)
        self.spill(self.XM, self.d_xmod)
        for l in range(2):
            self.load_layer_smalls(l)
            self.ffn_layer(l, taps if l == 0 else "")
        self.restore_add(self.XM, self.d_xmod, self.XS)

    def ffn_layer(self, l, taps=""):
        t, din = self.t, self.din
        for dt_ in range(NDT):
            t.dma("POOL", lambda h, dt_=dt_: h.dma_start(
                self.G8[dt_][:, 0:1536], din["wqkv"][l, dt_ * 128:(dt_ + 1) * 128, :]),
                writes=[self.G8[dt_]])
        for hh in range(HEADS):
            t.dma("POOL", lambda h, hh=hh: h.dma_start(
                self.WO8[hh][0:64, :], din["wo"][l, hh * 64:(hh + 1) * 64, :]),
                writes=[self.WO8[hh]])
        for xb in (self.XM[0], self.XM[1], self.XM[2], self.XM[3], self.VT7):
            t.op("DVE", lambda h, xb=xb: h.memset(xb[:, :], 0.0), writes=[xb])
        for hh in range(HEADS):
            vb, vc = self.vtbuf(hh)
            for half in range(2):
                for jt in range(4):
                    c = vc + half * 260 + jt * 65 + 64
                    t.op("DVE", lambda h, vb=vb, c=c: h.memset(vb[:, c:c + 1], 1.0),
                         writes=[vb])
        for w in range(NW):
            self.ffn_window(l, w, taps if w == 0 else "")
        if "attn" in taps:
            self.tap("attn", self.XS)
        self.ffn_mlp(l)
        if "mlp" in taps:
            self.tap("mlp", self.XS)

    def ffn_window(self, l, w, taps=""):
        t, din = self.t, self.din
        c0 = 1 + w * WIN
        jts = list(range(4, 8)) if w == 0 else list(range(8))
        for k in range(4):
            tb, tcc = self.tabq(k)
            t.dma("SP" if k % 2 == 0 else "ACT",
                  lambda h, k=k, tb=tb, tcc=tcc: h.dma_start(
                      tb[:, tcc:tcc + 512], self.d_tabsb[k, :, w * WIN:(w + 1) * WIN]),
                  reads=[self.d_tabsb], writes=[tb])
        if w > 0:
            for hh in range(HEADS):
                kb, kc = self.kbuf(hh)
                t.op("DVE", lambda h, kb=kb, kc=kc: h.tensor_copy(
                    kb[:, kc:kc + 512], kb[:, kc + 512:kc + 1024]), reads=[kb], writes=[kb])
                vb, vc = self.vtbuf(hh)
                t.op("DVE", lambda h, vb=vb, vc=vc: h.tensor_copy(
                    vb[:, vc:vc + 260], vb[:, vc + 260:vc + 520]), reads=[vb], writes=[vb])
        qraw = self.qraw_buf()
        q2 = self.q2buf()
        vraw = self.WK[1]
        for ct in range(12):
            ps = self.PB[ct % 2]
            if w == 0:
                pairs = []
                for dt_ in range(NDT):
                    cs = c0 if dt_ < 4 else c0 - 1
                    pairs.append((self.G8[dt_][:, ct * 128:(ct + 1) * 128],
                                  self.XS[dt_][:, cs:cs + 512]))
                self.mm(ps[:, 0:512], pairs, ps, reads=self.G8 + self.XS)
            else:
                fns = []
                for k, dt_ in enumerate((0, 1, 2)):
                    fns.append(lambda h, ct=ct, dt_=dt_, ps=ps, k=k: h.matmul(
                        ps[:, 0:512], self.G8[dt_][:, ct * 128:(ct + 1) * 128],
                        self.XS[dt_][:, c0:c0 + 512], start=(k == 0), stop=False))
                for dt_ in range(4, 8):
                    fns.append(lambda h, ct=ct, dt_=dt_, ps=ps: h.matmul(
                        ps[:, 0:1], self.G8[dt_][:, ct * 128:(ct + 1) * 128],
                        self.BCOLW[:, dt_ - 4:dt_ - 3], start=False, stop=False))
                    fns.append(lambda h, ct=ct, dt_=dt_, ps=ps: h.matmul(
                        ps[:, 1:512], self.G8[dt_][:, ct * 128:(ct + 1) * 128],
                        self.XS[dt_][:, c0:c0 + 511], start=False, stop=False))
                fns.append(lambda h, ct=ct, ps=ps: h.matmul(
                    ps[:, 0:512], self.G8[3][:, ct * 128:(ct + 1) * 128],
                    self.XS[3][:, c0:c0 + 512], start=False, stop=True))
                t.group("PE", fns, reads=self.G8 + self.XS + [self.BCOLW], writes=[ps])
            if ct < 8:
                t.op("ACT", lambda h, ct=ct, ps=ps: h.copy(
                    qraw[:, ct * 512:(ct + 1) * 512], ps[:, 0:512]),
                    reads=[ps], writes=[qraw])
                t.op("ACT", lambda h, ct=ct, ps=ps: h.activation(
                    q2[:, ct * 512:(ct + 1) * 512], ps[:, 0:512],
                    AF.Square), reads=[ps], writes=[q2])
            else:
                t.op("ACT", lambda h, ct=ct, ps=ps: h.copy(
                    vraw[:, 2048 + (ct - 8) * 512:2048 + (ct - 7) * 512], ps[:, 0:512]),
                    reads=[ps], writes=[vraw])
        pairs = []
        rds = []
        for i in range(4):
            pairs.append((self.sel[f"sel_q{i}"][:, :], q2[:, i * 512:(i + 1) * 512]))
            rds.append(self.sel[f"sel_q{i}"])
        for i in range(4):
            pairs.append((self.sel[f"sel_k{i}"][:, :], q2[:, (4 + i) * 512:(5 + i) * 512]))
            rds.append(self.sel[f"sel_k{i}"])
        self.mm(self.PB[2][0:16, 0:512], pairs, self.PB[2], reads=rds + [q2])
        C16 = self.FS
        t.op("DVE", lambda h: h.tensor_scalar_max(C16[0:16, 0:512],
                                                  self.PB[2][0:16, 0:512], 1e-24),
             reads=[self.PB[2]], writes=[C16])
        t.op("ACT", lambda h: h.activation(C16[0:16, 0:512], C16[0:16, 0:512],
                                           AF.Ln), reads=[C16], writes=[C16])
        t.op("ACT", lambda h: h.activation(self.C16B[0:16, 0:512], C16[0:16, 0:512],
                                           AF.Exp, scale=-0.5), reads=[C16], writes=[self.C16B])
        for ct in range(8):
            isq = ct < 4
            i4 = ct % 4
            rsel = self.sel[("rsel_q%d" if isq else "rsel_k%d") % i4]
            prr = self.PB[3] if ct % 2 == 0 else self.PB[0]
            prot = self.PB[4] if ct % 2 == 0 else self.PB[1]
            buf = self.WK[0] if ct % 2 == 0 else self.XM[7]
            self.mm(prr[:, 0:512], [(rsel[:, :], self.C16B[0:16, 0:512])],
                    prr, reads=[rsel, self.C16B])
            qn1 = buf
            t.op("DVE", lambda h, ct=ct, qn1=qn1, prr=prr: h.tensor_mul(
                qn1[:, 0:512], qraw[:, ct * 512:(ct + 1) * 512], prr[:, 0:512]),
                reads=[qraw, prr], writes=[qn1])
            self.mm(prot[:, 0:512], [(self.rotm_bf[:, :], qn1[:, 0:512])],
                    prot, reads=[self.rotm_bf, qn1])
            tb_c, tcc_c = self.tabq(0 if isq else 2)
            tb_s, tcc_s = self.tabq(1 if isq else 3)
            t1, t2 = buf, buf
            gc = self.qs_col if isq else self.ks_col
            gcr = self.qsr_col if isq else self.ksr_col
            t.op("DVE", lambda h, tb_c=tb_c, tcc_c=tcc_c, gc=gc, qn1=qn1, t1=t1:
                 h.scalar_tensor_tensor(
                     t1[:, 512:1024], qn1[:, 0:512], gc[:, :], tb_c[:, tcc_c:tcc_c + 512],
                     ALU.mult, ALU.mult),
                 reads=[qn1, gc, tb_c], writes=[t1])
            t.op("DVE", lambda h, tb_s=tb_s, tcc_s=tcc_s, gcr=gcr, t2=t2, prot=prot:
                 h.scalar_tensor_tensor(
                     t2[:, 1024:1536], prot[:, 0:512], gcr[:, :], tb_s[:, tcc_s:tcc_s + 512],
                     ALU.mult, ALU.mult),
                 reads=[prot, gcr, tb_s], writes=[t2])
            for half in range(2):
                hh = 2 * i4 + half
                r0 = half * 64
                eng = "DVE" if half == 0 else "POOL"
                if isq:
                    qb, qc = self.qbuf(hh)
                    t.op(eng, lambda h, qb=qb, qc=qc, r0=r0, t1=t1, t2=t2: h.tensor_add(
                        qb[r0:r0 + 64, qc:qc + 512], t1[r0:r0 + 64, 512:1024],
                        t2[r0:r0 + 64, 1024:1536]), reads=[t1, t2], writes=[qb])
                else:
                    kb, kc = self.kbuf(hh)
                    t.op(eng, lambda h, kb=kb, kc=kc, r0=r0, t1=t1, t2=t2: h.tensor_add(
                        kb[r0:r0 + 64, kc + 512:kc + 1024], t1[r0:r0 + 64, 512:1024],
                        t2[r0:r0 + 64, 1024:1536]), reads=[t1, t2], writes=[kb])
        for hh in range(HEADS):
            vb, vc = self.vtbuf(hh)
            vr0 = (hh % 2) * 64
            vcol = (hh // 2) * 512
            fns = []
            for jt in range(4):
                fns.append(lambda h, jt=jt, vr0=vr0, vcol=vcol: h.transpose(
                    self.PT[:, jt * 64:jt * 64 + 64],
                    vraw[vr0:vr0 + 64, 2048 + vcol + jt * 128:2048 + vcol + (jt + 1) * 128],
                    self.ident_bf[vr0:vr0 + 64, vr0:vr0 + 64]))
            t.group("PE", fns, reads=[vraw, self.ident_bf], writes=[self.PT])
            for jt in range(4):
                c = vc + 260 + jt * 65
                t.op("DVE", lambda h, jt=jt, vb=vb, c=c: h.tensor_copy(
                    vb[:, c:c + 64], self.PT[:, jt * 64:jt * 64 + 64]),
                    reads=[self.PT], writes=[vb])
        if "qkv" in taps:
            self.tap("qkv_qraw", [qraw], cols=T)
            self.tap("qkv_v", [self.WK[1]], cols=T)
        ptail = None
        for hh in range(HEADS):
            ptail = self.ffn_head(l, w, hh, jts, prev_tail=ptail)
        ptail()
        if "qkv" in taps:
            self.tap("qbuf", [self.XM[0]], cols=T)
            self.tap("kbuf", [self.XM[1], self.XM[2]], cols=T)
        if "ofin" in taps:
            self.tap("ofin", [self.XM[5]], cols=T)
        for dt_ in range(4, 8):
            t.op("DVE", lambda h, dt_=dt_: h.tensor_copy(
                self.BCOLW[:, dt_ - 4:dt_ - 3], self.XS[dt_][:, c0 + 511:c0 + 512]),
                reads=[self.XS[dt_]], writes=[self.BCOLW])
        ofin = self.XM[5]
        for dt_ in range(NDT):
            ps = self.PB[dt_ % 2]
            pairs = []
            for hh in range(HEADS):
                pairs.append((self.WO8[hh][0:64, dt_ * 128:(dt_ + 1) * 128],
                              ofin[0:64, hh * 512:(hh + 1) * 512]))
            self.mm(ps[:, 0:512], pairs, ps,
                    reads=[self.WO8[hh2] for hh2 in range(HEADS)] + [ofin])
            t.op("DVE", lambda h, dt_=dt_, ps=ps, c0=c0: h.tensor_add(
                self.XS[dt_][:, c0:c0 + 512], self.XS[dt_][:, c0:c0 + 512], ps[:, 0:512]),
                reads=[ps, self.XS[dt_]], writes=[self.XS[dt_]])

    def ffn_head(self, l, w, hh, jts, prev_tail=None):
        t, din = self.t, self.din
        qb, qc = self.qbuf(hh)
        kb, kc = self.kbuf(hh)
        vb, vc = self.vtbuf(hh)
        PO = self.PB[4]

        def emit_scores(it):
            mask, mo = self.masks["mask_mid"], 0
            nblank = max(0, 512 - it * 128) if w == 0 else 0
            if it % 2 == 0:
                pslo, plc, pshi, phc = self.PW, 0, self.PW, 512
            else:
                pslo, plc, pshi, phc = self.PB[0], 0, self.PB[1], 0
            negm = self.SC[it]
            fns = [
                lambda h, it=it, pslo=pslo, plc=plc: h.matmul(
                    pslo[:, plc:plc + 512],
                    qb[:, qc + it * 128:qc + (it + 1) * 128],
                    kb[:, kc + it * 128:kc + it * 128 + 512],
                    start=True, stop=False),
                lambda h, it=it, pshi=pshi, phc=phc: h.matmul(
                    pshi[:, phc:phc + 128],
                    qb[:, qc + it * 128:qc + (it + 1) * 128],
                    kb[:, kc + it * 128 + 512:kc + it * 128 + 640],
                    start=True, stop=False),
                lambda h, mask=mask, mo=mo, pslo=pslo, plc=plc, nblank=nblank: h.matmul(
                    pslo[:, plc:plc + 512], self.ident_bf[:, :],
                    mask[:, mo:mo + 512], start=False, stop=(nblank == 0)),
                lambda h, mask=mask, mo=mo, pshi=pshi, phc=phc: h.matmul(
                    pshi[:, phc:phc + 128], self.ident_bf[:, :],
                    mask[:, mo + 512:mo + 640], start=False, stop=True),
            ]
            if nblank > 0:
                fns.append(lambda h, pslo=pslo, plc=plc, nblank=nblank: h.matmul(
                    pslo[:, plc:plc + 512], self.ident_bf[:, :],
                    self.negbig_bf[:, 512 - nblank:1024 - nblank],
                    start=False, stop=True))
            wl = [self.PW] if it % 2 == 0 else [self.PB[0], self.PB[1]]
            rl = [qb, kb, self.ident_bf, mask] + ([self.negbig_bf] if nblank else [])
            t.group("PE", fns, reads=rl, writes=wl)
            ES, eb = self.es_slot(it)
            if it % 2 == 0:
                t.op("DVE", lambda h, it=it, negm=negm: h.reduce_max(
                    negm[:, 0:1], self.PW[:, 0:640], axis=AX.X, negate=True),
                    reads=[self.PW], writes=[negm])
                t.op("ACT", lambda h, it=it, ES=ES, eb=eb, negm=negm: h.activation(
                    ES[:, eb:eb + 640], self.PW[:, 0:640], AF.Exp, bias=negm[:, 0:1]),
                    reads=[self.PW, negm], writes=[ES])
            else:
                t.op("DVE", lambda h, it=it, negm=negm: h.reduce_max(
                    negm[:, 0:1], self.PB[0][:, 0:512], axis=AX.X, negate=True),
                    reads=[self.PB[0]], writes=[negm])
                t.op("DVE", lambda h, it=it, negm=negm: h.reduce_max(
                    negm[:, 1:2], self.PB[1][:, 0:128], axis=AX.X, negate=True),
                    reads=[self.PB[1]], writes=[negm])
                t.op("DVE", lambda h, it=it, negm=negm: h.tensor_tensor(
                    negm[:, 0:1], negm[:, 0:1], negm[:, 1:2],
                    ALU.min), reads=[negm], writes=[negm])
                t.op("ACT", lambda h, it=it, ES=ES, eb=eb, negm=negm: h.activation(
                    ES[:, eb:eb + 512], self.PB[0][:, 0:512], AF.Exp, bias=negm[:, 0:1]),
                    reads=[self.PB[0], negm], writes=[ES])
                t.op("ACT", lambda h, it=it, ES=ES, eb=eb, negm=negm: h.activation(
                    ES[:, eb + 512:eb + 640], self.PB[1][:, 0:128], AF.Exp,
                    bias=negm[:, 0:1]),
                    reads=[self.PB[1], negm], writes=[ES])

        AT = self.XM[7]

        def emit_tr(it):
            ES, eb = self.es_slot(it)
            fns = []
            for jt in range(5):
                fns.append(lambda h, jt=jt, ES=ES, eb=eb: h.transpose(
                    self.PT[:, jt * 128:(jt + 1) * 128],
                    ES[:, eb + jt * 128:eb + (jt + 1) * 128], self.ident_bf[:, :]))
            t.group("PE", fns, reads=[ES, self.ident_bf], writes=[self.PT])
            t.op("ACT", lambda h, it=it: h.copy(
                bass.AP(AT, it * 640, [[COLS, 128], [512, 5], [1, 128]]),
                self.PT[:, 0:640]),
                reads=[self.PT], writes=[AT])

        def emit_av():
            fns = [lambda h: h.matmul(PO[0:65, 0:512], vb[:, vc:vc + 65],
                                      self.negbig_bf[:, 512:1024],
                                      start=True, stop=False)]
            for jg in range(8):
                i0 = max(0, jg - 4)
                i1 = min(3, jg) + 1
                half = 0 if jg < 4 else 1
                cc = vc + half * 260 + (jg % 4) * 65
                fns.append(lambda h, jg=jg, i0=i0, i1=i1, cc=cc: h.matmul(
                    PO[0:65, i0 * 128:i1 * 128], vb[:, cc:cc + 65],
                    AT[:, jg * 512 + i0 * 128:jg * 512 + i1 * 128],
                    start=False, stop=False))
            fns.append(lambda h: h.matmul(PO[0:65, 0:512], vb[:, vc:vc + 65],
                                          self.negbig_bf[:, 512:1024],
                                          start=False, stop=True))
            t.group("PE", fns, reads=[vb, AT, self.negbig_bf], writes=[PO])

        emit_scores(0)
        emit_scores(1)
        if prev_tail is not None:
            prev_tail()
        emit_scores(2)
        emit_tr(0)
        emit_scores(3)
        emit_tr(1)
        emit_tr(2)
        emit_tr(3)
        emit_av()

        def tail():
            zrow = self.WK[0]
            t.op("ACT", lambda h: h.activation(self.FS[64:65, 0:512], PO[64:65, 0:512],
                                               AF.Ln), reads=[PO], writes=[self.FS])
            t.op("ACT", lambda h: h.activation(zrow[64:65, 3584:4096],
                                               self.FS[64:65, 0:512],
                                               AF.Exp, scale=-1.0),
                 reads=[self.FS], writes=[zrow])
            zp = self.PB[2]
            self.mm(zp[0:64, 0:512],
                    [(self.l1_bf[64:65, 64:128], zrow[64:65, 3584:4096])],
                    zp, reads=[self.l1_bf, zrow])
            zrep = self.WK[0]
            t.op("ACT", lambda h, zp=zp: h.copy(zrep[0:64, 3072:3584], zp[0:64, 0:512]),
                 reads=[zp], writes=[zrep])
            ob, oc = self.ofin_buf(hh)
            t.op("DVE", lambda h, ob=ob, oc=oc: h.tensor_mul(
                ob[0:64, oc:oc + 512], PO[0:64, 0:512], zrep[0:64, 3072:3584]),
                reads=[PO, zrep], writes=[ob])
        return tail

    def ffn_mlp(self, l):
        t, din = self.t, self.din
        self.ln_stats(self.XS, shifted=True, sq_scratch=[self.XM[i] for i in range(NDT)])
        t.op("DVE", lambda h: h.tensor_copy(
            self.lgb_bf[:, 0:16:2], self.lnb_col[:, 0:8]), reads=[self.lnb_col],
            writes=[self.lgb_bf])
        t.op("DVE", lambda h: h.tensor_copy(
            self.lgb_bf[:, 1:16:2], self.lng_col[:, 0:8]), reads=[self.lng_col],
            writes=[self.lgb_bf])
        for tcp in range(4):
            self.ffn_mlp_tcpair(l, 2 * tcp, 2 * tcp + 1)

    def mstB(self, j):
        return [self.XM[3], self.XM[4], self.XM[5]][j // 8], (j % 8) * 512

    def ffn_mlp_tcpair(self, l, ta, tb):
        t, din = self.t, self.din
        first_tc = ta == 0
        Z3 = [self.WK[0], self.WK[1]]
        rrepb = self.G8[6]
        mrow, rrow = self.RS[1], self.RS[0]
        for idx, tc in ((0, ta), (1, tb)):
            c0 = 1 + tc * WIN
            z3 = Z3[idx]
            self.cols_to_row(self.SC[3], tc * 4, 4, rrow, self.PB[0])
            self.bcast_row(rrow, 512, self.PB[1])
            t.op("DVE", lambda h, idx=idx: h.tensor_copy(
                rrepb[:, idx * 512:idx * 512 + 512], self.PB[1][:, 0:512]),
                reads=[self.PB[1]], writes=[rrepb])
            self.cols_to_row(self.SC[2], tc * 4, 4, mrow, self.PB[0])
            t.op("DVE", lambda h: h.tensor_scalar_mul(mrow[0:1, 0:512],
                                                      mrow[0:1, 0:512], -1.0),
                 reads=[mrow], writes=[mrow])
            self.bcast_row(mrow, 512, self.PW)
            for dt_ in range(NDT):
                if dt_ < 4 or idx == 1 or tc == 0:
                    cs = c0 if dt_ < 4 else c0 - 1
                    t.op("DVE", lambda h, dt_=dt_, cs=cs, z3=z3, idx=idx:
                         h.scalar_tensor_tensor(
                             z3[:, dt_ * 512:(dt_ + 1) * 512], self.XS[dt_][:, cs:cs + 512],
                             self.lng_col[:, dt_:dt_ + 1],
                             rrepb[:, idx * 512:idx * 512 + 512], ALU.mult, ALU.mult),
                         reads=[self.XS[dt_], self.lng_col, rrepb], writes=[z3])
                else:
                    t.op("DVE", lambda h, dt_=dt_, z3=z3, idx=idx: h.scalar_tensor_tensor(
                        z3[:, dt_ * 512:dt_ * 512 + 1], self.BCOLM[:, dt_ - 4:dt_ - 3],
                        self.lng_col[:, dt_:dt_ + 1],
                        rrepb[:, idx * 512:idx * 512 + 1], ALU.mult, ALU.mult),
                        reads=[self.BCOLM, self.lng_col, rrepb], writes=[z3])
                    t.op("DVE", lambda h, dt_=dt_, z3=z3, idx=idx, c0=c0:
                         h.scalar_tensor_tensor(
                             z3[:, dt_ * 512 + 1:(dt_ + 1) * 512],
                             self.XS[dt_][:, c0:c0 + 511],
                             self.lng_col[:, dt_:dt_ + 1],
                             rrepb[:, idx * 512 + 1:idx * 512 + 512], ALU.mult, ALU.mult),
                         reads=[self.XS[dt_], self.lng_col, rrepb], writes=[z3])
                t.op("DVE", lambda h, dt_=dt_, z3=z3: h.scalar_tensor_tensor(
                    z3[:, dt_ * 512:(dt_ + 1) * 512], self.PW[:, 0:512],
                    self.lng_col[:, dt_:dt_ + 1], z3[:, dt_ * 512:(dt_ + 1) * 512],
                    ALU.mult, ALU.add),
                    reads=[self.PW, self.lng_col, z3], writes=[z3])
        GG = self.G8[7]
        for j in range(NJ):
            PR = self.XM[6] if j % 2 == 0 else self.XM[7]
            for ag in range(2):
                t.dma("POOL" if ag == 0 else "SP",
                      lambda h, j=j, ag=ag, PR=PR: h.dma_start(
                          PR[:, ag * 1024:(ag + 1) * 1024], self.d_w1b[l, j, ag]),
                      reads=[self.d_w1b], writes=[PR])
            if first_tc:
                for ag, (bias, b1c) in enumerate(((self.BIASA, self.b1a_col),
                                                  (self.BIASG, self.b1g_col))):
                    pc = self.PB[4]
                    pairs = [(PR[:, ag * 1024 + dt_ * 128:ag * 1024 + (dt_ + 1) * 128],
                              self.lgb_bf[:, 2 * dt_:2 * dt_ + 1])
                             for dt_ in range(NDT)]
                    self.mm(pc[0:128, 1:2], pairs, pc, reads=[self.lgb_bf, PR])
                    t.op("DVE", lambda h, j=j, bias=bias, b1c=b1c: h.scalar_tensor_tensor(
                        bias[:, j:j + 1], pc[0:128, 1:2], 1.0, b1c[:, j:j + 1],
                        ALU.mult, ALU.add), reads=[pc, b1c], writes=[bias])
            for ag in range(2):
                psa, psb = (self.PB[0], self.PB[1]) if ag == 0 else (self.PB[2], self.PB[3])
                fns = []
                for dt_ in range(NDT):
                    for k, ps in ((0, psa), (1, psb)):
                        fns.append(lambda h, ag=ag, dt_=dt_, ps=ps, PR=PR, k=k: h.matmul(
                            ps[:, 0:512],
                            PR[:, ag * 1024 + dt_ * 128:ag * 1024 + (dt_ + 1) * 128],
                            Z3[k][:, dt_ * 512:(dt_ + 1) * 512],
                            start=(dt_ == 0), stop=(dt_ == NDT - 1)))
                t.group("PE", fns, reads=[PR, Z3[0], Z3[1]], writes=[psa, psb])
            for k in range(2):
                psa = self.PB[0] if k == 0 else self.PB[1]
                psg = self.PB[2] if k == 0 else self.PB[3]
                t.op("ACT", lambda h, j=j, psg=psg, k=k: h.activation(
                    GG[:, k * 512:(k + 1) * 512], psg[:, 0:512], AF.Gelu,
                    bias=self.BIASG[:, j:j + 1]),
                    reads=[psg, self.BIASG], writes=[GG])
                mb, mc = self.mst(j) if k == 0 else self.mstB(j)
                t.op("DVE", lambda h, j=j, mb=mb, mc=mc, psa=psa, k=k:
                     h.scalar_tensor_tensor(
                         mb[:, mc:mc + 512], psa[:, 0:512], self.BIASA[:, j:j + 1],
                         GG[:, k * 512:(k + 1) * 512], ALU.add, ALU.mult),
                     reads=[psa, self.BIASA, GG], writes=[mb])
        for idx, tc in ((0, ta), (1, tb)):
            c0 = 1 + tc * WIN
            for grp in range(2):
                for j in range(NJ):
                    wb = self.XM[7] if j % 2 == 0 else self.XM[6]
                    t.dma("POOL" if j % 2 == 0 else "SP",
                          lambda h, j=j, grp=grp, wb=wb: h.dma_start(
                              wb[:, 0:512], self.d_w2b[l, j, grp]),
                          reads=[self.d_w2b], writes=[wb])
                    mb, mc = self.mst(j) if idx == 0 else self.mstB(j)
                    fns = []
                    pss = []
                    for k, dt_ in enumerate(range(grp * 4, grp * 4 + 4)):
                        ps = self.PB[k]
                        pss.append(ps)
                        fns.append(lambda h, j=j, dt_=dt_, ps=ps, wb=wb, mb=mb, mc=mc,
                                   grp=grp:
                                   h.matmul(ps[:, 0:512],
                                            wb[:, (dt_ - grp * 4) * 128:
                                               (dt_ - grp * 4 + 1) * 128],
                                            mb[:, mc:mc + 512],
                                            start=(j == 0), stop=(j == NJ - 1)))
                    t.group("PE", fns, reads=[wb, mb], writes=pss)
                if grp == 1 and idx == 1:
                    for dt_ in range(4, 8):
                        t.op("DVE", lambda h, dt_=dt_, c0=c0: h.tensor_copy(
                            self.BCOLM[:, dt_ - 4:dt_ - 3],
                            self.XS[dt_][:, c0 + 511:c0 + 512]),
                            reads=[self.XS[dt_]], writes=[self.BCOLM])
                for k, dt_ in enumerate(range(grp * 4, grp * 4 + 4)):
                    ps = self.PB[k]
                    t.op("DVE", lambda h, dt_=dt_, ps=ps, c0=c0: h.scalar_tensor_tensor(
                        self.XS[dt_][:, c0:c0 + 512], ps[:, 0:512],
                        self.b2_col[:, dt_:dt_ + 1], self.XS[dt_][:, c0:c0 + 512],
                        ALU.add, ALU.add), reads=[ps, self.b2_col, self.XS[dt_]],
                        writes=[self.XS[dt_]])

    def attn_a(self):
        t, din = self.t, self.din
        taps = os.environ.get("MK_TAPS", "")
        self.ln_stats(self.XM, shifted=False, sq_scratch=self.XS)
        self.ln_apply(self.XM, self.XS)
        if "aln" in taps:
            self.tap("aln", self.XS)
        self.spill(self.XM, self.d_xmod)
        self.attn_a_fold()
        for ft in range(5):
            self.attn_a_ft(ft, taps if ft == 0 else "")
        self.attn_a_ifft(taps)
        for dt_ in range(NDT):
            t.op("DVE", lambda h, dt_=dt_: h.tensor_copy(
                self.XM[dt_][:, 0:COLS], self.XS[dt_][:, 0:COLS]),
                reads=[self.XS[dt_]], writes=[self.XM[dt_]])

    def attn_a_fold(self):
        t, din = self.t, self.din
        bcol = self.bcol_bf
        def dslot(ci, ec):
            return self.XM[ci * 4 + ec // 2], (ec % 2) * 640
        for ci, dnm in enumerate(("dft_cr", "dft_ci")):
            for ec in range(NDT):
                xb, xc = dslot(ci, ec)
                t.op("DVE", lambda h, xb=xb, xc=xc: h.memset(
                    xb[:, xc + 513:xc + 640], 0.0), writes=[xb])
                t.dma("POOL", lambda h, dnm=dnm, ec=ec, xb=xb, xc=xc: h.dma_start(
                    xb[:, xc:xc + 513], din[dnm][ec * 128:(ec + 1) * 128, :]),
                    writes=[xb])
        for wnm, bnm, cbase in (("Wq", "bq", 0), ("Wk", "bk", 2), ("Wv", "bv", 4)):
            for ec in range(NDT):
                t.dma("POOL", lambda h, wnm=wnm, ec=ec: h.dma_start(
                    self.G8[ec][:, 0:1024], din[wnm][ec * 128:(ec + 1) * 128, :]),
                    writes=[self.G8[ec]])
                t.dma("POOL", lambda h, bnm=bnm, ec=ec: h.dma_start(
                    bcol[:, ec:ec + 1],
                    bass.AP(din[bnm], ec * 128, [[1, 128], [1, 1]])),
                    writes=[bcol])
            for ci in range(2):
                comp = cbase + ci
                dnm = "dft_cr" if ci == 0 else "dft_ci"
                for half in range(4):
                    dts = (2 * half, 2 * half + 1)
                    for ec in range(NDT):
                        xb, xc = dslot(ci, ec)
                        if half == 0:
                            self.mm(self.PW[0:1, 0:512],
                                    [(bcol[:, ec:ec + 1], xb[:, xc:xc + 512])], self.PW,
                                    reads=[bcol, xb], start=(ec == 0), stop=(ec == 7))
                            self.mm(self.PW[0:1, 512:640],
                                    [(bcol[:, ec:ec + 1], xb[:, xc + 512:xc + 640])],
                                    self.PW,
                                    reads=[bcol, xb], start=(ec == 0), stop=(ec == 7))
                        for k, dt_ in enumerate(dts):
                            ps = self.PB[k]
                            self.mm(ps[:, 0:512],
                                    [(self.G8[ec][:, dt_ * 128:(dt_ + 1) * 128],
                                      xb[:, xc:xc + 512])],
                                    ps, reads=[self.G8[ec], xb], start=(ec == 0),
                                    stop=(ec == 7))
                            ps2 = self.PB[2 + k]
                            self.mm(ps2[:, 0:128],
                                    [(self.G8[ec][:, dt_ * 128:(dt_ + 1) * 128],
                                      xb[:, xc + 512:xc + 640])],
                                    ps2, reads=[self.G8[ec], xb], start=(ec == 0),
                                    stop=(ec == 7))
                    for k, dt_ in enumerate(dts):
                        out = self.WK[0]
                        t.op("DVE", lambda h, k=k: h.tensor_copy(
                            out[:, 0:512], self.PB[k][:, 0:512]),
                            reads=[self.PB[k]], writes=[out])
                        t.op("DVE", lambda h, k=k: h.tensor_copy(
                            out[:, 512:640], self.PB[2 + k][:, 0:128]),
                            reads=[self.PB[2 + k]], writes=[out])
                        t.dma("ACT", lambda h, comp=comp, dt_=dt_: h.dma_start(
                            self.d_wf[comp, dt_, :, :], out[:, 0:640]), reads=[out], writes=[self.d_wf])
                t.op("ACT", lambda h: h.copy(self.RS[0][0:1, 0:512], self.PW[0:1, 0:512]),
                     reads=[self.PW], writes=[self.RS[0]])
                t.op("ACT", lambda h: h.copy(self.RS[1][0:1, 0:128], self.PW[0:1, 512:640]),
                     reads=[self.PW], writes=[self.RS[1]])
                fns = []
                for ftk in range(4):
                    fns.append(lambda h, ftk=ftk: h.transpose(
                        self.PB[4][:, ftk:ftk + 1],
                        self.RS[0][0:1, ftk * 128:(ftk + 1) * 128],
                        self.ident_f[0:1, 0:1]))
                fns.append(lambda h: h.transpose(
                    self.PB[4][:, 4:5], self.RS[1][0:1, 0:128], self.ident_f[0:1, 0:1]))
                t.group("PE", fns, reads=[self.RS[0], self.RS[1], self.ident_f],
                        writes=[self.PB[4]])
                t.op("DVE", lambda h, comp=comp: h.tensor_copy(
                    self.bhat_col[:, comp * 5:comp * 5 + 5], self.PB[4][:, 0:5]),
                    reads=[self.PB[4]], writes=[self.bhat_col])

    def attn_a_ft(self, ft, taps=""):
        t, din = self.t, self.din
        XM = self.XM
        for comp in range(6):
            for dt_ in range(NDT):
                t.dma("POOL", lambda h, comp=comp, dt_=dt_, ft=ft: h.dma_start(
                    self.G8[dt_][:, comp * 128:(comp + 1) * 128],
                    self.d_wf[comp, dt_, :, ft * 128:(ft + 1) * 128]),
                    reads=[self.d_wf], writes=[self.G8[dt_]])
        for comp in range(6):
            for tc in range(NW):
                ps = self.PB[(comp + tc) % 2]
                pairs = [(self.G8[dt_][:, comp * 128:(comp + 1) * 128],
                          self.XS[dt_][:, 1 + tc * 512:1 + (tc + 1) * 512])
                         for dt_ in range(NDT)]
                self.mm(ps[:, 0:512], pairs, ps, reads=self.G8 + self.XS)
                t.op("DVE", lambda h, comp=comp, tc=tc, ps=ps, ft=ft: h.tensor_scalar_add(
                    XM[comp][:, tc * 512:(tc + 1) * 512], ps[:, 0:512],
                    self.bhat_col[:, comp * 5 + ft:comp * 5 + ft + 1]),
                    reads=[ps, self.bhat_col], writes=[XM[comp]])
        if "proj" in taps:
            self.tap("proj", [XM[i] for i in range(6)], cols=T)
        m2s = [self.WK[0], self.WK[1], XM[6]]
        TMP = XM[7]
        for s in range(3):
            r_, i_ = XM[2 * s], XM[2 * s + 1]
            m2 = m2s[s]
            t.op("ACT", lambda h, r_=r_, m2=m2: h.activation(
                m2[:, 0:T], r_[:, 0:T], AF.Square), reads=[r_], writes=[m2])
            t.op("ACT", lambda h, i_=i_: h.activation(
                TMP[:, 0:T], i_[:, 0:T], AF.Square), reads=[i_], writes=[TMP])
            t.op("DVE", lambda h, m2=m2: h.tensor_add(
                m2[:, 0:T], m2[:, 0:T], TMP[:, 0:T]), reads=[m2, TMP], writes=[m2])
            t.op("DVE", lambda h, m2=m2: h.tensor_scalar_max(
                m2[:, 0:T], m2[:, 0:T], 1e-38), reads=[m2], writes=[m2])
        for s in range(3):
            t.op("ACT", lambda h, s=s: h.activation(
                m2s[s][:, 0:T], m2s[s][:, 0:T], AF.Sqrt), reads=[m2s[s]], writes=[m2s[s]])
        for s in range(3):
            t.op("ACT", lambda h, s=s: h.activation(
                m2s[s][:, 0:T], m2s[s][:, 0:T], AF.Ln, bias=self.eps_hrr[:, :]),
                reads=[m2s[s], self.eps_hrr], writes=[m2s[s]])
        for s in range(3):
            t.op("ACT", lambda h, s=s: h.activation(
                m2s[s][:, 0:T], m2s[s][:, 0:T], AF.Exp, scale=-1.0),
                reads=[m2s[s]], writes=[m2s[s]])
        for s in range(3):
            for c in range(2):
                b = XM[2 * s + c]
                t.op("DVE", lambda h, b=b, s=s: h.tensor_mul(
                    b[:, 0:T], b[:, 0:T], m2s[s][:, 0:T]),
                    reads=[b, m2s[s]], writes=[b])
        if "unorm" in taps:
            self.tap("unorm", [XM[i] for i in range(6)], cols=T)
        t.op("DVE", lambda h: h.tensor_mul(TMP[:, 0:T], XM[2][:, 0:T], XM[4][:, 0:T]),
             reads=[XM[2], XM[4]], writes=[TMP])
        t.op("DVE", lambda h: h.tensor_mul(self.WK[0][:, 0:T], XM[3][:, 0:T], XM[5][:, 0:T]),
             reads=[XM[3], XM[5]], writes=[self.WK[0]])
        t.op("DVE", lambda h: h.tensor_mul(self.WK[1][:, 0:T], XM[2][:, 0:T], XM[5][:, 0:T]),
             reads=[XM[2], XM[5]], writes=[self.WK[1]])
        t.op("DVE", lambda h: h.tensor_sub(XM[6][:, 0:T], TMP[:, 0:T], self.WK[0][:, 0:T]),
             reads=[TMP, self.WK[0]], writes=[XM[6]])
        t.op("DVE", lambda h: h.tensor_mul(TMP[:, 0:T], XM[3][:, 0:T], XM[4][:, 0:T]),
             reads=[XM[3], XM[4]], writes=[TMP])
        t.op("DVE", lambda h: h.tensor_add(XM[7][:, 0:T], self.WK[1][:, 0:T], TMP[:, 0:T]),
             reads=[self.WK[1], TMP], writes=[XM[7]])
        if "bind" in taps:
            self.tap("bind", [XM[6], XM[7]], cols=T)
        for srcb, dstb in ((XM[6], XM[2]), (XM[7], XM[3]), (XM[0], XM[4]), (XM[1], XM[5])):
            self.transpose_full(srcb, dstb)
        CROW = self.RS[0]
        t.op("ACT", lambda h: h.memzero(CROW[0:1, 0:256]), writes=[CROW])
        psI = self.PB[0]
        psC = self.PB[1]
        for tt in range(32):
            fns = [
                lambda h, tt=tt: h.matmul(psI[:, 0:128], self.l1_bf[:, :],
                                          XM[2][:, tt * 128:(tt + 1) * 128],
                                          start=True, stop=False),
                lambda h, tt=tt: h.matmul(psI[:, 128:256], self.l1_bf[:, :],
                                          XM[3][:, tt * 128:(tt + 1) * 128],
                                          start=False, stop=False),
                lambda h: h.matmul(psI[:, 0:256], self.ones_row_f[0:1, :],
                                   CROW[0:1, 0:256], start=False, stop=True),
            ]
            t.group("PE", fns, reads=[self.l1_bf, XM[2], XM[3], self.ones_row_f, CROW],
                    writes=[psI])
            fns = [
                lambda h, tt=tt: h.matmul(psC[0:1, 0:128], self.ones_col_bf[:, :],
                                          XM[2][:, tt * 128:(tt + 1) * 128],
                                          start=True, stop=False),
                lambda h, tt=tt: h.matmul(psC[0:1, 128:256], self.ones_col_bf[:, :],
                                          XM[3][:, tt * 128:(tt + 1) * 128],
                                          start=False, stop=True),
            ]
            t.group("PE", fns, reads=[self.ones_col_bf, XM[2], XM[3]], writes=[psC])
            t.op("DVE", lambda h, tt=tt: h.tensor_copy(
                XM[2][:, tt * 128:(tt + 1) * 128], psI[:, 0:128]),
                reads=[psI], writes=[XM[2]])
            t.op("DVE", lambda h, tt=tt: h.tensor_copy(
                XM[3][:, tt * 128:(tt + 1) * 128], psI[:, 128:256]),
                reads=[psI], writes=[XM[3]])
            t.op("DVE", lambda h: h.tensor_add(CROW[0:1, 0:256], CROW[0:1, 0:256],
                                               psC[0:1, 0:256]),
                 reads=[psC, CROW], writes=[CROW])
        if "cums" in taps:
            self.tap("cums", [XM[2], XM[3]], cols=T)
        t.op("DVE", lambda h: h.tensor_mul(self.WK[0][:, 0:T], XM[2][:, 0:T], XM[4][:, 0:T]),
             reads=[XM[2], XM[4]], writes=[self.WK[0]])
        t.op("DVE", lambda h: h.tensor_mul(self.WK[1][:, 0:T], XM[3][:, 0:T], XM[5][:, 0:T]),
             reads=[XM[3], XM[5]], writes=[self.WK[1]])
        t.op("DVE", lambda h: h.tensor_mul(TMP[:, 0:T], XM[3][:, 0:T], XM[4][:, 0:T]),
             reads=[XM[3], XM[4]], writes=[TMP])
        t.op("DVE", lambda h: h.tensor_add(self.WK[0][:, 0:T], self.WK[0][:, 0:T],
                                           self.WK[1][:, 0:T]),
             reads=[self.WK[0], self.WK[1]], writes=[self.WK[0]])
        t.op("DVE", lambda h: h.tensor_mul(self.WK[1][:, 0:T], XM[2][:, 0:T], XM[5][:, 0:T]),
             reads=[XM[2], XM[5]], writes=[self.WK[1]])
        t.op("DVE", lambda h: h.tensor_sub(TMP[:, 0:T], TMP[:, 0:T], self.WK[1][:, 0:T]),
             reads=[TMP, self.WK[1]], writes=[TMP])
        self.transpose_full(self.WK[0], XM[0])
        self.transpose_full(TMP, XM[1])
        t.dma("ACT", lambda h, ft=ft: h.dma_start(self.d_wT[0, ft, :, :], XM[0][:, 0:T]),
              reads=[XM[0]], writes=[self.d_wT])
        t.dma("ACT", lambda h, ft=ft: h.dma_start(self.d_wT[1, ft, :, :], XM[1][:, 0:T]),
              reads=[XM[1]], writes=[self.d_wT])

    def transpose_full(self, srcb, dstb):
        t = self.t
        for g in range(4):
            fns = []
            for k in range(8):
                tt = g * 8 + k
                fns.append(lambda h, tt=tt, k=k: h.transpose(
                    self.PT[:, k * 128:(k + 1) * 128],
                    srcb[:, tt * 128:(tt + 1) * 128], self.ident_bf[:, :]))
            t.group("PE", fns, reads=[srcb, self.ident_bf], writes=[self.PT])
            t.op("DVE", lambda h, g=g: h.tensor_copy(
                dstb[:, g * 1024:(g + 1) * 1024], self.PT[:, 0:1024]),
                reads=[self.PT], writes=[dstb])

    def attn_a_ifft(self, taps=""):
        t, din = self.t, self.din
        XM = self.XM
        for fc in range(8):
            t.dma("POOL", lambda h, fc=fc: h.dma_start(
                self.G8[fc][:, 0:1024], din["ifft_b"][fc * 128:(fc + 1) * 128, :]),
                writes=[self.G8[fc]])
        for fc in (8, 9):
            for dt_ in range(NDT):
                t.dma("POOL", lambda h, fc=fc, dt_=dt_: h.dma_start(
                    self.G8[dt_][:, 1024 + (fc - 8) * 128:1024 + (fc - 7) * 128],
                    din["ifft_b"][fc * 128:(fc + 1) * 128, dt_ * 128:(dt_ + 1) * 128]),
                    writes=[self.G8[dt_]])
        for comp in range(2):
            for ft in range(5):
                fc = comp * 5 + ft
                dst = XM[fc] if fc < 8 else (self.WK[0] if fc == 8 else self.WK[1])
                t.dma("POOL", lambda h, comp=comp, ft=ft, dst=dst: h.dma_start(
                    dst[:, 0:T], self.d_wT[comp, ft, :, :]), reads=[self.d_wT], writes=[dst])
        for dt_ in range(NDT):
            t.dma("ACT", lambda h, dt_=dt_: h.dma_start(
                self.XS[dt_][:, 0:COLS], self.d_xmod[dt_, :, :]), reads=[self.d_xmod], writes=[self.XS[dt_]])
        wt = [XM[0], XM[1], XM[2], XM[3], XM[4], XM[5], XM[6], XM[7], self.WK[0], self.WK[1]]
        for dt_ in range(NDT):
            for tc in range(NW):
                ps = self.PB[(dt_ + tc) % 3]
                fns = []
                for fc in range(10):
                    if fc < 8:
                        lt = self.G8[fc][:, dt_ * 128:(dt_ + 1) * 128]
                    else:
                        lt = self.G8[dt_][:, 1024 + (fc - 8) * 128:1024 + (fc - 7) * 128]
                    fns.append(lambda h, fc=fc, lt=lt, tc=tc, ps=ps: h.matmul(
                        ps[:, 0:512], lt, wt[fc][:, tc * 512:(tc + 1) * 512],
                        start=(fc == 0), stop=(fc == 9)))
                t.group("PE", fns, reads=self.G8 + wt, writes=[ps])
                t.op("DVE", lambda h, dt_=dt_, tc=tc, ps=ps: h.tensor_add(
                    self.XS[dt_][:, 1 + tc * 512:1 + (tc + 1) * 512],
                    self.XS[dt_][:, 1 + tc * 512:1 + (tc + 1) * 512], ps[:, 0:512]),
                    reads=[ps, self.XS[dt_]], writes=[self.XS[dt_]])


    def attn_b(self):
        t, din = self.t, self.din
        taps = os.environ.get("MK_TAPS", "")
        XM = self.XM
        self.ln_stats(self.XM, shifted=False, sq_scratch=self.XS)
        self.ln_apply(self.XM, self.XS)
        if "bln" in taps:
            self.tap("bln", self.XS)
        self.spill(self.XM, self.d_xmod)
        for dt_ in range(NDT):
            t.dma("POOL", lambda h, dt_=dt_: h.dma_start(
                self.G8[dt_][:, 0:16], din["BwT"][dt_ * 128:(dt_ + 1) * 128, :]),
                writes=[self.G8[dt_]])
        t.op("DVE", lambda h: h.memset(self.C16B[0:16, 0:256], 0.0), writes=[self.C16B])
        t.dma("ACT", lambda h: h.dma_start(self.d_u[:, 0:256], self.C16B[0:16, 0:256]),
              reads=[self.C16B], writes=[self.d_u])
        for tc in range(NW):
            ps = self.PB[0]
            pairs = [(self.G8[dt_][:, 0:16], self.XS[dt_][:, 1 + tc * 512:1 + (tc + 1) * 512])
                     for dt_ in range(NDT)]
            self.mm(ps[0:16, 0:512], pairs, ps, reads=self.G8 + self.XS)
            t.op("DVE", lambda h, ps=ps: h.tensor_copy(self.C16B[0:16, 0:512],
                                                       ps[0:16, 0:512]),
                 reads=[ps], writes=[self.C16B])
            t.dma("ACT", lambda h, tc=tc: h.dma_start(
                self.d_u[:, 256 + tc * 512:256 + (tc + 1) * 512], self.C16B[0:16, 0:512]),
                reads=[self.C16B], writes=[self.d_u])
        if "bu" in taps:
            o = self.nc.dram_tensor("dbg_bu0", [S_ST, ULEN], BF16, kind="ExternalOutput")
            self.dbg_out["dbg_bu0"] = o
            t.op("DVE", lambda h: h.memset(self.C16B[0:16, 0:256], 0.0), writes=[self.C16B])
            t.dma("POOL", lambda h: h.dma_start(o[:, :], self.d_u[:, :]), reads=[self.d_u])
        self.spill(self.XS, self.d_ln3)
        snT = self.FS
        CT = self.WK[0]
        for dt_ in range(NDT):
            t.dma("SP", lambda h, dt_=dt_: h.dma_start(
                self.SC[1][:, 0:16], din["logA"][dt_ * 128:(dt_ + 1) * 128, :]),
                writes=[self.SC[1]])
            t.op("ACT", lambda h: h.activation(self.SC[1][:, 16:32], self.SC[1][:, 0:16],
                                               AF.Exp), reads=[self.SC[1]], writes=[self.SC[1]])
            t.op("DVE", lambda h: h.tensor_scalar_mul(self.SC[1][:, 16:32],
                                                      self.SC[1][:, 16:32], -1.0),
                 reads=[self.SC[1]], writes=[self.SC[1]])
            t.group("PE", [lambda h: h.transpose(
                self.PB[1][0:16, 0:128], self.SC[1][:, 16:32], self.ident_f[:, :])],
                reads=[self.SC[1], self.ident_f], writes=[self.PB[1]])
            t.op("ACT", lambda h, dt_=dt_: h.copy(
                snT[0:16, dt_ * 128:(dt_ + 1) * 128], self.PB[1][0:16, 0:128]),
                reads=[self.PB[1]], writes=[snT])
            t.dma("SP", lambda h, dt_=dt_: h.dma_start(
                self.SC[1][:, 32:48], din["Cw"][dt_ * 128:(dt_ + 1) * 128, :]),
                writes=[self.SC[1]])
            t.group("PE", [lambda h: h.transpose(
                self.PB[2][0:16, 0:128], self.SC[1][:, 32:48], self.ident_f[:, :])],
                reads=[self.SC[1], self.ident_f], writes=[self.PB[2]])
            t.op("ACT", lambda h, dt_=dt_: h.copy(
                CT[0:16, dt_ * 128:(dt_ + 1) * 128], self.PB[2][0:16, 0:128]),
                reads=[self.PB[2]], writes=[CT])
        AQ = lambda s: (XM[4 + s // 4], (s % 4) * 1024)
        for s in range(S_ST):
            for half in range(2):
                t.dma("SP", lambda h, s=s, half=half: h.dma_start(
                    self.RS[0][0:1, 0:512], snT[s:s + 1, half * 512:(half + 1) * 512]),
                    reads=[snT], writes=[self.RS[0]])
                t.dma("POOL", lambda h, s=s, half=half: h.dma_start(
                    self.RS[1][0:1, 0:512], CT[s:s + 1, half * 512:(half + 1) * 512]),
                    reads=[CT], writes=[self.RS[1]])
                self.mm(self.PB[3][:, 0:512],
                        [(self.jrev_row[0:1, :], self.RS[0][0:1, 0:512])],
                        self.PB[3], reads=[self.jrev_row, self.RS[0]])
                t.op("ACT", lambda h, half=half: h.activation(
                    CT[:, 1024 + half * 512:1024 + (half + 1) * 512], self.PB[3][:, 0:512],
                    AF.Exp), reads=[self.PB[3]], writes=[CT])
                self.bcast_row(self.RS[1], 512, self.PB[4])
                t.op("DVE", lambda h, half=half: h.tensor_mul(
                    self.WK[1][:, half * 512:(half + 1) * 512],
                    CT[:, 1024 + half * 512:1024 + (half + 1) * 512], self.PB[4][:, 0:512]),
                    reads=[CT, self.PB[4]], writes=[self.WK[1]])
                aqb, aqc = AQ(s)
                self.bcast_row(self.RS[0], 512, self.PB[2])
                t.op("ACT", lambda h, aqb=aqb, aqc=aqc, half=half: h.activation(
                    aqb[:, aqc + half * 512:aqc + (half + 1) * 512], self.PB[2][:, 0:512],
                    AF.Exp, scale=128.0), reads=[self.PB[2]], writes=[aqb])
            t.dma("ACT", lambda h, s=s: h.dma_start(self.d_k0[s, :, :],
                                                    self.WK[1][:, 0:1024]),
                  reads=[self.WK[1]], writes=[self.d_k0])
        if "bk0" in taps:
            self.tap("bk0", [self.WK[1]], cols=1024)
        def k0slot(s):
            if s < 12:
                return self.XS[4 + s // 3], (s % 3) * 1024
            return self.G8[s - 12], 0
        for s in range(S_ST):
            kb, kc = k0slot(s)
            t.dma("POOL" if s % 2 == 0 else "SP",
                  lambda h, s=s, kb=kb, kc=kc: h.dma_start(
                      kb[:, kc:kc + 1024], self.d_k0[s, :, :]),
                  reads=[self.d_k0], writes=[kb])
        for xb in (XM[0], XM[1], XM[2], XM[3]):
            t.op("DVE", lambda h, xb=xb: h.memset(xb[:, 0:T], 0.0), writes=[xb])
        OS = [self.G8[4], self.G8[5], self.G8[7]]
        YBUF = self.G8[6]
        YST = self.WK[1]
        def uload(tg):
            for s in range(S_ST):
                ub = self.XS[(tg % 2) * 2 + s // 8]
                uc = (s % 8) * 512
                t.dma("POOL" if s % 2 == 0 else "SP",
                      lambda h, s=s, tg=tg, ub=ub, uc=uc: h.dma_start(
                          ub[:, uc:uc + 512],
                          bass.AP(self.d_u, s * ULEN + 129 + tg * 512, [[1, 128], [1, 512]])),
                      reads=[self.d_u], writes=[ub])
        uload(0)
        for tg in range(NW):
            if tg + 1 < NW:
                uload(tg + 1)
            for crel in range(4):
                c = tg * 4 + crel
                for s in range(S_ST):
                    zb, zc = XM[s // 4], (s % 4) * 1024
                    kb, kc = k0slot(s)
                    ub = self.XS[(tg % 2) * 2 + s // 8]
                    uc = (s % 8) * 512
                    psA = self.PB[2 * (s % 2)]
                    psB = self.PB[2 * (s % 2) + 1]
                    osl = OS[s % 3]
                    fns = []
                    for half, ps in ((0, psA), (1, psB)):
                        fns.append(lambda h, ps=ps, half=half, ub=ub, uc=uc, crel=crel,
                                   kb=kb, kc=kc: h.matmul(
                            ps[:, 0:512],
                            ub[:, uc + crel * 128:uc + crel * 128 + 128],
                            kb[:, kc + half * 512:kc + (half + 1) * 512],
                            start=True, stop=False))
                    for half, ps in ((0, psA), (1, psB)):
                        fns.append(lambda h, ps=ps, zb=zb, zc=zc, half=half: h.matmul(
                            ps[:, 0:512],
                            self.ident_bf[:, :],
                            zb[:, zc + half * 512:zc + (half + 1) * 512],
                            start=False, stop=True))
                    t.group("PE", fns, reads=[ub, kb, zb, self.ident_bf],
                            writes=[psA, psB])
                    t.op("ACT", lambda h, psA=psA, osl=osl: h.copy(
                        osl[:, 0:512], psA[:, 0:512]), reads=[psA], writes=[osl])
                    t.op("DVE", lambda h, psB=psB, osl=osl: h.tensor_copy(
                        osl[:, 512:1024], psB[:, 0:512]), reads=[psB], writes=[osl])
                    aqb, aqc = AQ(s)
                    t.op("DVE", lambda h, zb=zb, zc=zc, aqb=aqb, aqc=aqc, psA=psA:
                         h.tensor_mul(zb[:, zc:zc + 512], psA[:, 0:512],
                                      aqb[:, aqc:aqc + 512]),
                         reads=[psA, aqb], writes=[zb])
                    t.op("POOL", lambda h, zb=zb, zc=zc, aqb=aqb, aqc=aqc, osl=osl:
                         h.tensor_mul(zb[:, zc + 512:zc + 1024], osl[:, 512:1024],
                                      aqb[:, aqc + 512:aqc + 1024]),
                         reads=[osl, aqb], writes=[zb])
                    if s > 1:
                        sp = s - 2
                        osp = OS[sp % 3]
                        t.group("PE", [
                            lambda h, sp=sp, osp=osp: h.matmul(
                                self.PB[4][:, 0:512], self.ident_bf[:, :],
                                osp[:, 0:512], start=(sp == 0), stop=False),
                            lambda h, sp=sp, osp=osp: h.matmul(
                                self.PW[:, 0:512], self.ident_bf[:, :],
                                osp[:, 512:1024], start=(sp == 0), stop=False)],
                            reads=[osp, self.ident_bf], writes=[self.PB[4], self.PW])
                for sp in (S_ST - 2, S_ST - 1):
                    osp = OS[sp % 3]
                    t.group("PE", [
                        lambda h, sp=sp, osp=osp: h.matmul(
                            self.PB[4][:, 0:512], self.ident_bf[:, :],
                            osp[:, 0:512], start=False, stop=(sp == S_ST - 1)),
                        lambda h, sp=sp, osp=osp: h.matmul(
                            self.PW[:, 0:512], self.ident_bf[:, :],
                            osp[:, 512:1024], start=False, stop=(sp == S_ST - 1))],
                        reads=[osp, self.ident_bf], writes=[self.PB[4], self.PW])
                t.op("ACT", lambda h: h.copy(YBUF[:, 0:512], self.PB[4][:, 0:512]),
                     reads=[self.PB[4]], writes=[YBUF])
                t.op("ACT", lambda h: h.copy(YBUF[:, 512:1024], self.PW[:, 0:512]),
                     reads=[self.PW], writes=[YBUF])
                fns = []
                for dsl in range(NDT):
                    fns.append(lambda h, dsl=dsl: h.transpose(
                        self.PT[:, dsl * 128:(dsl + 1) * 128],
                        YBUF[:, dsl * 128:(dsl + 1) * 128],
                        self.ident_bf[:, :]))
                t.group("PE", fns, reads=[YBUF, self.ident_bf], writes=[self.PT])
                for dsl in range(NDT):
                    t.op("ACT", lambda h, dsl=dsl, crel=crel: h.copy(
                        YST[:, dsl * 512 + crel * 128:dsl * 512 + crel * 128 + 128],
                        self.PT[:, dsl * 128:(dsl + 1) * 128]),
                        reads=[self.PT], writes=[YST])
            for dsl in range(NDT):
                t.dma("ACT", lambda h, dsl=dsl, tg=tg: h.dma_start(
                    self.d_y[dsl, :, tg * 512:(tg + 1) * 512],
                    YST[:, dsl * 512:(dsl + 1) * 512]),
                    reads=[YST], writes=[self.d_y])
        for dt_ in range(NDT):
            t.dma("ACT", lambda h, dt_=dt_: h.dma_start(
                self.WK[1][:, 0:T], self.d_y[dt_, :, :]), reads=[self.d_y],
                writes=[self.WK[1]])
            t.dma("POOL", lambda h, dt_=dt_: h.dma_start(
                self.XS[dt_][:, 0:COLS], self.d_ln3[dt_, :, :]), reads=[self.d_ln3],
                writes=[self.XS[dt_]])
            t.op("DVE", lambda h, dt_=dt_: h.scalar_tensor_tensor(
                self.WK[1][:, 0:T], self.XS[dt_][:, 1:T + 1],
                self.Dvec_col[:, dt_:dt_ + 1], self.WK[1][:, 0:T],
                ALU.mult, ALU.add), reads=[self.XS[dt_], self.Dvec_col, self.WK[1]],
                writes=[self.WK[1]])
            t.dma("ACT", lambda h, dt_=dt_: h.dma_start(
                self.WK[0][:, 0:T], self.d_xmod[dt_, :, 1:T + 1]), reads=[self.d_xmod],
                writes=[self.WK[0]])
            t.op("DVE", lambda h, dt_=dt_: h.tensor_add(
                self.XM[dt_][:, 1:T + 1], self.WK[0][:, 0:T], self.WK[1][:, 0:T]),
                reads=[self.WK[0], self.WK[1]], writes=[self.XM[dt_]])
            t.op("DVE", lambda h, dt_=dt_: h.memset(self.XM[dt_][:, 0:1], 0.0),
                 writes=[self.XM[dt_]])
            t.op("DVE", lambda h, dt_=dt_: h.memset(self.XM[dt_][:, T + 1:COLS], 0.0),
                 writes=[self.XM[dt_]])


    def write_out(self):
        t = self.t
        for dt_ in range(NDT):
            for cc in range(4):
                t.op("DVE", lambda h, dt_=dt_, cc=cc: h.tensor_copy(
                    self.FS[:, 0:1024], self.XM[dt_][:, 1 + cc * 1024:1 + (cc + 1) * 1024]),
                    reads=[self.XM[dt_]], writes=[self.FS])
                t.dma("POOL", lambda h, dt_=dt_, cc=cc: h.dma_start(
                    self.xout[dt_ * 128:(dt_ + 1) * 128, cc * 1024:(cc + 1) * 1024],
                    self.FS[:, 0:1024]), reads=[self.FS])


def kernel(**inputs):
    consts = build_constants()
    g = host_layout(inputs)
    x = np.asarray(inputs["L0_attn_a_x"], np.float32)
    B = x.shape[0]
    mk = MK(consts, stages=os.environ.get("MK_STAGES", "afbf"))
    nc = mk.build()
    in_maps = []
    for b in range(B):
        m = {"xin": np.ascontiguousarray(x[b].T)}
        for nm, arr in g.items():
            m[nm] = arr
        for nm, arr in consts.items():
            m["c_" + nm] = arr
        in_maps.append(m)
    res = run_bass_kernel_spmd(nc, in_maps, core_ids=list(range(B)))
    global _LAST_DBG
    _LAST_DBG = [{k: np.asarray(v) for k, v in r.items() if k.startswith("dbg_")}
                 for r in res.results]
    out = np.stack([np.asarray(r["xout"], np.float32).T for r in res.results])
    return out


_LAST_DBG = None


# version 15
# speedup vs baseline: 1.0083x; 1.0083x over previous
import os
import math
import contextlib
import numpy as np
import ml_dtypes
import concourse.bass as bass
import concourse.mybir as mybir
from concourse.bass_utils import run_bass_kernel_spmd

F32 = mybir.dt.float32
BF16 = mybir.dt.bfloat16
AF = mybir.ActivationFunctionType
ALU = mybir.AluOpType
AX = mybir.AxisListType

T = 4096
D = 1024
COLS = 4104
NDT = 8
HEADS = 8
DH = 64
WIN = 512
NW = T // WIN
INNER = 512
HID = 2730
HIDP = 2816
NJ = HIDP // 128
NF = 640
NFR = 513
S_ST = 16
QCH = 128
NCH = T // QCH
UPAD = 256
ULEN = UPAD + T

ENGS = ("PE", "ACT", "DVE", "POOL", "SP")
SECT = {"PE": "tensor", "ACT": "scalar", "DVE": "vector", "POOL": "gpsimd", "SP": "sync"}



class Trk:

    def __init__(self, nc, stack):
        self.nc = nc
        self.idx = {e: 0 for e in ENGS}
        self.qcnt = {e: 0 for e in ENGS}
        self.buf = {e: [] for e in ENGS}
        self.pubs = {e: [] for e in ENGS}
        self.pubsem = {e: stack.enter_context(nc.semaphore(f"pub{e}")) for e in ENGS}
        self.QK = 8
        self.qsems = {e: [stack.enter_context(nc.semaphore(f"q{e}{i}")) for i in range(self.QK)]
                      for e in ENGS}
        self.clock = {e: {} for e in ENGS}
        self.hist = {e: [(-1, {})] for e in ENGS}
        self.qsnap = {e: [] for e in ENGS}
        self.wr = {}
        self.rd = {}
        self.space = {}
        self.names = {}
        self.attach = {}
        self.nwaits = 0

    def reg(self, buf, name, space):
        self.space[id(buf)] = space
        self.names[id(buf)] = name
        return buf

    def _snap(self, e, opi):
        h = self.hist[e]
        for k in range(len(h) - 1, -1, -1):
            if h[k][0] <= opi:
                return h[k][1]
        return h[0][1]

    def _learn(self, e, d):
        c = self.clock[e]
        ch = False
        for k, v in d.items():
            if isinstance(v, dict):
                cur = c.get(k)
                if cur is None:
                    c[k] = dict(v)
                    ch = True
                else:
                    for s, pv in v.items():
                        if cur.get(s, -1) < pv:
                            cur[s] = pv
                            ch = True
            elif c.get(k, -1) < v:
                c[k] = v
                ch = True
        if ch:
            self.hist[e].append((self.idx[e], self._copyclock(c)))

    def _copyclock(self, c):
        return {k: (dict(v) if isinstance(v, dict) else v) for k, v in c.items()}

    def _learn_slot(self, e, key, slot, pidx):
        c = self.clock[e]
        cur = c.setdefault(key, {})
        if cur.get(slot, -1) < pidx:
            cur[slot] = pidx
            self.hist[e].append((self.idx[e], self._copyclock(c)))

    def _ensure(self, cons, key, pidx):
        if key[0] == "Q":
            slotv = self.clock[cons].get(key)
            slot = (pidx - 1) % self.QK
            if slotv is not None and slotv.get(slot, -1) >= pidx:
                return
        elif self.clock[cons].get(key, -1) >= pidx:
            return
        if key[0] == "Q":
            qe = key[1:]
            slot = (pidx - 1) % self.QK
            lap = (pidx - 1) // self.QK + 1
            self.buf[cons].append(("wait", self.qsems[qe][slot], 16 * lap))
            self.nwaits += 1
            snap = self.qsnap[qe][pidx - 1]
            self._learn(cons, dict(snap))
            self._learn_slot(cons, key, slot, pidx)
        else:
            pubs = self.pubs[key]
            pick = None
            for (pi, cum) in pubs:
                if pi >= pidx:
                    pick = (pi, cum)
                    break
            if pick is None:
                assert pidx < self.idx[key], (key, pidx, self.idx[key])
                cum = (pubs[-1][1] + 1) if pubs else 1
                pubs.append((pidx, cum))
                self.attach[(key, pidx)] = (self.pubsem[key], 1)
                pick = (pidx, cum)
            self.buf[cons].append(("wait", self.pubsem[key], pick[1]))
            self.nwaits += 1
            snap = self._snap(key, pick[0])
            self._learn(cons, dict(snap, **{key: pick[0]}))

    def _pre(self, eng, reads, writes):
        for b in reads:
            w = self.wr.get(id(b))
            if w and w != (eng, self.idx[eng]):
                self._ensure(eng, w[0], w[1])
            if self.space.get(id(b)) == "P":
                for rk, ri in list(self.rd.get(id(b), {}).items()):
                    if rk != eng:
                        self._ensure(eng, rk, ri)
        for b in writes:
            w = self.wr.get(id(b))
            if w:
                self._ensure(eng, w[0], w[1])
            for rk, ri in list(self.rd.get(id(b), {}).items()):
                if not (rk == eng and ri == self.idx[eng]):
                    self._ensure(eng, rk, ri)

    def _post_reads(self, key, opi, reads):
        for b in reads:
            d = self.rd.setdefault(id(b), {})
            if d.get(key, -1) < opi:
                d[key] = opi

    def _post_writes(self, key, opi, writes):
        for b in writes:
            self.wr[id(b)] = (key, opi)
            self.rd[id(b)] = {}

    def op(self, eng, fn, reads=(), writes=()):
        self._pre(eng, reads, writes)
        opi = self.idx[eng]
        self.buf[eng].append(("op", fn, opi))
        self.idx[eng] = opi + 1
        self._post_reads(eng, opi, reads)
        self._post_writes(eng, opi, writes)
        return opi

    def group(self, eng, fns, reads=(), writes=()):
        self._pre(eng, reads, writes)
        base = self.idx[eng]
        n = len(fns)
        for k, fn in enumerate(fns):
            self.buf[eng].append(("gop", fn, base, k == n - 1))
        self.idx[eng] = base + 1
        self._post_reads(eng, base, reads)
        self._post_writes(eng, base, writes)
        return base

    def dma(self, eng, fn, reads=(), writes=()):
        self._pre(eng, reads, writes)
        self.qcnt[eng] += 1
        qi = self.qcnt[eng]
        slot = (qi - 1) % self.QK
        if qi > self.QK:
            prev = qi - self.QK
            lapp = (prev - 1) // self.QK + 1
            self.buf[eng].append(("wait", self.qsems[eng][slot], 16 * lapp))
            snapp = self.qsnap[eng][prev - 1]
            self._learn(eng, dict(snapp))
            self._learn_slot(eng, "Q" + eng, slot, prev)
        self.qsnap[eng].append(self._copyclock(self.clock[eng]))
        self.buf[eng].append(("dma", fn, qi, slot))
        key = "Q" + eng
        self._post_reads(key, qi, reads)
        self._post_writes(key, qi, writes)
        return qi

    def wait_all_of(self, cons, key):
        if key[0] == "Q":
            qe = key[1:]
            n = self.qcnt[qe]
            for pidx in range(max(1, n - self.QK + 1), n + 1):
                self._ensure(cons, key, pidx)
        else:
            if self.idx[key] > 0:
                self._ensure(cons, key, self.idx[key] - 1)

    def finish(self, eng="SP"):
        for e in ENGS:
            if self.qcnt[e] > 0:
                self.wait_all_of(eng, "Q" + e)
        for e in ENGS:
            if e != eng and self.idx[e] > 0:
                self._ensure(eng, e, self.idx[e] - 1)

    def flush(self, block):
        for e in ENGS:
            entries = self.buf[e]
            if not entries:
                continue
            trk = self

            def run(h, _e=e, _entries=entries):
                for ent in _entries:
                    kind = ent[0]
                    if kind == "wait":
                        h.wait_ge(ent[1], ent[2])
                    elif kind == "op":
                        ins = ent[1](h)
                        a = trk.attach.get((_e, ent[2]))
                        if a is not None:
                            ins.then_inc(a[0], a[1])
                    elif kind == "gop":
                        ins = ent[1](h)
                        if ent[3]:
                            a = trk.attach.get((_e, ent[2]))
                            if a is not None:
                                ins.then_inc(a[0], a[1])
                    elif kind == "dma":
                        ins = ent[1](h)
                        ins.then_inc(trk.qsems[_e][ent[3]], 16)

            getattr(block, SECT[e])(run)



def build_constants():
    c = {}
    ee = np.arange(D, dtype=np.float64)
    ff = np.arange(NFR, dtype=np.float64)
    ang = 2.0 * np.pi * np.outer(ee, ff) / D
    c["dft_cr"] = np.cos(ang).astype(np.float32)
    c["dft_ci"] = (-np.sin(ang)).astype(np.float32)
    cf = np.full(NFR, 2.0 / D)
    cf[0] = 1.0 / D
    cf[NFR - 1] = 1.0 / D
    br = (cf[:, None] * np.cos(ang.T)).astype(np.float32)
    bi = (-cf[:, None] * np.sin(ang.T)).astype(np.float32)
    B = np.zeros((2 * NF, D), np.float32)
    B[0:NFR] = br
    B[NF:NF + NFR] = bi
    c["ifft_b"] = B

    hi = np.arange(0, DH, 2, dtype=np.float64)
    inv_freq = 1.0 / 10000.0 ** (hi / DH)
    t = np.arange(T, dtype=np.float64)
    freqs = t[:, None] * inv_freq[None, :]
    freqs = np.concatenate([freqs, freqs], -1)
    base = (hi + 0.4 * DH) / (1.4 * DH)
    power = (t - T // 2) / float(WIN // 2)
    scale = base[None, :] ** power[:, None]
    scale = np.concatenate([scale, scale], -1)
    cosf, sinf = np.cos(freqs), np.sin(freqs)
    for nm, arr in (("cs_q", cosf * scale), ("ss_q", sinf * scale),
                    ("cs_k", cosf / scale), ("ss_k", sinf / scale)):
        a = arr.T.astype(np.float32)
        c[nm] = np.concatenate([a, a], 0)

    R = np.zeros((128, 128), np.float32)
    for h0 in (0, 64):
        for f in range(32):
            R[h0 + f + 32, h0 + f] = -1.0
            R[h0 + f, h0 + f + 32] = 1.0
    c["rotm"] = R

    c["ident"] = np.eye(128, dtype=np.float32)
    c["l1"] = np.triu(np.ones((128, 128), np.float32))
    c["ones_col"] = np.ones((128, 1), np.float32)
    c["ones_row"] = np.ones((1, 128), np.float32)
    c["negbig"] = np.concatenate([np.full((128, 512), -1.0e38, np.float32),
                                  np.zeros((128, 512), np.float32)], axis=1)
    c["ones128"] = np.ones((128, 128), np.float32)
    c["jrev_row"] = np.arange(127, -1, -1, dtype=np.float32).reshape(1, 128)
    c["eps_ln"] = np.full((128, 1), 1e-5, np.float32)
    c["eps_hrr"] = np.full((128, 1), 1e-8, np.float32)

    NEG = np.float32(-1.0e38)
    mm = np.zeros((128, NF), np.float32)
    for oi in range(128):
        mm[oi, :oi] = NEG
        mm[oi, oi + 513:] = NEG
    c["mask_mid"] = mm
    for it in range(4):
        m0 = mm.copy()
        cut = 512 - it * 128
        m0[:, :cut] = NEG
        c[f"mask_w0_{it}"] = m0
    for jt in range(8):
        mt = np.zeros((128, WIN), np.float32)
        for p in range(128):
            jl = jt * 128 + p
            lo = max(0, jl - 512)
            hi_ = min(WIN, jl + 1)
            mt[p, :lo] = NEG
            mt[p, hi_:] = NEG
        c[f"maskT_{jt}"] = mt

    for i in range(4):
        sq = np.zeros((128, 16), np.float32)
        sq[0:64, 2 * i] = 1.0
        sq[64:128, 2 * i + 1] = 1.0
        c[f"sel_q{i}"] = sq
        sk = np.zeros((128, 16), np.float32)
        sk[0:64, 8 + 2 * i] = 1.0
        sk[64:128, 8 + 2 * i + 1] = 1.0
        c[f"sel_k{i}"] = sk
        rq = np.zeros((16, 128), np.float32)
        rq[2 * i, 0:64] = 1.0
        rq[2 * i + 1, 64:128] = 1.0
        c[f"rsel_q{i}"] = rq
        rk = np.zeros((16, 128), np.float32)
        rk[8 + 2 * i, 0:64] = 1.0
        rk[8 + 2 * i + 1, 64:128] = 1.0
        c[f"rsel_k{i}"] = rk
    return c


def host_layout(inputs):
    g = {}
    f32 = lambda a: np.ascontiguousarray(np.asarray(a, np.float32))
    wqkv = f32(inputs["L0_ffn_wqkv"])
    wo = f32(inputs["L0_ffn_wo"])
    w1 = f32(inputs["L0_ffn_w1"])
    w2 = f32(inputs["L0_ffn_w2"])
    b1 = f32(inputs["L0_ffn_b1"])
    b2 = f32(inputs["L0_ffn_b2"])
    lng = f32(inputs["L0_ffn_ln_g"])
    lnb = f32(inputs["L0_ffn_ln_b"])
    qs = f32(inputs["L0_ffn_q_scale"])
    ks = f32(inputs["L0_ffn_k_scale"])
    w1p = np.zeros((2, D, 2 * HIDP), np.float32)
    w1p[:, :, 0:HID] = w1[:, :, 0:HID]
    w1p[:, :, HIDP:HIDP + HID] = w1[:, :, HID:2 * HID]
    b1p = np.zeros((2, 2 * HIDP), np.float32)
    b1p[:, 0:HID] = b1[:, 0:HID]
    b1p[:, HIDP:HIDP + HID] = b1[:, HID:2 * HID]
    w2p = np.zeros((2, HIDP, D), np.float32)
    w2p[:, 0:HID, :] = w2
    g["wqkv"] = wqkv
    g["wo"] = wo
    g["w1dev"] = np.ascontiguousarray(
        w1p.reshape(2, NDT, 128, 2, NJ, 128).transpose(0, 4, 3, 2, 1, 5)
    ).reshape(2, NJ, 2, 128, 1024)
    g["w2p"] = w2p
    g["b1a_col"] = np.ascontiguousarray(b1p[:, :HIDP].reshape(2, NJ, 128).transpose(0, 2, 1))
    g["b1g_col"] = np.ascontiguousarray(b1p[:, HIDP:].reshape(2, NJ, 128).transpose(0, 2, 1))
    g["lng_col"] = np.ascontiguousarray(lng.reshape(2, NDT, 128).transpose(0, 2, 1))
    g["lnb_col"] = np.ascontiguousarray(lnb.reshape(2, NDT, 128).transpose(0, 2, 1))
    g["b2_col"] = np.ascontiguousarray(b2.reshape(2, NDT, 128).transpose(0, 2, 1))

    def gcol(a):
        return f32(np.concatenate([a, a])[:, None])

    qsr = np.concatenate([qs[:, 32:64], qs[:, 0:32]], axis=1)
    ksr = np.concatenate([ks[:, 32:64], ks[:, 0:32]], axis=1)
    g["qs_col"] = np.stack([gcol(qs[l]) for l in range(2)])
    g["qsr_col"] = np.stack([gcol(qsr[l]) for l in range(2)])
    g["ks_col"] = np.stack([gcol(ks[l]) for l in range(2)])
    g["ksr_col"] = np.stack([gcol(ksr[l]) for l in range(2)])
    g["Wq"] = f32(inputs["L0_attn_a_Wq"])
    g["Wk"] = f32(inputs["L0_attn_a_Wk"])
    g["Wv"] = f32(inputs["L0_attn_a_Wv"])
    g["bq"] = f32(inputs["L0_attn_a_bq"]).reshape(1, D)
    g["bk"] = f32(inputs["L0_attn_a_bk"]).reshape(1, D)
    g["bv"] = f32(inputs["L0_attn_a_bv"]).reshape(1, D)
    g["logA"] = f32(inputs["L0_attn_b_log_A_diag"])
    g["BwT"] = np.ascontiguousarray(f32(inputs["L0_attn_b_B_w"]).T)
    g["Cw"] = f32(inputs["L0_attn_b_C_w"])
    g["Dvec_col"] = np.ascontiguousarray(f32(inputs["L0_attn_b_D_vec"]).reshape(NDT, 128).T)
    return g



class MK:
    def __init__(self, consts, stages="afbf", dbg_taps=()):
        self.stages = [s for s in stages if s in "afb"]
        self.dbg_taps = list(dbg_taps)
        self.consts = consts
        self.stack = contextlib.ExitStack()
        nc = bass.Bass("TRN2", target_bir_lowering=False, debug=False)
        self.nc = nc
        self.t = Trk(nc, self.stack)
        self.din = {}
        self.dbg_out = {}
        self._declare_io()
        self._alloc()

    def _declare_io(self):
        nc = self.nc
        A = lambda name, shape: nc.dram_tensor(name, list(shape), F32, kind="ExternalInput")
        self.xin = A("xin", (D, T))
        for nm, shape in [
            ("wqkv", (2, D, 1536)), ("wo", (2, INNER, D)), ("w1dev", (2, NJ, 2, 128, 1024)),
            ("w2p", (2, HIDP, D)), ("b1a_col", (2, 128, NJ)), ("b1g_col", (2, 128, NJ)),
            ("lng_col", (2, 128, NDT)), ("lnb_col", (2, 128, NDT)), ("b2_col", (2, 128, NDT)),
            ("qs_col", (2, 128, 1)), ("qsr_col", (2, 128, 1)),
            ("ks_col", (2, 128, 1)), ("ksr_col", (2, 128, 1)),
            ("Wq", (D, D)), ("Wk", (D, D)), ("Wv", (D, D)),
            ("bq", (1, D)), ("bk", (1, D)), ("bv", (1, D)),
            ("logA", (D, S_ST)), ("BwT", (D, S_ST)), ("Cw", (D, S_ST)),
            ("Dvec_col", (128, NDT)),
        ]:
            self.din[nm] = A(nm, shape)
        for nm, arr in self.consts.items():
            self.din[nm] = A("c_" + nm, arr.shape)
        self.xout = nc.dram_tensor("xout", [D, T], F32, kind="ExternalOutput")
        I = lambda name, shape, dt=BF16: nc.dram_tensor(name, list(shape), dt, kind="Internal")
        self.d_xmod = self.t.reg(I("d_xmod", (NDT, 128, COLS)), "d_xmod", "D")
        self.d_ln3 = self.t.reg(I("d_ln3", (NDT, 128, COLS)), "d_ln3", "D")
        self.d_wf = self.t.reg(I("d_wf", (6, NDT, 128, NF)), "d_wf", "D")
        self.d_wT = self.t.reg(I("d_wT", (2, 5, 128, T)), "d_wT", "D")
        self.d_u = self.t.reg(I("d_u", (S_ST, ULEN)), "d_u", "D")
        self.d_k0 = self.t.reg(I("d_k0", (S_ST, 128, D)), "d_k0", "D")
        self.d_y = self.t.reg(I("d_y", (NDT, 128, T)), "d_y", "D")
        self.d_w1b = self.t.reg(I("d_w1b", (2, NJ, 2, 128, 1024)), "d_w1b", "D")
        self.d_w2b = self.t.reg(I("d_w2b", (2, NJ, 2, 128, 512)), "d_w2b", "D")
        self.d_tabsb = self.t.reg(I("d_tabsb", (4, 128, T)), "d_tabsb", "D")

    def _alloc(self):
        nc, t = self.nc, self.t
        SB = lambda nm, shape, dt: t.reg(nc.alloc_sbuf_tensor(nm, list(shape), dt), nm, "S")
        PS = lambda nm, shape, dt: t.reg(nc.alloc_psum_tensor(nm, list(shape), dt), nm, "P")
        self.XM = [SB(f"XM{i}", (128, COLS), BF16) for i in range(NDT)]
        self.XS = [SB(f"XS{i}", (128, COLS), BF16) for i in range(NDT)]
        self.G8 = [SB(f"G8_{i}", (128, 1536), BF16) for i in range(NDT)]
        self.WO8 = [SB(f"WO8_{i}", (64, D), BF16) for i in range(HEADS)]
        self.VT7 = SB("VT7", (128, 520), BF16)
        self.FS = SB("FS", (128, 1024), F32)
        self.WK = [SB(f"WK{i}", (128, T), BF16) for i in range(2)]
        self.SC = [SB(f"SC{i}", (128, 64), F32) for i in range(4)]
        self.RS = [SB(f"RS{i}", (1, 512), F32) for i in range(2)]
        self.NMR = SB("NMR", (1, 512), BF16)
        self.BIASA = SB("BIASA", (128, NJ), F32)
        self.BIASG = SB("BIASG", (128, NJ), F32)
        self.C16B = SB("C16B", (16, 512), BF16)
        self.ident_bf = SB("ident_bf", (128, 128), BF16)
        self.ident_f = SB("ident_f", (128, 128), F32)
        self.l1_bf = SB("l1_bf", (128, 128), BF16)
        self.rotm_bf = SB("rotm_bf", (128, 128), BF16)
        self.ones_col_bf = SB("ones_col_bf", (128, 1), BF16)
        self.ones_row_f = SB("ones_row_f", (1, 128), F32)
        self.ones128_f = SB("ones128_f", (128, 128), F32)
        self.jrev_row = SB("jrev_row", (1, 128), F32)
        self.eps_ln = SB("eps_ln", (128, 1), F32)
        self.eps_hrr = SB("eps_hrr", (128, 1), F32)
        self.masks = {}
        self.masks["mask_mid"] = SB("mask_mid", (128, NF), BF16)
        self.negbig_bf = SB("negbig_bf", (128, 1024), BF16)
        self.BCOLW = SB("BCOLW", (128, 4), BF16)
        self.BCOLM = SB("BCOLM", (128, 4), BF16)
        self.sel = {}
        for i in range(4):
            for p in ("sel_q", "sel_k"):
                self.sel[f"{p}{i}"] = SB(f"{p}{i}", (128, 16), BF16)
            for p in ("rsel_q", "rsel_k"):
                self.sel[f"{p}{i}"] = SB(f"{p}{i}", (16, 128), BF16)
        self.qs_col = SB("sb_qs_col", (128, 1), F32)
        self.qsr_col = SB("sb_qsr_col", (128, 1), F32)
        self.ks_col = SB("sb_ks_col", (128, 1), F32)
        self.ksr_col = SB("sb_ksr_col", (128, 1), F32)
        self.lng_col = SB("sb_lng_col", (128, NDT), F32)
        self.lnb_col = SB("sb_lnb_col", (128, NDT), F32)
        self.lgb_bf = SB("lgb_bf", (128, 16), BF16)
        self.b2_col = SB("sb_b2_col", (128, NDT), F32)
        self.b1a_col = SB("sb_b1a_col", (128, NJ), F32)
        self.b1g_col = SB("sb_b1g_col", (128, NJ), F32)
        self.Dvec_col = SB("sb_Dvec_col", (128, NDT), F32)
        self.bhat_col = SB("bhat_col", (128, 32), F32)
        self.bcol_bf = SB("bcol_bf", (128, 8), BF16)
        self.PB = [PS(f"PB{i}", (128, 512), F32) for i in range(5)]
        self.PW = PS("PW", (128, NF), F32)
        self.PT = PS("PT", (128, 1024), BF16)

    def es_slot(self, it):
        return [(self.XM[4], 0), (self.XM[6], 2048), (self.XM[4], 1024),
                (self.WK[0], 0)][it]

    def qbuf(self, h):
        return self.XM[0], h * 512
    def kbuf(self, h):
        return (self.XM[1] if h % 2 == 0 else self.XM[2]), (h // 2) * 1024
    def vtbuf(self, h):
        if h < 7:
            return self.XM[3], h * 520
        return self.VT7, 0
    def attnT_buf(self):
        return self.XM[4]
    def qraw_buf(self):
        return self.XM[4]
    def ofin_buf(self, h):
        return self.XM[5], h * 512
    def q2buf(self):
        return self.XM[7]
    def tabq(self, k):
        return self.XM[6], k * 512
    def maskT(self, jt):
        if jt < 4:
            return self.masks[f"maskT_{jt}"], 0
        return self.XM[6], 2048 + (jt - 4) * 512
    def mst(self, j):
        return self.XM[j // 8], (j % 8) * 512
    def prbuf(self):
        return self.XM[6]
    def w2buf(self, r):
        return self.XM[7], r * 1024
    def lgra(self):
        return self.XM[3]
    def lgrg(self):
        return self.XM[4]

    def cast_load(self, dram_ap, sb_f32_tmp, out_bf, shape_cols, eng="SP"):
        t = self.t
        t.dma(eng, lambda h: h.dma_start(sb_f32_tmp[:, 0:shape_cols], dram_ap), writes=[sb_f32_tmp])
        t.op("DVE", lambda h: h.tensor_copy(out_bf[:, 0:shape_cols], sb_f32_tmp[:, 0:shape_cols]),
             reads=[sb_f32_tmp], writes=[out_bf])

    def mm(self, psum_ap, pairs, psum_buf, reads, start=True, stop=True):
        t = self.t
        n = len(pairs)
        fns = []
        for k, (lt, rh) in enumerate(pairs):
            fns.append(lambda h, lt=lt, rh=rh, k=k: h.matmul(
                psum_ap, lt, rh, start=(start and k == 0), stop=(stop and k == n - 1)))
        t.group("PE", fns, reads=list(reads), writes=[psum_buf])

    def tap(self, name, bufs, cols=COLS, dt=None):
        nc, t = self.nc, self.t
        n = len(bufs)
        dtt = dt or BF16
        k = 0
        while f"dbg_{name}{k}" in self.dbg_out:
            k += 1
        nm = f"dbg_{name}{k}"
        o = nc.dram_tensor(nm, [n, 128, cols], dtt, kind="ExternalOutput")
        self.dbg_out[nm] = o
        for i, b in enumerate(bufs):
            t.dma("POOL", lambda h, i=i, b=b: h.dma_start(o[i, :, 0:cols], b[:, 0:cols]),
                  reads=[b])

    def build(self):
        t = self.t
        self.ffn_converted = False
        self.load_consts()
        if "f" in self.stages:
            for k, nm in enumerate(("cs_q", "ss_q", "cs_k", "ss_k")):
                t.dma("POOL", lambda h, k=k, nm=nm: h.dma_start(
                    self.d_tabsb[k], self.din[nm][:, :]), writes=[self.d_tabsb])
        self.load_x()
        for st in self.stages:
            if st == "a":
                self.attn_a()
            elif st == "f":
                self.ffn()
            elif st == "b":
                self.attn_b()
        self.write_out()
        t.finish("SP")
        block = self.stack.enter_context(self.nc.Block())
        t.flush(block)
        self.stack.close()
        return self.nc

    def load_consts(self):
        t, din = self.t, self.din

        def ldb(name, dst, rows, cols):
            t.dma("POOL", lambda h: h.dma_start(dst[0:rows, 0:cols], din[name][0:rows, 0:cols]),
                  writes=[dst])

        def ldf(name, dst, rows, cols):
            t.dma("SP", lambda h: h.dma_start(dst[0:rows, 0:cols], din[name][0:rows, 0:cols]),
                  writes=[dst])

        ldb("ident", self.ident_bf, 128, 128)
        ldf("ident", self.ident_f, 128, 128)
        ldb("l1", self.l1_bf, 128, 128)
        ldb("rotm", self.rotm_bf, 128, 128)
        ldb("ones_col", self.ones_col_bf, 128, 1)
        ldf("ones_row", self.ones_row_f, 1, 128)
        ldf("ones128", self.ones128_f, 128, 128)
        ldf("jrev_row", self.jrev_row, 1, 128)
        ldf("eps_ln", self.eps_ln, 128, 1)
        ldf("eps_hrr", self.eps_hrr, 128, 1)
        ldb("mask_mid", self.masks["mask_mid"], 128, NF)
        ldb("negbig", self.negbig_bf, 128, 1024)
        for nm, dst in self.sel.items():
            if nm.startswith("rsel"):
                ldb(nm, dst, 16, 128)
            else:
                ldb(nm, dst, 128, 16)
        t.dma("SP", lambda h: h.dma_start(self.Dvec_col[:, :], din["Dvec_col"][:, :]),
              writes=[self.Dvec_col])

    def load_layer_smalls(self, l):
        t, din = self.t, self.din
        for nm, dst in [("qs_col", self.qs_col), ("qsr_col", self.qsr_col),
                        ("ks_col", self.ks_col), ("ksr_col", self.ksr_col),
                        ("lng_col", self.lng_col), ("lnb_col", self.lnb_col),
                        ("b2_col", self.b2_col), ("b1a_col", self.b1a_col),
                        ("b1g_col", self.b1g_col)]:
            t.dma("SP", lambda h, nm=nm, dst=dst: h.dma_start(
                dst[:, :], din[nm][l]), writes=[dst])
        for dst in (self.qs_col, self.qsr_col):
            t.op("DVE", lambda h, dst=dst: h.tensor_scalar_mul(dst[:, :], dst[:, :], 8.0),
                 reads=[dst], writes=[dst])

    def load_x(self):
        t = self.t
        for dt_ in range(NDT):
            t.dma("POOL", lambda h, dt_=dt_: h.dma_start(
                self.XM[dt_][:, 1:T + 1], self.xin[dt_ * 128:(dt_ + 1) * 128, 0:T]),
                writes=[self.XM[dt_]])
            t.op("DVE", lambda h, dt_=dt_: h.memset(self.XM[dt_][:, 0:1], 0.0),
                 reads=[], writes=[self.XM[dt_]])
            t.op("DVE", lambda h, dt_=dt_: h.memset(self.XM[dt_][:, T + 1:COLS], 0.0),
                 reads=[], writes=[self.XM[dt_]])

    def row_to_col(self, row_f32, col_out, psum, ncols=32):
        t = self.t
        fns = []
        for ch in range(ncols):
            fns.append(lambda h, ch=ch: h.transpose(
                psum[:, ch:ch + 1], row_f32[0:1, ch * 128:(ch + 1) * 128], self.ident_f[:, :]))
        t.group("PE", fns, reads=[row_f32, self.ident_f], writes=[psum])
        t.op("DVE", lambda h: h.tensor_copy(col_out[:, 0:ncols], psum[:, 0:ncols]),
             reads=[psum], writes=[col_out])

    def col_to_row(self, col_f32, row_out, psum, ncols=32):
        t = self.t
        for blk in range(0, ncols, 4):
            nb = min(4, ncols - blk)
            fns = []
            for ch in range(blk, blk + nb):
                fns.append(lambda h, ch=ch: h.transpose(
                    psum[0:1, (ch % 4) * 128:(ch % 4) * 128 + 128],
                    col_f32[:, ch:ch + 1], self.ident_f[:, :]))
            t.group("PE", fns, reads=[col_f32, self.ident_f], writes=[psum])
            t.op("ACT", lambda h, blk=blk, nb=nb: h.copy(
                row_out[0:1, blk * 128:(blk + nb) * 128], psum[0:1, 0:nb * 128]),
                reads=[psum], writes=[row_out])

    def ln_stats(self, XG, shifted=False, sq_scratch=None):
        t = self.t
        PBa = self.PB[0]
        sumc, sqc, muc, rstdc = self.SC[0], self.SC[1], self.SC[2], self.SC[3]
        sq_tiles = sq_scratch

        def src_ap(dt_, c0, n):
            off = -1 if (shifted and dt_ >= 4) else 0
            return XG[dt_][:, c0 + off:c0 + off + n]

        for dt_ in range(NDT):
            t.op("ACT", lambda h, dt_=dt_: h.activation(
                sq_tiles[dt_][:, 0:T], src_ap(dt_, 1, T), AF.Square),
                reads=[XG[dt_]], writes=[sq_tiles[dt_]])
        for tc in range(NW):
            c0 = 1 + tc * 512
            self.mm(PBa[0:1, 0:512],
                    [(self.ones_col_bf[:, :], src_ap(dt_, c0, 512)) for dt_ in range(NDT)],
                    PBa, reads=list(XG) + [self.ones_col_bf])
            t.op("ACT", lambda h: h.copy(self.RS[0][0:1, 0:512], PBa[0:1, 0:512]),
                 reads=[PBa], writes=[self.RS[0]])
            self.row_to_col4(self.RS[0], sumc, tc * 4, self.PB[1])
            self.mm(PBa[0:1, 0:512],
                    [(self.ones_col_bf[:, :], sq_tiles[dt_][:, tc * 512:tc * 512 + 512])
                     for dt_ in range(NDT)],
                    PBa, reads=list(sq_tiles) + [self.ones_col_bf])
            t.op("ACT", lambda h: h.copy(self.RS[0][0:1, 0:512], PBa[0:1, 0:512]),
                 reads=[PBa], writes=[self.RS[0]])
            self.row_to_col4(self.RS[0], sqc, tc * 4, self.PB[1])
        t.op("DVE", lambda h: h.tensor_scalar_mul(muc[:, 0:32], sumc[:, 0:32], 1.0 / D),
             reads=[sumc], writes=[muc])
        t.op("DVE", lambda h: h.tensor_scalar_mul(sqc[:, 0:32], sqc[:, 0:32], 1.0 / D),
             reads=[sqc], writes=[sqc])
        t.op("DVE", lambda h: h.tensor_mul(sumc[:, 0:32], muc[:, 0:32], muc[:, 0:32]),
             reads=[muc], writes=[sumc])
        t.op("DVE", lambda h: h.tensor_sub(sqc[:, 0:32], sqc[:, 0:32], sumc[:, 0:32]),
             reads=[sqc, sumc], writes=[sqc])
        t.op("ACT", lambda h: h.activation(sqc[:, 0:32], sqc[:, 0:32], AF.Ln,
                                           bias=self.eps_ln[:, :]),
             reads=[sqc, self.eps_ln], writes=[sqc])
        t.op("ACT", lambda h: h.activation(rstdc[:, 0:32], sqc[:, 0:32], AF.Exp, scale=-0.5),
             reads=[sqc], writes=[rstdc])
        t.op("DVE", lambda h: h.tensor_mul(muc[:, 0:32], muc[:, 0:32], rstdc[:, 0:32]),
             reads=[muc, rstdc], writes=[muc])

    def row_to_col4(self, row_f32, col_out, cidx, psum):
        t = self.t
        fns = []
        for k in range(4):
            fns.append(lambda h, k=k: h.transpose(
                psum[:, k:k + 1], row_f32[0:1, k * 128:(k + 1) * 128], self.ident_f[0:1, 0:1]))
        t.group("PE", fns, reads=[row_f32, self.ident_f], writes=[psum])
        t.op("DVE", lambda h: h.tensor_copy(col_out[:, cidx:cidx + 4], psum[:, 0:4]),
             reads=[psum], writes=[col_out])

    def ln_apply(self, XG, OUT):
        t = self.t
        arep, brep = self.WK[0], self.WK[1]
        for tc in range(NW):
            c0 = tc * 512
            if tc % 2 == 0:
                ra, roa, rb, rob = self.RS[0], 0, self.RS[1], 0
                pca, pcb, pba, pbb = self.PB[2], self.PB[3], self.PB[0], self.PB[1]
            else:
                ra, roa, rb, rob = self.FS, 0, self.FS, 512
                pca, pcb, pba, pbb = self.PB[4], self.PW, self.PB[4], self.PW
            self.cols_to_row2(self.SC[3], tc * 4, 4, ra, roa, pca)
            self.mm(pba[0:128, 0:512],
                    [(self.ones_row_f[0:1, 0:128], ra[0:1, roa:roa + 512])],
                    pba, reads=[self.ones_row_f, ra])
            t.op("DVE", lambda h, c0=c0, pba=pba: h.tensor_copy(
                arep[:, c0:c0 + 512], pba[:, 0:512]),
                reads=[pba], writes=[arep])
            self.cols_to_row2(self.SC[2], tc * 4, 4, rb, rob, pcb)
            t.op("DVE", lambda h, rb=rb, rob=rob: h.tensor_scalar_mul(
                rb[0:1, rob:rob + 512], rb[0:1, rob:rob + 512], -1.0),
                reads=[rb], writes=[rb])
            self.mm(pbb[0:128, 0:512],
                    [(self.ones_row_f[0:1, 0:128], rb[0:1, rob:rob + 512])],
                    pbb, reads=[self.ones_row_f, rb])
            t.op("DVE", lambda h, c0=c0, pbb=pbb: h.tensor_copy(
                brep[:, c0:c0 + 512], pbb[:, 0:512]),
                reads=[pbb], writes=[brep])
        for dt_ in range(NDT):
            t.op("DVE", lambda h, dt_=dt_: h.tensor_mul(
                OUT[dt_][:, 1:T + 1], XG[dt_][:, 1:T + 1], arep[:, 0:T]),
                reads=[XG[dt_], arep], writes=[OUT[dt_]])
            t.op("DVE", lambda h, dt_=dt_: h.tensor_add(
                OUT[dt_][:, 1:T + 1], OUT[dt_][:, 1:T + 1], brep[:, 0:T]),
                reads=[OUT[dt_], brep], writes=[OUT[dt_]])
            t.op("DVE", lambda h, dt_=dt_: h.memset(OUT[dt_][:, 0:1], 0.0),
                 reads=[], writes=[OUT[dt_]])
            t.op("DVE", lambda h, dt_=dt_: h.memset(OUT[dt_][:, T + 1:COLS], 0.0),
                 reads=[], writes=[OUT[dt_]])

    def spill(self, XG, dram):
        t = self.t
        for dt_ in range(NDT):
            t.dma("ACT", lambda h, dt_=dt_: h.dma_start(
                dram[dt_, :, :], XG[dt_][:, :]), reads=[XG[dt_]], writes=[dram])

    def restore_add(self, XG, dram, OTHER):
        t = self.t
        f = self.WK[1]
        for dt_ in range(NDT):
            t.dma("ACT", lambda h, dt_=dt_: h.dma_start(
                f[:, 0:T], dram[dt_, :, 1:T + 1]), reads=[dram], writes=[f])
            t.op("DVE", lambda h, dt_=dt_: h.tensor_add(
                XG[dt_][:, 1:T + 1], OTHER[dt_][:, 1:T + 1], f[:, 0:T]),
                reads=[OTHER[dt_], f], writes=[XG[dt_]])
            t.op("DVE", lambda h, dt_=dt_: h.memset(XG[dt_][:, 0:1], 0.0),
                 reads=[], writes=[XG[dt_]])
            t.op("DVE", lambda h, dt_=dt_: h.memset(XG[dt_][:, T + 1:COLS], 0.0),
                 reads=[], writes=[XG[dt_]])

    def cols_to_row2(self, colbuf, cidx, n, rowt, rowoff, rowpsum):
        t = self.t
        fns = []
        for k in range(n):
            fns.append(lambda h, k=k: h.transpose(
                rowpsum[0:1, k * 128:(k + 1) * 128], colbuf[:, cidx + k:cidx + k + 1],
                self.ident_f[:, :]))
        t.group("PE", fns, reads=[colbuf, self.ident_f], writes=[rowpsum])
        t.op("ACT", lambda h, n=n: h.copy(rowt[0:1, rowoff:rowoff + n * 128],
                                          rowpsum[0:1, 0:n * 128]),
             reads=[rowpsum], writes=[rowt])

    def cols_to_row(self, colbuf, cidx, n, row_out, rowpsum):
        t = self.t
        fns = []
        for k in range(n):
            fns.append(lambda h, k=k: h.transpose(
                rowpsum[0:1, k * 128:(k + 1) * 128], colbuf[:, cidx + k:cidx + k + 1],
                self.ident_f[:, :]))
        t.group("PE", fns, reads=[colbuf, self.ident_f], writes=[rowpsum])
        t.op("ACT", lambda h, n=n: h.copy(row_out[0:1, 0:n * 128], rowpsum[0:1, 0:n * 128]),
             reads=[rowpsum], writes=[row_out])

    def bcast_row(self, row_f32, ncols, psum_out, po=128):
        self.mm(psum_out[0:po, 0:ncols],
                [(self.ones_row_f[0:1, 0:po], row_f32[0:1, 0:ncols])],
                psum_out, reads=[self.ones_row_f, row_f32])

    def ffn(self, first=True):
        t, din = self.t, self.din
        taps = os.environ.get("MK_TAPS", "")
        if not self.ffn_converted:
            self.ffn_converted = True
            for l in range(2):
                for j in range(NJ):
                    for ag in range(2):
                        t.dma("POOL", lambda h, l=l, j=j, ag=ag: h.dma_start(
                            self.d_w1b[l, j, ag], self.din["w1dev"][l, j, ag]),
                            writes=[self.d_w1b])
                    for grp in range(2):
                        t.dma("POOL", lambda h, l=l, j=j, grp=grp: h.dma_start(
                            self.d_w2b[l, j, grp],
                            self.din["w2p"][l, j * 128:(j + 1) * 128,
                                            grp * 512:(grp + 1) * 512]),
                            writes=[self.d_w2b])
        self.ln_stats(self.XM, shifted=False, sq_scratch=self.XS)
        self.ln_apply(self.XM, self.XS)
        if "ln" in taps:
            self.tap("ln", self.XS)
        self.spill(self.XM, self.d_xmod)
        for l in range(2):
            self.load_layer_smalls(l)
            self.ffn_layer(l, taps if l == 0 else "")
        self.restore_add(self.XM, self.d_xmod, self.XS)

    def ffn_layer(self, l, taps=""):
        t, din = self.t, self.din
        for dt_ in range(NDT):
            t.dma("POOL", lambda h, dt_=dt_: h.dma_start(
                self.G8[dt_][:, 0:1536], din["wqkv"][l, dt_ * 128:(dt_ + 1) * 128, :]),
                writes=[self.G8[dt_]])
        for hh in range(HEADS):
            t.dma("POOL", lambda h, hh=hh: h.dma_start(
                self.WO8[hh][0:64, :], din["wo"][l, hh * 64:(hh + 1) * 64, :]),
                writes=[self.WO8[hh]])
        for xb in (self.XM[0], self.XM[1], self.XM[2], self.XM[3], self.VT7):
            t.op("DVE", lambda h, xb=xb: h.memset(xb[:, :], 0.0), writes=[xb])
        for hh in range(HEADS):
            vb, vc = self.vtbuf(hh)
            for half in range(2):
                for jt in range(4):
                    c = vc + half * 260 + jt * 65 + 64
                    t.op("DVE", lambda h, vb=vb, c=c: h.memset(vb[:, c:c + 1], 1.0),
                         writes=[vb])
        for w in range(NW):
            self.ffn_window(l, w, taps if w == 0 else "")
        if "attn" in taps:
            self.tap("attn", self.XS)
        self.ffn_mlp(l)
        if "mlp" in taps:
            self.tap("mlp", self.XS)

    def ffn_window(self, l, w, taps=""):
        t, din = self.t, self.din
        c0 = 1 + w * WIN
        jts = list(range(4, 8)) if w == 0 else list(range(8))
        for k in range(4):
            tb, tcc = self.tabq(k)
            t.dma("SP" if k % 2 == 0 else "ACT",
                  lambda h, k=k, tb=tb, tcc=tcc: h.dma_start(
                      tb[:, tcc:tcc + 512], self.d_tabsb[k, :, w * WIN:(w + 1) * WIN]),
                  reads=[self.d_tabsb], writes=[tb])
        if w > 0:
            for hh in range(HEADS):
                kb, kc = self.kbuf(hh)
                t.op("DVE", lambda h, kb=kb, kc=kc: h.tensor_copy(
                    kb[:, kc:kc + 512], kb[:, kc + 512:kc + 1024]), reads=[kb], writes=[kb])
                vb, vc = self.vtbuf(hh)
                t.op("DVE", lambda h, vb=vb, vc=vc: h.tensor_copy(
                    vb[:, vc:vc + 260], vb[:, vc + 260:vc + 520]), reads=[vb], writes=[vb])
        qraw = self.qraw_buf()
        q2 = self.q2buf()
        vraw = self.WK[1]
        for ct in range(12):
            ps = self.PB[ct % 2]
            if w == 0:
                pairs = []
                for dt_ in range(NDT):
                    cs = c0 if dt_ < 4 else c0 - 1
                    pairs.append((self.G8[dt_][:, ct * 128:(ct + 1) * 128],
                                  self.XS[dt_][:, cs:cs + 512]))
                self.mm(ps[:, 0:512], pairs, ps, reads=self.G8 + self.XS)
            else:
                fns = []
                for k, dt_ in enumerate((0, 1, 2)):
                    fns.append(lambda h, ct=ct, dt_=dt_, ps=ps, k=k: h.matmul(
                        ps[:, 0:512], self.G8[dt_][:, ct * 128:(ct + 1) * 128],
                        self.XS[dt_][:, c0:c0 + 512], start=(k == 0), stop=False))
                for dt_ in range(4, 8):
                    fns.append(lambda h, ct=ct, dt_=dt_, ps=ps: h.matmul(
                        ps[:, 0:1], self.G8[dt_][:, ct * 128:(ct + 1) * 128],
                        self.BCOLW[:, dt_ - 4:dt_ - 3], start=False, stop=False))
                    fns.append(lambda h, ct=ct, dt_=dt_, ps=ps: h.matmul(
                        ps[:, 1:512], self.G8[dt_][:, ct * 128:(ct + 1) * 128],
                        self.XS[dt_][:, c0:c0 + 511], start=False, stop=False))
                fns.append(lambda h, ct=ct, ps=ps: h.matmul(
                    ps[:, 0:512], self.G8[3][:, ct * 128:(ct + 1) * 128],
                    self.XS[3][:, c0:c0 + 512], start=False, stop=True))
                t.group("PE", fns, reads=self.G8 + self.XS + [self.BCOLW], writes=[ps])
            if ct < 8:
                t.op("ACT", lambda h, ct=ct, ps=ps: h.copy(
                    qraw[:, ct * 512:(ct + 1) * 512], ps[:, 0:512]),
                    reads=[ps], writes=[qraw])
                t.op("ACT", lambda h, ct=ct, ps=ps: h.activation(
                    q2[:, ct * 512:(ct + 1) * 512], ps[:, 0:512],
                    AF.Square), reads=[ps], writes=[q2])
            else:
                t.op("ACT", lambda h, ct=ct, ps=ps: h.copy(
                    vraw[:, 2048 + (ct - 8) * 512:2048 + (ct - 7) * 512], ps[:, 0:512]),
                    reads=[ps], writes=[vraw])
        pairs = []
        rds = []
        for i in range(4):
            pairs.append((self.sel[f"sel_q{i}"][:, :], q2[:, i * 512:(i + 1) * 512]))
            rds.append(self.sel[f"sel_q{i}"])
        for i in range(4):
            pairs.append((self.sel[f"sel_k{i}"][:, :], q2[:, (4 + i) * 512:(5 + i) * 512]))
            rds.append(self.sel[f"sel_k{i}"])
        self.mm(self.PB[2][0:16, 0:512], pairs, self.PB[2], reads=rds + [q2])
        C16 = self.FS
        t.op("DVE", lambda h: h.tensor_scalar_max(C16[0:16, 0:512],
                                                  self.PB[2][0:16, 0:512], 1e-24),
             reads=[self.PB[2]], writes=[C16])
        t.op("ACT", lambda h: h.activation(C16[0:16, 0:512], C16[0:16, 0:512],
                                           AF.Ln), reads=[C16], writes=[C16])
        t.op("ACT", lambda h: h.activation(self.C16B[0:16, 0:512], C16[0:16, 0:512],
                                           AF.Exp, scale=-0.5), reads=[C16], writes=[self.C16B])
        for ct in range(8):
            isq = ct < 4
            i4 = ct % 4
            rsel = self.sel[("rsel_q%d" if isq else "rsel_k%d") % i4]
            prr = self.PB[3] if ct % 2 == 0 else self.PB[0]
            prot = self.PB[4] if ct % 2 == 0 else self.PB[1]
            buf = self.WK[0] if ct % 2 == 0 else self.XM[7]
            self.mm(prr[:, 0:512], [(rsel[:, :], self.C16B[0:16, 0:512])],
                    prr, reads=[rsel, self.C16B])
            qn1 = buf
            t.op("DVE", lambda h, ct=ct, qn1=qn1, prr=prr: h.tensor_mul(
                qn1[:, 0:512], qraw[:, ct * 512:(ct + 1) * 512], prr[:, 0:512]),
                reads=[qraw, prr], writes=[qn1])
            self.mm(prot[:, 0:512], [(self.rotm_bf[:, :], qn1[:, 0:512])],
                    prot, reads=[self.rotm_bf, qn1])
            tb_c, tcc_c = self.tabq(0 if isq else 2)
            tb_s, tcc_s = self.tabq(1 if isq else 3)
            t1, t2 = buf, buf
            gc = self.qs_col if isq else self.ks_col
            gcr = self.qsr_col if isq else self.ksr_col
            t.op("DVE", lambda h, tb_c=tb_c, tcc_c=tcc_c, gc=gc, qn1=qn1, t1=t1:
                 h.scalar_tensor_tensor(
                     t1[:, 512:1024], qn1[:, 0:512], gc[:, :], tb_c[:, tcc_c:tcc_c + 512],
                     ALU.mult, ALU.mult),
                 reads=[qn1, gc, tb_c], writes=[t1])
            t.op("DVE", lambda h, tb_s=tb_s, tcc_s=tcc_s, gcr=gcr, t2=t2, prot=prot:
                 h.scalar_tensor_tensor(
                     t2[:, 1024:1536], prot[:, 0:512], gcr[:, :], tb_s[:, tcc_s:tcc_s + 512],
                     ALU.mult, ALU.mult),
                 reads=[prot, gcr, tb_s], writes=[t2])
            for half in range(2):
                hh = 2 * i4 + half
                r0 = half * 64
                eng = "DVE" if half == 0 else "POOL"
                if isq:
                    qb, qc = self.qbuf(hh)
                    t.op(eng, lambda h, qb=qb, qc=qc, r0=r0, t1=t1, t2=t2: h.tensor_add(
                        qb[r0:r0 + 64, qc:qc + 512], t1[r0:r0 + 64, 512:1024],
                        t2[r0:r0 + 64, 1024:1536]), reads=[t1, t2], writes=[qb])
                else:
                    kb, kc = self.kbuf(hh)
                    t.op(eng, lambda h, kb=kb, kc=kc, r0=r0, t1=t1, t2=t2: h.tensor_add(
                        kb[r0:r0 + 64, kc + 512:kc + 1024], t1[r0:r0 + 64, 512:1024],
                        t2[r0:r0 + 64, 1024:1536]), reads=[t1, t2], writes=[kb])
        for hh in range(HEADS):
            vb, vc = self.vtbuf(hh)
            vr0 = (hh % 2) * 64
            vcol = (hh // 2) * 512
            fns = []
            for jt in range(4):
                fns.append(lambda h, jt=jt, vr0=vr0, vcol=vcol: h.transpose(
                    self.PT[:, jt * 64:jt * 64 + 64],
                    vraw[vr0:vr0 + 64, 2048 + vcol + jt * 128:2048 + vcol + (jt + 1) * 128],
                    self.ident_bf[vr0:vr0 + 64, vr0:vr0 + 64]))
            t.group("PE", fns, reads=[vraw, self.ident_bf], writes=[self.PT])
            for jt in range(4):
                c = vc + 260 + jt * 65
                t.op("DVE", lambda h, jt=jt, vb=vb, c=c: h.tensor_copy(
                    vb[:, c:c + 64], self.PT[:, jt * 64:jt * 64 + 64]),
                    reads=[self.PT], writes=[vb])
        if "qkv" in taps:
            self.tap("qkv_qraw", [qraw], cols=T)
            self.tap("qkv_v", [self.WK[1]], cols=T)
        ptail = None
        for hh in range(HEADS):
            ptail = self.ffn_head(l, w, hh, jts, prev_tail=ptail)
        ptail()
        if "qkv" in taps:
            self.tap("qbuf", [self.XM[0]], cols=T)
            self.tap("kbuf", [self.XM[1], self.XM[2]], cols=T)
        if "ofin" in taps:
            self.tap("ofin", [self.XM[5]], cols=T)
        for dt_ in range(4, 8):
            t.op("DVE", lambda h, dt_=dt_: h.tensor_copy(
                self.BCOLW[:, dt_ - 4:dt_ - 3], self.XS[dt_][:, c0 + 511:c0 + 512]),
                reads=[self.XS[dt_]], writes=[self.BCOLW])
        ofin = self.XM[5]
        for dt_ in range(NDT):
            ps = self.PB[dt_ % 2]
            pairs = []
            for hh in range(HEADS):
                pairs.append((self.WO8[hh][0:64, dt_ * 128:(dt_ + 1) * 128],
                              ofin[0:64, hh * 512:(hh + 1) * 512]))
            self.mm(ps[:, 0:512], pairs, ps,
                    reads=[self.WO8[hh2] for hh2 in range(HEADS)] + [ofin])
            t.op("DVE", lambda h, dt_=dt_, ps=ps, c0=c0: h.tensor_add(
                self.XS[dt_][:, c0:c0 + 512], self.XS[dt_][:, c0:c0 + 512], ps[:, 0:512]),
                reads=[ps, self.XS[dt_]], writes=[self.XS[dt_]])

    def ffn_head(self, l, w, hh, jts, prev_tail=None):
        t, din = self.t, self.din
        qb, qc = self.qbuf(hh)
        kb, kc = self.kbuf(hh)
        vb, vc = self.vtbuf(hh)
        PO = self.PB[4]

        def emit_scores(it):
            mask, mo = self.masks["mask_mid"], 0
            nblank = max(0, 512 - it * 128) if w == 0 else 0
            if it % 2 == 0:
                pslo, plc, pshi, phc = self.PW, 0, self.PW, 512
            else:
                pslo, plc, pshi, phc = self.PB[0], 0, self.PB[1], 0
            negm = self.SC[it]
            fns = [
                lambda h, it=it, pslo=pslo, plc=plc: h.matmul(
                    pslo[:, plc:plc + 512],
                    qb[:, qc + it * 128:qc + (it + 1) * 128],
                    kb[:, kc + it * 128:kc + it * 128 + 512],
                    start=True, stop=False),
                lambda h, it=it, pshi=pshi, phc=phc: h.matmul(
                    pshi[:, phc:phc + 128],
                    qb[:, qc + it * 128:qc + (it + 1) * 128],
                    kb[:, kc + it * 128 + 512:kc + it * 128 + 640],
                    start=True, stop=False),
                lambda h, mask=mask, mo=mo, pslo=pslo, plc=plc, nblank=nblank: h.matmul(
                    pslo[:, plc:plc + 512], self.ident_bf[:, :],
                    mask[:, mo:mo + 512], start=False, stop=(nblank == 0)),
                lambda h, mask=mask, mo=mo, pshi=pshi, phc=phc: h.matmul(
                    pshi[:, phc:phc + 128], self.ident_bf[:, :],
                    mask[:, mo + 512:mo + 640], start=False, stop=True),
            ]
            if nblank > 0:
                fns.append(lambda h, pslo=pslo, plc=plc, nblank=nblank: h.matmul(
                    pslo[:, plc:plc + 512], self.ident_bf[:, :],
                    self.negbig_bf[:, 512 - nblank:1024 - nblank],
                    start=False, stop=True))
            wl = [self.PW] if it % 2 == 0 else [self.PB[0], self.PB[1]]
            rl = [qb, kb, self.ident_bf, mask] + ([self.negbig_bf] if nblank else [])
            t.group("PE", fns, reads=rl, writes=wl)
            ES, eb = self.es_slot(it)
            if it % 2 == 0:
                t.op("DVE", lambda h, it=it, negm=negm: h.reduce_max(
                    negm[:, 0:1], self.PW[:, 0:640], axis=AX.X, negate=True),
                    reads=[self.PW], writes=[negm])
                t.op("ACT", lambda h, it=it, ES=ES, eb=eb, negm=negm: h.activation(
                    ES[:, eb:eb + 640], self.PW[:, 0:640], AF.Exp, bias=negm[:, 0:1]),
                    reads=[self.PW, negm], writes=[ES])
            else:
                t.op("DVE", lambda h, it=it, negm=negm: h.reduce_max(
                    negm[:, 0:1], self.PB[0][:, 0:512], axis=AX.X, negate=True),
                    reads=[self.PB[0]], writes=[negm])
                t.op("DVE", lambda h, it=it, negm=negm: h.reduce_max(
                    negm[:, 1:2], self.PB[1][:, 0:128], axis=AX.X, negate=True),
                    reads=[self.PB[1]], writes=[negm])
                t.op("DVE", lambda h, it=it, negm=negm: h.tensor_tensor(
                    negm[:, 0:1], negm[:, 0:1], negm[:, 1:2],
                    ALU.min), reads=[negm], writes=[negm])
                t.op("ACT", lambda h, it=it, ES=ES, eb=eb, negm=negm: h.activation(
                    ES[:, eb:eb + 512], self.PB[0][:, 0:512], AF.Exp, bias=negm[:, 0:1]),
                    reads=[self.PB[0], negm], writes=[ES])
                t.op("ACT", lambda h, it=it, ES=ES, eb=eb, negm=negm: h.activation(
                    ES[:, eb + 512:eb + 640], self.PB[1][:, 0:128], AF.Exp,
                    bias=negm[:, 0:1]),
                    reads=[self.PB[1], negm], writes=[ES])

        AT = self.XM[7]

        def emit_tr(it):
            ES, eb = self.es_slot(it)
            fns = []
            for jt in range(5):
                fns.append(lambda h, jt=jt, ES=ES, eb=eb: h.transpose(
                    self.PT[:, jt * 128:(jt + 1) * 128],
                    ES[:, eb + jt * 128:eb + (jt + 1) * 128], self.ident_bf[:, :]))
            t.group("PE", fns, reads=[ES, self.ident_bf], writes=[self.PT])
            t.op("ACT", lambda h, it=it: h.copy(
                bass.AP(AT, it * 640, [[COLS, 128], [512, 5], [1, 128]]),
                self.PT[:, 0:640]),
                reads=[self.PT], writes=[AT])

        def emit_av():
            fns = [lambda h: h.matmul(PO[0:65, 0:512], vb[:, vc:vc + 65],
                                      self.negbig_bf[:, 512:1024],
                                      start=True, stop=False)]
            for jg in range(8):
                i0 = max(0, jg - 4)
                i1 = min(3, jg) + 1
                half = 0 if jg < 4 else 1
                cc = vc + half * 260 + (jg % 4) * 65
                fns.append(lambda h, jg=jg, i0=i0, i1=i1, cc=cc: h.matmul(
                    PO[0:65, i0 * 128:i1 * 128], vb[:, cc:cc + 65],
                    AT[:, jg * 512 + i0 * 128:jg * 512 + i1 * 128],
                    start=False, stop=False))
            fns.append(lambda h: h.matmul(PO[0:65, 0:512], vb[:, vc:vc + 65],
                                          self.negbig_bf[:, 512:1024],
                                          start=False, stop=True))
            t.group("PE", fns, reads=[vb, AT, self.negbig_bf], writes=[PO])

        emit_scores(0)
        emit_scores(1)
        emit_scores(2)
        emit_tr(0)
        emit_scores(3)
        if prev_tail is not None:
            prev_tail()
        emit_tr(1)
        emit_tr(2)
        emit_tr(3)
        emit_av()

        def tail():
            zrow = self.WK[0]
            t.op("ACT", lambda h: h.activation(self.FS[64:65, 0:512], PO[64:65, 0:512],
                                               AF.Ln), reads=[PO], writes=[self.FS])
            t.op("ACT", lambda h: h.activation(zrow[64:65, 3584:4096],
                                               self.FS[64:65, 0:512],
                                               AF.Exp, scale=-1.0),
                 reads=[self.FS], writes=[zrow])
            zp = self.PB[2]
            self.mm(zp[0:64, 0:512],
                    [(self.l1_bf[64:65, 64:128], zrow[64:65, 3584:4096])],
                    zp, reads=[self.l1_bf, zrow])
            zrep = self.WK[0]
            t.op("ACT", lambda h, zp=zp: h.copy(zrep[0:64, 3072:3584], zp[0:64, 0:512]),
                 reads=[zp], writes=[zrep])
            ob, oc = self.ofin_buf(hh)
            t.op("DVE", lambda h, ob=ob, oc=oc: h.tensor_mul(
                ob[0:64, oc:oc + 512], PO[0:64, 0:512], zrep[0:64, 3072:3584]),
                reads=[PO, zrep], writes=[ob])
        return tail

    def ffn_mlp(self, l):
        t, din = self.t, self.din
        self.ln_stats(self.XS, shifted=True, sq_scratch=[self.XM[i] for i in range(NDT)])
        t.op("DVE", lambda h: h.tensor_copy(
            self.lgb_bf[:, 0:16:2], self.lnb_col[:, 0:8]), reads=[self.lnb_col],
            writes=[self.lgb_bf])
        t.op("DVE", lambda h: h.tensor_copy(
            self.lgb_bf[:, 1:16:2], self.lng_col[:, 0:8]), reads=[self.lng_col],
            writes=[self.lgb_bf])
        for tcp in range(4):
            self.ffn_mlp_tcpair(l, 2 * tcp, 2 * tcp + 1)

    def mstB(self, j):
        return [self.XM[3], self.XM[4], self.XM[5]][j // 8], (j % 8) * 512

    def ffn_mlp_tcpair(self, l, ta, tb):
        t, din = self.t, self.din
        first_tc = ta == 0
        Z3 = [self.WK[0], self.WK[1]]
        rrepb = self.G8[6]
        mrow, rrow = self.RS[1], self.RS[0]
        for idx, tc in ((0, ta), (1, tb)):
            c0 = 1 + tc * WIN
            z3 = Z3[idx]
            self.cols_to_row(self.SC[3], tc * 4, 4, rrow, self.PB[0])
            self.bcast_row(rrow, 512, self.PB[1])
            t.op("DVE", lambda h, idx=idx: h.tensor_copy(
                rrepb[:, idx * 512:idx * 512 + 512], self.PB[1][:, 0:512]),
                reads=[self.PB[1]], writes=[rrepb])
            self.cols_to_row(self.SC[2], tc * 4, 4, mrow, self.PB[0])
            t.op("DVE", lambda h: h.tensor_scalar_mul(mrow[0:1, 0:512],
                                                      mrow[0:1, 0:512], -1.0),
                 reads=[mrow], writes=[mrow])
            self.bcast_row(mrow, 512, self.PW)
            for dt_ in range(NDT):
                if dt_ < 4 or idx == 1 or tc == 0:
                    cs = c0 if dt_ < 4 else c0 - 1
                    t.op("DVE", lambda h, dt_=dt_, cs=cs, z3=z3, idx=idx:
                         h.scalar_tensor_tensor(
                             z3[:, dt_ * 512:(dt_ + 1) * 512], self.XS[dt_][:, cs:cs + 512],
                             self.lng_col[:, dt_:dt_ + 1],
                             rrepb[:, idx * 512:idx * 512 + 512], ALU.mult, ALU.mult),
                         reads=[self.XS[dt_], self.lng_col, rrepb], writes=[z3])
                else:
                    t.op("DVE", lambda h, dt_=dt_, z3=z3, idx=idx: h.scalar_tensor_tensor(
                        z3[:, dt_ * 512:dt_ * 512 + 1], self.BCOLM[:, dt_ - 4:dt_ - 3],
                        self.lng_col[:, dt_:dt_ + 1],
                        rrepb[:, idx * 512:idx * 512 + 1], ALU.mult, ALU.mult),
                        reads=[self.BCOLM, self.lng_col, rrepb], writes=[z3])
                    t.op("DVE", lambda h, dt_=dt_, z3=z3, idx=idx, c0=c0:
                         h.scalar_tensor_tensor(
                             z3[:, dt_ * 512 + 1:(dt_ + 1) * 512],
                             self.XS[dt_][:, c0:c0 + 511],
                             self.lng_col[:, dt_:dt_ + 1],
                             rrepb[:, idx * 512 + 1:idx * 512 + 512], ALU.mult, ALU.mult),
                         reads=[self.XS[dt_], self.lng_col, rrepb], writes=[z3])
                t.op("DVE", lambda h, dt_=dt_, z3=z3: h.scalar_tensor_tensor(
                    z3[:, dt_ * 512:(dt_ + 1) * 512], self.PW[:, 0:512],
                    self.lng_col[:, dt_:dt_ + 1], z3[:, dt_ * 512:(dt_ + 1) * 512],
                    ALU.mult, ALU.add),
                    reads=[self.PW, self.lng_col, z3], writes=[z3])
        GG = self.G8[7]
        for j in range(NJ):
            PR = self.XM[6] if j % 2 == 0 else self.XM[7]
            for ag in range(2):
                t.dma("POOL" if ag == 0 else "SP",
                      lambda h, j=j, ag=ag, PR=PR: h.dma_start(
                          PR[:, ag * 1024:(ag + 1) * 1024], self.d_w1b[l, j, ag]),
                      reads=[self.d_w1b], writes=[PR])
            if first_tc:
                for ag, (bias, b1c) in enumerate(((self.BIASA, self.b1a_col),
                                                  (self.BIASG, self.b1g_col))):
                    pc = self.PB[4]
                    pairs = [(PR[:, ag * 1024 + dt_ * 128:ag * 1024 + (dt_ + 1) * 128],
                              self.lgb_bf[:, 2 * dt_:2 * dt_ + 1])
                             for dt_ in range(NDT)]
                    self.mm(pc[0:128, 1:2], pairs, pc, reads=[self.lgb_bf, PR])
                    t.op("DVE", lambda h, j=j, bias=bias, b1c=b1c: h.scalar_tensor_tensor(
                        bias[:, j:j + 1], pc[0:128, 1:2], 1.0, b1c[:, j:j + 1],
                        ALU.mult, ALU.add), reads=[pc, b1c], writes=[bias])
            for ag in range(2):
                psa, psb = (self.PB[0], self.PB[1]) if ag == 0 else (self.PB[2], self.PB[3])
                fns = []
                for dt_ in range(NDT):
                    for k, ps in ((0, psa), (1, psb)):
                        fns.append(lambda h, ag=ag, dt_=dt_, ps=ps, PR=PR, k=k: h.matmul(
                            ps[:, 0:512],
                            PR[:, ag * 1024 + dt_ * 128:ag * 1024 + (dt_ + 1) * 128],
                            Z3[k][:, dt_ * 512:(dt_ + 1) * 512],
                            start=(dt_ == 0), stop=(dt_ == NDT - 1)))
                t.group("PE", fns, reads=[PR, Z3[0], Z3[1]], writes=[psa, psb])
            for k in range(2):
                psa = self.PB[0] if k == 0 else self.PB[1]
                psg = self.PB[2] if k == 0 else self.PB[3]
                t.op("ACT", lambda h, j=j, psg=psg, k=k: h.activation(
                    GG[:, k * 512:(k + 1) * 512], psg[:, 0:512], AF.Gelu,
                    bias=self.BIASG[:, j:j + 1]),
                    reads=[psg, self.BIASG], writes=[GG])
                mb, mc = self.mst(j) if k == 0 else self.mstB(j)
                t.op("DVE", lambda h, j=j, mb=mb, mc=mc, psa=psa, k=k:
                     h.scalar_tensor_tensor(
                         mb[:, mc:mc + 512], psa[:, 0:512], self.BIASA[:, j:j + 1],
                         GG[:, k * 512:(k + 1) * 512], ALU.add, ALU.mult),
                     reads=[psa, self.BIASA, GG], writes=[mb])
        for idx, tc in ((0, ta), (1, tb)):
            c0 = 1 + tc * WIN
            for grp in range(2):
                for j in range(NJ):
                    wb = self.XM[7] if j % 2 == 0 else self.XM[6]
                    t.dma("POOL" if j % 2 == 0 else "SP",
                          lambda h, j=j, grp=grp, wb=wb: h.dma_start(
                              wb[:, 0:512], self.d_w2b[l, j, grp]),
                          reads=[self.d_w2b], writes=[wb])
                    mb, mc = self.mst(j) if idx == 0 else self.mstB(j)
                    fns = []
                    pss = []
                    for k, dt_ in enumerate(range(grp * 4, grp * 4 + 4)):
                        ps = self.PB[k]
                        pss.append(ps)
                        fns.append(lambda h, j=j, dt_=dt_, ps=ps, wb=wb, mb=mb, mc=mc,
                                   grp=grp:
                                   h.matmul(ps[:, 0:512],
                                            wb[:, (dt_ - grp * 4) * 128:
                                               (dt_ - grp * 4 + 1) * 128],
                                            mb[:, mc:mc + 512],
                                            start=(j == 0), stop=(j == NJ - 1)))
                    t.group("PE", fns, reads=[wb, mb], writes=pss)
                if grp == 1 and idx == 1:
                    for dt_ in range(4, 8):
                        t.op("DVE", lambda h, dt_=dt_, c0=c0: h.tensor_copy(
                            self.BCOLM[:, dt_ - 4:dt_ - 3],
                            self.XS[dt_][:, c0 + 511:c0 + 512]),
                            reads=[self.XS[dt_]], writes=[self.BCOLM])
                for k, dt_ in enumerate(range(grp * 4, grp * 4 + 4)):
                    ps = self.PB[k]
                    t.op("DVE", lambda h, dt_=dt_, ps=ps, c0=c0: h.scalar_tensor_tensor(
                        self.XS[dt_][:, c0:c0 + 512], ps[:, 0:512],
                        self.b2_col[:, dt_:dt_ + 1], self.XS[dt_][:, c0:c0 + 512],
                        ALU.add, ALU.add), reads=[ps, self.b2_col, self.XS[dt_]],
                        writes=[self.XS[dt_]])

    def attn_a(self):
        t, din = self.t, self.din
        taps = os.environ.get("MK_TAPS", "")
        self.ln_stats(self.XM, shifted=False, sq_scratch=self.XS)
        self.ln_apply(self.XM, self.XS)
        if "aln" in taps:
            self.tap("aln", self.XS)
        self.spill(self.XM, self.d_xmod)
        self.attn_a_fold()
        for ft in range(5):
            self.attn_a_ft(ft, taps if ft == 0 else "")
        self.attn_a_ifft(taps)
        for dt_ in range(NDT):
            t.op("DVE", lambda h, dt_=dt_: h.tensor_copy(
                self.XM[dt_][:, 0:COLS], self.XS[dt_][:, 0:COLS]),
                reads=[self.XS[dt_]], writes=[self.XM[dt_]])

    def attn_a_fold(self):
        t, din = self.t, self.din
        bcol = self.bcol_bf
        def dslot(ci, ec):
            return self.XM[ci * 4 + ec // 2], (ec % 2) * 640
        for ci, dnm in enumerate(("dft_cr", "dft_ci")):
            for ec in range(NDT):
                xb, xc = dslot(ci, ec)
                t.op("DVE", lambda h, xb=xb, xc=xc: h.memset(
                    xb[:, xc + 513:xc + 640], 0.0), writes=[xb])
                t.dma("POOL", lambda h, dnm=dnm, ec=ec, xb=xb, xc=xc: h.dma_start(
                    xb[:, xc:xc + 513], din[dnm][ec * 128:(ec + 1) * 128, :]),
                    writes=[xb])
        for wnm, bnm, cbase in (("Wq", "bq", 0), ("Wk", "bk", 2), ("Wv", "bv", 4)):
            for ec in range(NDT):
                t.dma("POOL", lambda h, wnm=wnm, ec=ec: h.dma_start(
                    self.G8[ec][:, 0:1024], din[wnm][ec * 128:(ec + 1) * 128, :]),
                    writes=[self.G8[ec]])
                t.dma("POOL", lambda h, bnm=bnm, ec=ec: h.dma_start(
                    bcol[:, ec:ec + 1],
                    bass.AP(din[bnm], ec * 128, [[1, 128], [1, 1]])),
                    writes=[bcol])
            for ci in range(2):
                comp = cbase + ci
                dnm = "dft_cr" if ci == 0 else "dft_ci"
                for half in range(4):
                    dts = (2 * half, 2 * half + 1)
                    for ec in range(NDT):
                        xb, xc = dslot(ci, ec)
                        if half == 0:
                            self.mm(self.PW[0:1, 0:512],
                                    [(bcol[:, ec:ec + 1], xb[:, xc:xc + 512])], self.PW,
                                    reads=[bcol, xb], start=(ec == 0), stop=(ec == 7))
                            self.mm(self.PW[0:1, 512:640],
                                    [(bcol[:, ec:ec + 1], xb[:, xc + 512:xc + 640])],
                                    self.PW,
                                    reads=[bcol, xb], start=(ec == 0), stop=(ec == 7))
                        for k, dt_ in enumerate(dts):
                            ps = self.PB[k]
                            self.mm(ps[:, 0:512],
                                    [(self.G8[ec][:, dt_ * 128:(dt_ + 1) * 128],
                                      xb[:, xc:xc + 512])],
                                    ps, reads=[self.G8[ec], xb], start=(ec == 0),
                                    stop=(ec == 7))
                            ps2 = self.PB[2 + k]
                            self.mm(ps2[:, 0:128],
                                    [(self.G8[ec][:, dt_ * 128:(dt_ + 1) * 128],
                                      xb[:, xc + 512:xc + 640])],
                                    ps2, reads=[self.G8[ec], xb], start=(ec == 0),
                                    stop=(ec == 7))
                    for k, dt_ in enumerate(dts):
                        out = self.WK[0]
                        t.op("DVE", lambda h, k=k: h.tensor_copy(
                            out[:, 0:512], self.PB[k][:, 0:512]),
                            reads=[self.PB[k]], writes=[out])
                        t.op("DVE", lambda h, k=k: h.tensor_copy(
                            out[:, 512:640], self.PB[2 + k][:, 0:128]),
                            reads=[self.PB[2 + k]], writes=[out])
                        t.dma("ACT", lambda h, comp=comp, dt_=dt_: h.dma_start(
                            self.d_wf[comp, dt_, :, :], out[:, 0:640]), reads=[out], writes=[self.d_wf])
                t.op("ACT", lambda h: h.copy(self.RS[0][0:1, 0:512], self.PW[0:1, 0:512]),
                     reads=[self.PW], writes=[self.RS[0]])
                t.op("ACT", lambda h: h.copy(self.RS[1][0:1, 0:128], self.PW[0:1, 512:640]),
                     reads=[self.PW], writes=[self.RS[1]])
                fns = []
                for ftk in range(4):
                    fns.append(lambda h, ftk=ftk: h.transpose(
                        self.PB[4][:, ftk:ftk + 1],
                        self.RS[0][0:1, ftk * 128:(ftk + 1) * 128],
                        self.ident_f[0:1, 0:1]))
                fns.append(lambda h: h.transpose(
                    self.PB[4][:, 4:5], self.RS[1][0:1, 0:128], self.ident_f[0:1, 0:1]))
                t.group("PE", fns, reads=[self.RS[0], self.RS[1], self.ident_f],
                        writes=[self.PB[4]])
                t.op("DVE", lambda h, comp=comp: h.tensor_copy(
                    self.bhat_col[:, comp * 5:comp * 5 + 5], self.PB[4][:, 0:5]),
                    reads=[self.PB[4]], writes=[self.bhat_col])

    def attn_a_ft(self, ft, taps=""):
        t, din = self.t, self.din
        XM = self.XM
        for comp in range(6):
            for dt_ in range(NDT):
                t.dma("POOL", lambda h, comp=comp, dt_=dt_, ft=ft: h.dma_start(
                    self.G8[dt_][:, comp * 128:(comp + 1) * 128],
                    self.d_wf[comp, dt_, :, ft * 128:(ft + 1) * 128]),
                    reads=[self.d_wf], writes=[self.G8[dt_]])
        for comp in range(6):
            for tc in range(NW):
                ps = self.PB[(comp + tc) % 2]
                pairs = [(self.G8[dt_][:, comp * 128:(comp + 1) * 128],
                          self.XS[dt_][:, 1 + tc * 512:1 + (tc + 1) * 512])
                         for dt_ in range(NDT)]
                self.mm(ps[:, 0:512], pairs, ps, reads=self.G8 + self.XS)
                t.op("DVE", lambda h, comp=comp, tc=tc, ps=ps, ft=ft: h.tensor_scalar_add(
                    XM[comp][:, tc * 512:(tc + 1) * 512], ps[:, 0:512],
                    self.bhat_col[:, comp * 5 + ft:comp * 5 + ft + 1]),
                    reads=[ps, self.bhat_col], writes=[XM[comp]])
        if "proj" in taps:
            self.tap("proj", [XM[i] for i in range(6)], cols=T)
        m2s = [self.WK[0], self.WK[1], XM[6]]
        TMP = XM[7]
        for s in range(3):
            r_, i_ = XM[2 * s], XM[2 * s + 1]
            m2 = m2s[s]
            t.op("ACT", lambda h, r_=r_, m2=m2: h.activation(
                m2[:, 0:T], r_[:, 0:T], AF.Square), reads=[r_], writes=[m2])
            t.op("ACT", lambda h, i_=i_: h.activation(
                TMP[:, 0:T], i_[:, 0:T], AF.Square), reads=[i_], writes=[TMP])
            t.op("DVE", lambda h, m2=m2: h.tensor_add(
                m2[:, 0:T], m2[:, 0:T], TMP[:, 0:T]), reads=[m2, TMP], writes=[m2])
            t.op("DVE", lambda h, m2=m2: h.tensor_scalar_max(
                m2[:, 0:T], m2[:, 0:T], 1e-38), reads=[m2], writes=[m2])
        for s in range(3):
            t.op("ACT", lambda h, s=s: h.activation(
                m2s[s][:, 0:T], m2s[s][:, 0:T], AF.Sqrt), reads=[m2s[s]], writes=[m2s[s]])
        for s in range(3):
            t.op("ACT", lambda h, s=s: h.activation(
                m2s[s][:, 0:T], m2s[s][:, 0:T], AF.Ln, bias=self.eps_hrr[:, :]),
                reads=[m2s[s], self.eps_hrr], writes=[m2s[s]])
        for s in range(3):
            t.op("ACT", lambda h, s=s: h.activation(
                m2s[s][:, 0:T], m2s[s][:, 0:T], AF.Exp, scale=-1.0),
                reads=[m2s[s]], writes=[m2s[s]])
        for s in range(3):
            for c in range(2):
                b = XM[2 * s + c]
                t.op("DVE", lambda h, b=b, s=s: h.tensor_mul(
                    b[:, 0:T], b[:, 0:T], m2s[s][:, 0:T]),
                    reads=[b, m2s[s]], writes=[b])
        if "unorm" in taps:
            self.tap("unorm", [XM[i] for i in range(6)], cols=T)
        t.op("DVE", lambda h: h.tensor_mul(TMP[:, 0:T], XM[2][:, 0:T], XM[4][:, 0:T]),
             reads=[XM[2], XM[4]], writes=[TMP])
        t.op("DVE", lambda h: h.tensor_mul(self.WK[0][:, 0:T], XM[3][:, 0:T], XM[5][:, 0:T]),
             reads=[XM[3], XM[5]], writes=[self.WK[0]])
        t.op("DVE", lambda h: h.tensor_mul(self.WK[1][:, 0:T], XM[2][:, 0:T], XM[5][:, 0:T]),
             reads=[XM[2], XM[5]], writes=[self.WK[1]])
        t.op("DVE", lambda h: h.tensor_sub(XM[6][:, 0:T], TMP[:, 0:T], self.WK[0][:, 0:T]),
             reads=[TMP, self.WK[0]], writes=[XM[6]])
        t.op("DVE", lambda h: h.tensor_mul(TMP[:, 0:T], XM[3][:, 0:T], XM[4][:, 0:T]),
             reads=[XM[3], XM[4]], writes=[TMP])
        t.op("DVE", lambda h: h.tensor_add(XM[7][:, 0:T], self.WK[1][:, 0:T], TMP[:, 0:T]),
             reads=[self.WK[1], TMP], writes=[XM[7]])
        if "bind" in taps:
            self.tap("bind", [XM[6], XM[7]], cols=T)
        for srcb, dstb in ((XM[6], XM[2]), (XM[7], XM[3]), (XM[0], XM[4]), (XM[1], XM[5])):
            self.transpose_full(srcb, dstb)
        CROW = self.RS[0]
        t.op("ACT", lambda h: h.memzero(CROW[0:1, 0:256]), writes=[CROW])
        psI = self.PB[0]
        psC = self.PB[1]
        for tt in range(32):
            fns = [
                lambda h, tt=tt: h.matmul(psI[:, 0:128], self.l1_bf[:, :],
                                          XM[2][:, tt * 128:(tt + 1) * 128],
                                          start=True, stop=False),
                lambda h, tt=tt: h.matmul(psI[:, 128:256], self.l1_bf[:, :],
                                          XM[3][:, tt * 128:(tt + 1) * 128],
                                          start=False, stop=False),
                lambda h: h.matmul(psI[:, 0:256], self.ones_row_f[0:1, :],
                                   CROW[0:1, 0:256], start=False, stop=True),
            ]
            t.group("PE", fns, reads=[self.l1_bf, XM[2], XM[3], self.ones_row_f, CROW],
                    writes=[psI])
            fns = [
                lambda h, tt=tt: h.matmul(psC[0:1, 0:128], self.ones_col_bf[:, :],
                                          XM[2][:, tt * 128:(tt + 1) * 128],
                                          start=True, stop=False),
                lambda h, tt=tt: h.matmul(psC[0:1, 128:256], self.ones_col_bf[:, :],
                                          XM[3][:, tt * 128:(tt + 1) * 128],
                                          start=False, stop=True),
            ]
            t.group("PE", fns, reads=[self.ones_col_bf, XM[2], XM[3]], writes=[psC])
            t.op("DVE", lambda h, tt=tt: h.tensor_copy(
                XM[2][:, tt * 128:(tt + 1) * 128], psI[:, 0:128]),
                reads=[psI], writes=[XM[2]])
            t.op("DVE", lambda h, tt=tt: h.tensor_copy(
                XM[3][:, tt * 128:(tt + 1) * 128], psI[:, 128:256]),
                reads=[psI], writes=[XM[3]])
            t.op("DVE", lambda h: h.tensor_add(CROW[0:1, 0:256], CROW[0:1, 0:256],
                                               psC[0:1, 0:256]),
                 reads=[psC, CROW], writes=[CROW])
        if "cums" in taps:
            self.tap("cums", [XM[2], XM[3]], cols=T)
        t.op("DVE", lambda h: h.tensor_mul(self.WK[0][:, 0:T], XM[2][:, 0:T], XM[4][:, 0:T]),
             reads=[XM[2], XM[4]], writes=[self.WK[0]])
        t.op("DVE", lambda h: h.tensor_mul(self.WK[1][:, 0:T], XM[3][:, 0:T], XM[5][:, 0:T]),
             reads=[XM[3], XM[5]], writes=[self.WK[1]])
        t.op("DVE", lambda h: h.tensor_mul(TMP[:, 0:T], XM[3][:, 0:T], XM[4][:, 0:T]),
             reads=[XM[3], XM[4]], writes=[TMP])
        t.op("DVE", lambda h: h.tensor_add(self.WK[0][:, 0:T], self.WK[0][:, 0:T],
                                           self.WK[1][:, 0:T]),
             reads=[self.WK[0], self.WK[1]], writes=[self.WK[0]])
        t.op("DVE", lambda h: h.tensor_mul(self.WK[1][:, 0:T], XM[2][:, 0:T], XM[5][:, 0:T]),
             reads=[XM[2], XM[5]], writes=[self.WK[1]])
        t.op("DVE", lambda h: h.tensor_sub(TMP[:, 0:T], TMP[:, 0:T], self.WK[1][:, 0:T]),
             reads=[TMP, self.WK[1]], writes=[TMP])
        self.transpose_full(self.WK[0], XM[0])
        self.transpose_full(TMP, XM[1])
        t.dma("ACT", lambda h, ft=ft: h.dma_start(self.d_wT[0, ft, :, :], XM[0][:, 0:T]),
              reads=[XM[0]], writes=[self.d_wT])
        t.dma("ACT", lambda h, ft=ft: h.dma_start(self.d_wT[1, ft, :, :], XM[1][:, 0:T]),
              reads=[XM[1]], writes=[self.d_wT])

    def transpose_full(self, srcb, dstb):
        t = self.t
        for g in range(4):
            fns = []
            for k in range(8):
                tt = g * 8 + k
                fns.append(lambda h, tt=tt, k=k: h.transpose(
                    self.PT[:, k * 128:(k + 1) * 128],
                    srcb[:, tt * 128:(tt + 1) * 128], self.ident_bf[:, :]))
            t.group("PE", fns, reads=[srcb, self.ident_bf], writes=[self.PT])
            t.op("DVE", lambda h, g=g: h.tensor_copy(
                dstb[:, g * 1024:(g + 1) * 1024], self.PT[:, 0:1024]),
                reads=[self.PT], writes=[dstb])

    def attn_a_ifft(self, taps=""):
        t, din = self.t, self.din
        XM = self.XM
        for fc in range(8):
            t.dma("POOL", lambda h, fc=fc: h.dma_start(
                self.G8[fc][:, 0:1024], din["ifft_b"][fc * 128:(fc + 1) * 128, :]),
                writes=[self.G8[fc]])
        for fc in (8, 9):
            for dt_ in range(NDT):
                t.dma("POOL", lambda h, fc=fc, dt_=dt_: h.dma_start(
                    self.G8[dt_][:, 1024 + (fc - 8) * 128:1024 + (fc - 7) * 128],
                    din["ifft_b"][fc * 128:(fc + 1) * 128, dt_ * 128:(dt_ + 1) * 128]),
                    writes=[self.G8[dt_]])
        for comp in range(2):
            for ft in range(5):
                fc = comp * 5 + ft
                dst = XM[fc] if fc < 8 else (self.WK[0] if fc == 8 else self.WK[1])
                t.dma("POOL", lambda h, comp=comp, ft=ft, dst=dst: h.dma_start(
                    dst[:, 0:T], self.d_wT[comp, ft, :, :]), reads=[self.d_wT], writes=[dst])
        for dt_ in range(NDT):
            t.dma("ACT", lambda h, dt_=dt_: h.dma_start(
                self.XS[dt_][:, 0:COLS], self.d_xmod[dt_, :, :]), reads=[self.d_xmod], writes=[self.XS[dt_]])
        wt = [XM[0], XM[1], XM[2], XM[3], XM[4], XM[5], XM[6], XM[7], self.WK[0], self.WK[1]]
        for dt_ in range(NDT):
            for tc in range(NW):
                ps = self.PB[(dt_ + tc) % 3]
                fns = []
                for fc in range(10):
                    if fc < 8:
                        lt = self.G8[fc][:, dt_ * 128:(dt_ + 1) * 128]
                    else:
                        lt = self.G8[dt_][:, 1024 + (fc - 8) * 128:1024 + (fc - 7) * 128]
                    fns.append(lambda h, fc=fc, lt=lt, tc=tc, ps=ps: h.matmul(
                        ps[:, 0:512], lt, wt[fc][:, tc * 512:(tc + 1) * 512],
                        start=(fc == 0), stop=(fc == 9)))
                t.group("PE", fns, reads=self.G8 + wt, writes=[ps])
                t.op("DVE", lambda h, dt_=dt_, tc=tc, ps=ps: h.tensor_add(
                    self.XS[dt_][:, 1 + tc * 512:1 + (tc + 1) * 512],
                    self.XS[dt_][:, 1 + tc * 512:1 + (tc + 1) * 512], ps[:, 0:512]),
                    reads=[ps, self.XS[dt_]], writes=[self.XS[dt_]])


    def attn_b(self):
        t, din = self.t, self.din
        taps = os.environ.get("MK_TAPS", "")
        XM = self.XM
        self.ln_stats(self.XM, shifted=False, sq_scratch=self.XS)
        self.ln_apply(self.XM, self.XS)
        if "bln" in taps:
            self.tap("bln", self.XS)
        self.spill(self.XM, self.d_xmod)
        for dt_ in range(NDT):
            t.dma("POOL", lambda h, dt_=dt_: h.dma_start(
                self.G8[dt_][:, 0:16], din["BwT"][dt_ * 128:(dt_ + 1) * 128, :]),
                writes=[self.G8[dt_]])
        t.op("DVE", lambda h: h.memset(self.C16B[0:16, 0:256], 0.0), writes=[self.C16B])
        t.dma("ACT", lambda h: h.dma_start(self.d_u[:, 0:256], self.C16B[0:16, 0:256]),
              reads=[self.C16B], writes=[self.d_u])
        for tc in range(NW):
            ps = self.PB[0]
            pairs = [(self.G8[dt_][:, 0:16], self.XS[dt_][:, 1 + tc * 512:1 + (tc + 1) * 512])
                     for dt_ in range(NDT)]
            self.mm(ps[0:16, 0:512], pairs, ps, reads=self.G8 + self.XS)
            t.op("DVE", lambda h, ps=ps: h.tensor_copy(self.C16B[0:16, 0:512],
                                                       ps[0:16, 0:512]),
                 reads=[ps], writes=[self.C16B])
            t.dma("ACT", lambda h, tc=tc: h.dma_start(
                self.d_u[:, 256 + tc * 512:256 + (tc + 1) * 512], self.C16B[0:16, 0:512]),
                reads=[self.C16B], writes=[self.d_u])
        if "bu" in taps:
            o = self.nc.dram_tensor("dbg_bu0", [S_ST, ULEN], BF16, kind="ExternalOutput")
            self.dbg_out["dbg_bu0"] = o
            t.op("DVE", lambda h: h.memset(self.C16B[0:16, 0:256], 0.0), writes=[self.C16B])
            t.dma("POOL", lambda h: h.dma_start(o[:, :], self.d_u[:, :]), reads=[self.d_u])
        self.spill(self.XS, self.d_ln3)
        snT = self.FS
        CT = self.WK[0]
        for dt_ in range(NDT):
            t.dma("SP", lambda h, dt_=dt_: h.dma_start(
                self.SC[1][:, 0:16], din["logA"][dt_ * 128:(dt_ + 1) * 128, :]),
                writes=[self.SC[1]])
            t.op("ACT", lambda h: h.activation(self.SC[1][:, 16:32], self.SC[1][:, 0:16],
                                               AF.Exp), reads=[self.SC[1]], writes=[self.SC[1]])
            t.op("DVE", lambda h: h.tensor_scalar_mul(self.SC[1][:, 16:32],
                                                      self.SC[1][:, 16:32], -1.0),
                 reads=[self.SC[1]], writes=[self.SC[1]])
            t.group("PE", [lambda h: h.transpose(
                self.PB[1][0:16, 0:128], self.SC[1][:, 16:32], self.ident_f[:, :])],
                reads=[self.SC[1], self.ident_f], writes=[self.PB[1]])
            t.op("ACT", lambda h, dt_=dt_: h.copy(
                snT[0:16, dt_ * 128:(dt_ + 1) * 128], self.PB[1][0:16, 0:128]),
                reads=[self.PB[1]], writes=[snT])
            t.dma("SP", lambda h, dt_=dt_: h.dma_start(
                self.SC[1][:, 32:48], din["Cw"][dt_ * 128:(dt_ + 1) * 128, :]),
                writes=[self.SC[1]])
            t.group("PE", [lambda h: h.transpose(
                self.PB[2][0:16, 0:128], self.SC[1][:, 32:48], self.ident_f[:, :])],
                reads=[self.SC[1], self.ident_f], writes=[self.PB[2]])
            t.op("ACT", lambda h, dt_=dt_: h.copy(
                CT[0:16, dt_ * 128:(dt_ + 1) * 128], self.PB[2][0:16, 0:128]),
                reads=[self.PB[2]], writes=[CT])
        AQ = lambda s: (XM[4 + s // 4], (s % 4) * 1024)
        for s in range(S_ST):
            for half in range(2):
                t.dma("SP", lambda h, s=s, half=half: h.dma_start(
                    self.RS[0][0:1, 0:512], snT[s:s + 1, half * 512:(half + 1) * 512]),
                    reads=[snT], writes=[self.RS[0]])
                t.dma("POOL", lambda h, s=s, half=half: h.dma_start(
                    self.RS[1][0:1, 0:512], CT[s:s + 1, half * 512:(half + 1) * 512]),
                    reads=[CT], writes=[self.RS[1]])
                self.mm(self.PB[3][:, 0:512],
                        [(self.jrev_row[0:1, :], self.RS[0][0:1, 0:512])],
                        self.PB[3], reads=[self.jrev_row, self.RS[0]])
                t.op("ACT", lambda h, half=half: h.activation(
                    CT[:, 1024 + half * 512:1024 + (half + 1) * 512], self.PB[3][:, 0:512],
                    AF.Exp), reads=[self.PB[3]], writes=[CT])
                self.bcast_row(self.RS[1], 512, self.PB[4])
                t.op("DVE", lambda h, half=half: h.tensor_mul(
                    self.WK[1][:, half * 512:(half + 1) * 512],
                    CT[:, 1024 + half * 512:1024 + (half + 1) * 512], self.PB[4][:, 0:512]),
                    reads=[CT, self.PB[4]], writes=[self.WK[1]])
                aqb, aqc = AQ(s)
                self.bcast_row(self.RS[0], 512, self.PB[2])
                t.op("ACT", lambda h, aqb=aqb, aqc=aqc, half=half: h.activation(
                    aqb[:, aqc + half * 512:aqc + (half + 1) * 512], self.PB[2][:, 0:512],
                    AF.Exp, scale=128.0), reads=[self.PB[2]], writes=[aqb])
            t.dma("ACT", lambda h, s=s: h.dma_start(self.d_k0[s, :, :],
                                                    self.WK[1][:, 0:1024]),
                  reads=[self.WK[1]], writes=[self.d_k0])
        if "bk0" in taps:
            self.tap("bk0", [self.WK[1]], cols=1024)
        def k0slot(s):
            if s < 12:
                return self.XS[4 + s // 3], (s % 3) * 1024
            return self.G8[s - 12], 0
        for s in range(S_ST):
            kb, kc = k0slot(s)
            t.dma("POOL" if s % 2 == 0 else "SP",
                  lambda h, s=s, kb=kb, kc=kc: h.dma_start(
                      kb[:, kc:kc + 1024], self.d_k0[s, :, :]),
                  reads=[self.d_k0], writes=[kb])
        for xb in (XM[0], XM[1], XM[2], XM[3]):
            t.op("DVE", lambda h, xb=xb: h.memset(xb[:, 0:T], 0.0), writes=[xb])
        OS = [self.G8[4], self.G8[5], self.G8[7]]
        YBUF = self.G8[6]
        YST = self.WK[1]
        for tg in range(NW):
            for s in range(S_ST):
                ub, uc = self.XS[s // 4], (s % 4) * 512
                t.dma("POOL" if s % 2 == 0 else "SP",
                      lambda h, s=s, tg=tg, ub=ub, uc=uc: h.dma_start(
                          ub[:, uc:uc + 512],
                          bass.AP(self.d_u, s * ULEN + 129 + tg * 512, [[1, 128], [1, 512]])),
                      reads=[self.d_u], writes=[ub])
            for crel in range(4):
                c = tg * 4 + crel
                for s in range(S_ST):
                    zb, zc = XM[s // 4], (s % 4) * 1024
                    kb, kc = k0slot(s)
                    ub, uc = self.XS[s // 4], (s % 4) * 512
                    psA = self.PB[2 * (s % 2)]
                    psB = self.PB[2 * (s % 2) + 1]
                    osl = OS[s % 3]
                    fns = []
                    for half, ps in ((0, psA), (1, psB)):
                        fns.append(lambda h, ps=ps, half=half, ub=ub, uc=uc, crel=crel,
                                   kb=kb, kc=kc: h.matmul(
                            ps[:, 0:512],
                            ub[:, uc + crel * 128:uc + crel * 128 + 128],
                            kb[:, kc + half * 512:kc + (half + 1) * 512],
                            start=True, stop=False))
                    for half, ps in ((0, psA), (1, psB)):
                        fns.append(lambda h, ps=ps, zb=zb, zc=zc, half=half: h.matmul(
                            ps[:, 0:512],
                            self.ident_bf[:, :],
                            zb[:, zc + half * 512:zc + (half + 1) * 512],
                            start=False, stop=True))
                    t.group("PE", fns, reads=[ub, kb, zb, self.ident_bf],
                            writes=[psA, psB])
                    t.op("ACT", lambda h, psA=psA, osl=osl: h.copy(
                        osl[:, 0:512], psA[:, 0:512]), reads=[psA], writes=[osl])
                    t.op("DVE", lambda h, psB=psB, osl=osl: h.tensor_copy(
                        osl[:, 512:1024], psB[:, 0:512]), reads=[psB], writes=[osl])
                    aqb, aqc = AQ(s)
                    t.op("DVE", lambda h, zb=zb, zc=zc, aqb=aqb, aqc=aqc, psA=psA:
                         h.tensor_mul(zb[:, zc:zc + 512], psA[:, 0:512],
                                      aqb[:, aqc:aqc + 512]),
                         reads=[psA, aqb], writes=[zb])
                    t.op("POOL", lambda h, zb=zb, zc=zc, aqb=aqb, aqc=aqc, osl=osl:
                         h.tensor_mul(zb[:, zc + 512:zc + 1024], osl[:, 512:1024],
                                      aqb[:, aqc + 512:aqc + 1024]),
                         reads=[osl, aqb], writes=[zb])
                    if s > 1:
                        sp = s - 2
                        osp = OS[sp % 3]
                        t.group("PE", [
                            lambda h, sp=sp, osp=osp: h.matmul(
                                self.PB[4][:, 0:512], self.ident_bf[:, :],
                                osp[:, 0:512], start=(sp == 0), stop=False),
                            lambda h, sp=sp, osp=osp: h.matmul(
                                self.PW[:, 0:512], self.ident_bf[:, :],
                                osp[:, 512:1024], start=(sp == 0), stop=False)],
                            reads=[osp, self.ident_bf], writes=[self.PB[4], self.PW])
                for sp in (S_ST - 2, S_ST - 1):
                    osp = OS[sp % 3]
                    t.group("PE", [
                        lambda h, sp=sp, osp=osp: h.matmul(
                            self.PB[4][:, 0:512], self.ident_bf[:, :],
                            osp[:, 0:512], start=False, stop=(sp == S_ST - 1)),
                        lambda h, sp=sp, osp=osp: h.matmul(
                            self.PW[:, 0:512], self.ident_bf[:, :],
                            osp[:, 512:1024], start=False, stop=(sp == S_ST - 1))],
                        reads=[osp, self.ident_bf], writes=[self.PB[4], self.PW])
                t.op("ACT", lambda h: h.copy(YBUF[:, 0:512], self.PB[4][:, 0:512]),
                     reads=[self.PB[4]], writes=[YBUF])
                t.op("ACT", lambda h: h.copy(YBUF[:, 512:1024], self.PW[:, 0:512]),
                     reads=[self.PW], writes=[YBUF])
                fns = []
                for dsl in range(NDT):
                    fns.append(lambda h, dsl=dsl: h.transpose(
                        self.PT[:, dsl * 128:(dsl + 1) * 128],
                        YBUF[:, dsl * 128:(dsl + 1) * 128],
                        self.ident_bf[:, :]))
                t.group("PE", fns, reads=[YBUF, self.ident_bf], writes=[self.PT])
                for dsl in range(NDT):
                    t.op("ACT", lambda h, dsl=dsl, crel=crel: h.copy(
                        YST[:, dsl * 512 + crel * 128:dsl * 512 + crel * 128 + 128],
                        self.PT[:, dsl * 128:(dsl + 1) * 128]),
                        reads=[self.PT], writes=[YST])
            for dsl in range(NDT):
                t.dma("ACT", lambda h, dsl=dsl, tg=tg: h.dma_start(
                    self.d_y[dsl, :, tg * 512:(tg + 1) * 512],
                    YST[:, dsl * 512:(dsl + 1) * 512]),
                    reads=[YST], writes=[self.d_y])
        for dt_ in range(NDT):
            t.dma("ACT", lambda h, dt_=dt_: h.dma_start(
                self.WK[1][:, 0:T], self.d_y[dt_, :, :]), reads=[self.d_y],
                writes=[self.WK[1]])
            t.dma("POOL", lambda h, dt_=dt_: h.dma_start(
                self.XS[dt_][:, 0:COLS], self.d_ln3[dt_, :, :]), reads=[self.d_ln3],
                writes=[self.XS[dt_]])
            t.op("DVE", lambda h, dt_=dt_: h.scalar_tensor_tensor(
                self.WK[1][:, 0:T], self.XS[dt_][:, 1:T + 1],
                self.Dvec_col[:, dt_:dt_ + 1], self.WK[1][:, 0:T],
                ALU.mult, ALU.add), reads=[self.XS[dt_], self.Dvec_col, self.WK[1]],
                writes=[self.WK[1]])
            t.dma("ACT", lambda h, dt_=dt_: h.dma_start(
                self.WK[0][:, 0:T], self.d_xmod[dt_, :, 1:T + 1]), reads=[self.d_xmod],
                writes=[self.WK[0]])
            t.op("DVE", lambda h, dt_=dt_: h.tensor_add(
                self.XM[dt_][:, 1:T + 1], self.WK[0][:, 0:T], self.WK[1][:, 0:T]),
                reads=[self.WK[0], self.WK[1]], writes=[self.XM[dt_]])
            t.op("DVE", lambda h, dt_=dt_: h.memset(self.XM[dt_][:, 0:1], 0.0),
                 writes=[self.XM[dt_]])
            t.op("DVE", lambda h, dt_=dt_: h.memset(self.XM[dt_][:, T + 1:COLS], 0.0),
                 writes=[self.XM[dt_]])


    def write_out(self):
        t = self.t
        for dt_ in range(NDT):
            for cc in range(4):
                t.op("DVE", lambda h, dt_=dt_, cc=cc: h.tensor_copy(
                    self.FS[:, 0:1024], self.XM[dt_][:, 1 + cc * 1024:1 + (cc + 1) * 1024]),
                    reads=[self.XM[dt_]], writes=[self.FS])
                t.dma("POOL", lambda h, dt_=dt_, cc=cc: h.dma_start(
                    self.xout[dt_ * 128:(dt_ + 1) * 128, cc * 1024:(cc + 1) * 1024],
                    self.FS[:, 0:1024]), reads=[self.FS])


def kernel(**inputs):
    consts = build_constants()
    g = host_layout(inputs)
    x = np.asarray(inputs["L0_attn_a_x"], np.float32)
    B = x.shape[0]
    mk = MK(consts, stages=os.environ.get("MK_STAGES", "afbf"))
    nc = mk.build()
    in_maps = []
    for b in range(B):
        m = {"xin": np.ascontiguousarray(x[b].T)}
        for nm, arr in g.items():
            m[nm] = arr
        for nm, arr in consts.items():
            m["c_" + nm] = arr
        in_maps.append(m)
    res = run_bass_kernel_spmd(nc, in_maps, core_ids=list(range(B)))
    global _LAST_DBG
    _LAST_DBG = [{k: np.asarray(v) for k, v in r.items() if k.startswith("dbg_")}
                 for r in res.results]
    out = np.stack([np.asarray(r["xout"], np.float32).T for r in res.results])
    return out


_LAST_DBG = None


# version 16 (final)
# speedup vs baseline: 1.0135x; 1.0052x over previous
import os
import math
import contextlib
import numpy as np
import ml_dtypes
import concourse.bass as bass
import concourse.mybir as mybir
from concourse.bass_utils import run_bass_kernel_spmd

F32 = mybir.dt.float32
BF16 = mybir.dt.bfloat16
AF = mybir.ActivationFunctionType
ALU = mybir.AluOpType
AX = mybir.AxisListType

T = 4096
D = 1024
COLS = 4104
NDT = 8
HEADS = 8
DH = 64
WIN = 512
NW = T // WIN
INNER = 512
HID = 2730
HIDP = 2816
NJ = HIDP // 128
NF = 640
NFR = 513
S_ST = 16
QCH = 128
NCH = T // QCH
UPAD = 256
ULEN = UPAD + T

ENGS = ("PE", "ACT", "DVE", "POOL", "SP")
SECT = {"PE": "tensor", "ACT": "scalar", "DVE": "vector", "POOL": "gpsimd", "SP": "sync"}



class Trk:

    def __init__(self, nc, stack):
        self.nc = nc
        self.idx = {e: 0 for e in ENGS}
        self.qcnt = {e: 0 for e in ENGS}
        self.buf = {e: [] for e in ENGS}
        self.pubs = {e: [] for e in ENGS}
        self.pubsem = {e: stack.enter_context(nc.semaphore(f"pub{e}")) for e in ENGS}
        self.QK = 8
        self.qsems = {e: [stack.enter_context(nc.semaphore(f"q{e}{i}")) for i in range(self.QK)]
                      for e in ENGS}
        self.clock = {e: {} for e in ENGS}
        self.hist = {e: [(-1, {})] for e in ENGS}
        self.qsnap = {e: [] for e in ENGS}
        self.wr = {}
        self.rd = {}
        self.space = {}
        self.names = {}
        self.attach = {}
        self.nwaits = 0

    def reg(self, buf, name, space):
        self.space[id(buf)] = space
        self.names[id(buf)] = name
        return buf

    def _snap(self, e, opi):
        h = self.hist[e]
        for k in range(len(h) - 1, -1, -1):
            if h[k][0] <= opi:
                return h[k][1]
        return h[0][1]

    def _learn(self, e, d):
        c = self.clock[e]
        ch = False
        for k, v in d.items():
            if isinstance(v, dict):
                cur = c.get(k)
                if cur is None:
                    c[k] = dict(v)
                    ch = True
                else:
                    for s, pv in v.items():
                        if cur.get(s, -1) < pv:
                            cur[s] = pv
                            ch = True
            elif c.get(k, -1) < v:
                c[k] = v
                ch = True
        if ch:
            self.hist[e].append((self.idx[e], self._copyclock(c)))

    def _copyclock(self, c):
        return {k: (dict(v) if isinstance(v, dict) else v) for k, v in c.items()}

    def _learn_slot(self, e, key, slot, pidx):
        c = self.clock[e]
        cur = c.setdefault(key, {})
        if cur.get(slot, -1) < pidx:
            cur[slot] = pidx
            self.hist[e].append((self.idx[e], self._copyclock(c)))

    def _ensure(self, cons, key, pidx):
        if key[0] == "Q":
            slotv = self.clock[cons].get(key)
            slot = (pidx - 1) % self.QK
            if slotv is not None and slotv.get(slot, -1) >= pidx:
                return
        elif self.clock[cons].get(key, -1) >= pidx:
            return
        if key[0] == "Q":
            qe = key[1:]
            slot = (pidx - 1) % self.QK
            lap = (pidx - 1) // self.QK + 1
            self.buf[cons].append(("wait", self.qsems[qe][slot], 16 * lap))
            self.nwaits += 1
            snap = self.qsnap[qe][pidx - 1]
            self._learn(cons, dict(snap))
            self._learn_slot(cons, key, slot, pidx)
        else:
            pubs = self.pubs[key]
            pick = None
            for (pi, cum) in pubs:
                if pi >= pidx:
                    pick = (pi, cum)
                    break
            if pick is None:
                assert pidx < self.idx[key], (key, pidx, self.idx[key])
                cum = (pubs[-1][1] + 1) if pubs else 1
                pubs.append((pidx, cum))
                self.attach[(key, pidx)] = (self.pubsem[key], 1)
                pick = (pidx, cum)
            self.buf[cons].append(("wait", self.pubsem[key], pick[1]))
            self.nwaits += 1
            snap = self._snap(key, pick[0])
            self._learn(cons, dict(snap, **{key: pick[0]}))

    def _pre(self, eng, reads, writes):
        for b in reads:
            w = self.wr.get(id(b))
            if w and w != (eng, self.idx[eng]):
                self._ensure(eng, w[0], w[1])
            if self.space.get(id(b)) == "P":
                for rk, ri in list(self.rd.get(id(b), {}).items()):
                    if rk != eng:
                        self._ensure(eng, rk, ri)
        for b in writes:
            w = self.wr.get(id(b))
            if w:
                self._ensure(eng, w[0], w[1])
            for rk, ri in list(self.rd.get(id(b), {}).items()):
                if not (rk == eng and ri == self.idx[eng]):
                    self._ensure(eng, rk, ri)

    def _post_reads(self, key, opi, reads):
        for b in reads:
            d = self.rd.setdefault(id(b), {})
            if d.get(key, -1) < opi:
                d[key] = opi

    def _post_writes(self, key, opi, writes):
        for b in writes:
            self.wr[id(b)] = (key, opi)
            self.rd[id(b)] = {}

    def op(self, eng, fn, reads=(), writes=()):
        self._pre(eng, reads, writes)
        opi = self.idx[eng]
        self.buf[eng].append(("op", fn, opi))
        self.idx[eng] = opi + 1
        self._post_reads(eng, opi, reads)
        self._post_writes(eng, opi, writes)
        return opi

    def group(self, eng, fns, reads=(), writes=()):
        self._pre(eng, reads, writes)
        base = self.idx[eng]
        n = len(fns)
        for k, fn in enumerate(fns):
            self.buf[eng].append(("gop", fn, base, k == n - 1))
        self.idx[eng] = base + 1
        self._post_reads(eng, base, reads)
        self._post_writes(eng, base, writes)
        return base

    def dma(self, eng, fn, reads=(), writes=()):
        self._pre(eng, reads, writes)
        self.qcnt[eng] += 1
        qi = self.qcnt[eng]
        slot = (qi - 1) % self.QK
        if qi > self.QK:
            prev = qi - self.QK
            lapp = (prev - 1) // self.QK + 1
            self.buf[eng].append(("wait", self.qsems[eng][slot], 16 * lapp))
            snapp = self.qsnap[eng][prev - 1]
            self._learn(eng, dict(snapp))
            self._learn_slot(eng, "Q" + eng, slot, prev)
        self.qsnap[eng].append(self._copyclock(self.clock[eng]))
        self.buf[eng].append(("dma", fn, qi, slot))
        key = "Q" + eng
        self._post_reads(key, qi, reads)
        self._post_writes(key, qi, writes)
        return qi

    def wait_all_of(self, cons, key):
        if key[0] == "Q":
            qe = key[1:]
            n = self.qcnt[qe]
            for pidx in range(max(1, n - self.QK + 1), n + 1):
                self._ensure(cons, key, pidx)
        else:
            if self.idx[key] > 0:
                self._ensure(cons, key, self.idx[key] - 1)

    def finish(self, eng="SP"):
        for e in ENGS:
            if self.qcnt[e] > 0:
                self.wait_all_of(eng, "Q" + e)
        for e in ENGS:
            if e != eng and self.idx[e] > 0:
                self._ensure(eng, e, self.idx[e] - 1)

    def flush(self, block):
        for e in ENGS:
            entries = self.buf[e]
            if not entries:
                continue
            trk = self

            def run(h, _e=e, _entries=entries):
                for ent in _entries:
                    kind = ent[0]
                    if kind == "wait":
                        h.wait_ge(ent[1], ent[2])
                    elif kind == "op":
                        ins = ent[1](h)
                        a = trk.attach.get((_e, ent[2]))
                        if a is not None:
                            ins.then_inc(a[0], a[1])
                    elif kind == "gop":
                        ins = ent[1](h)
                        if ent[3]:
                            a = trk.attach.get((_e, ent[2]))
                            if a is not None:
                                ins.then_inc(a[0], a[1])
                    elif kind == "dma":
                        ins = ent[1](h)
                        ins.then_inc(trk.qsems[_e][ent[3]], 16)

            getattr(block, SECT[e])(run)



def build_constants():
    c = {}
    ee = np.arange(D, dtype=np.float64)
    ff = np.arange(NFR, dtype=np.float64)
    ang = 2.0 * np.pi * np.outer(ee, ff) / D
    c["dft_cr"] = np.cos(ang).astype(np.float32)
    c["dft_ci"] = (-np.sin(ang)).astype(np.float32)
    cf = np.full(NFR, 2.0 / D)
    cf[0] = 1.0 / D
    cf[NFR - 1] = 1.0 / D
    br = (cf[:, None] * np.cos(ang.T)).astype(np.float32)
    bi = (-cf[:, None] * np.sin(ang.T)).astype(np.float32)
    B = np.zeros((2 * NF, D), np.float32)
    B[0:NFR] = br
    B[NF:NF + NFR] = bi
    c["ifft_b"] = B

    hi = np.arange(0, DH, 2, dtype=np.float64)
    inv_freq = 1.0 / 10000.0 ** (hi / DH)
    t = np.arange(T, dtype=np.float64)
    freqs = t[:, None] * inv_freq[None, :]
    freqs = np.concatenate([freqs, freqs], -1)
    base = (hi + 0.4 * DH) / (1.4 * DH)
    power = (t - T // 2) / float(WIN // 2)
    scale = base[None, :] ** power[:, None]
    scale = np.concatenate([scale, scale], -1)
    cosf, sinf = np.cos(freqs), np.sin(freqs)
    for nm, arr in (("cs_q", cosf * scale), ("ss_q", sinf * scale),
                    ("cs_k", cosf / scale), ("ss_k", sinf / scale)):
        a = arr.T.astype(np.float32)
        c[nm] = np.concatenate([a, a], 0)

    R = np.zeros((128, 128), np.float32)
    for h0 in (0, 64):
        for f in range(32):
            R[h0 + f + 32, h0 + f] = -1.0
            R[h0 + f, h0 + f + 32] = 1.0
    c["rotm"] = R

    c["ident"] = np.eye(128, dtype=np.float32)
    c["l1"] = np.triu(np.ones((128, 128), np.float32))
    c["ones_col"] = np.ones((128, 1), np.float32)
    c["ones_row"] = np.ones((1, 128), np.float32)
    c["negbig"] = np.concatenate([np.full((128, 512), -1.0e38, np.float32),
                                  np.zeros((128, 512), np.float32)], axis=1)
    c["ones128"] = np.ones((128, 128), np.float32)
    c["jrev_row"] = np.arange(127, -1, -1, dtype=np.float32).reshape(1, 128)
    c["eps_ln"] = np.full((128, 1), 1e-5, np.float32)
    c["eps_hrr"] = np.full((128, 1), 1e-8, np.float32)

    NEG = np.float32(-1.0e38)
    mm = np.zeros((128, NF), np.float32)
    for oi in range(128):
        mm[oi, :oi] = NEG
        mm[oi, oi + 513:] = NEG
    c["mask_mid"] = mm
    for it in range(4):
        m0 = mm.copy()
        cut = 512 - it * 128
        m0[:, :cut] = NEG
        c[f"mask_w0_{it}"] = m0
    for jt in range(8):
        mt = np.zeros((128, WIN), np.float32)
        for p in range(128):
            jl = jt * 128 + p
            lo = max(0, jl - 512)
            hi_ = min(WIN, jl + 1)
            mt[p, :lo] = NEG
            mt[p, hi_:] = NEG
        c[f"maskT_{jt}"] = mt

    for i in range(4):
        sq = np.zeros((128, 16), np.float32)
        sq[0:64, 2 * i] = 1.0
        sq[64:128, 2 * i + 1] = 1.0
        c[f"sel_q{i}"] = sq
        sk = np.zeros((128, 16), np.float32)
        sk[0:64, 8 + 2 * i] = 1.0
        sk[64:128, 8 + 2 * i + 1] = 1.0
        c[f"sel_k{i}"] = sk
        rq = np.zeros((16, 128), np.float32)
        rq[2 * i, 0:64] = 1.0
        rq[2 * i + 1, 64:128] = 1.0
        c[f"rsel_q{i}"] = rq
        rk = np.zeros((16, 128), np.float32)
        rk[8 + 2 * i, 0:64] = 1.0
        rk[8 + 2 * i + 1, 64:128] = 1.0
        c[f"rsel_k{i}"] = rk
    return c


def host_layout(inputs):
    g = {}
    f32 = lambda a: np.ascontiguousarray(np.asarray(a, np.float32))
    wqkv = f32(inputs["L0_ffn_wqkv"])
    wo = f32(inputs["L0_ffn_wo"])
    w1 = f32(inputs["L0_ffn_w1"])
    w2 = f32(inputs["L0_ffn_w2"])
    b1 = f32(inputs["L0_ffn_b1"])
    b2 = f32(inputs["L0_ffn_b2"])
    lng = f32(inputs["L0_ffn_ln_g"])
    lnb = f32(inputs["L0_ffn_ln_b"])
    qs = f32(inputs["L0_ffn_q_scale"])
    ks = f32(inputs["L0_ffn_k_scale"])
    w1p = np.zeros((2, D, 2 * HIDP), np.float32)
    w1p[:, :, 0:HID] = w1[:, :, 0:HID]
    w1p[:, :, HIDP:HIDP + HID] = w1[:, :, HID:2 * HID]
    b1p = np.zeros((2, 2 * HIDP), np.float32)
    b1p[:, 0:HID] = b1[:, 0:HID]
    b1p[:, HIDP:HIDP + HID] = b1[:, HID:2 * HID]
    w2p = np.zeros((2, HIDP, D), np.float32)
    w2p[:, 0:HID, :] = w2
    g["wqkv"] = wqkv
    g["wo"] = wo
    g["w1dev"] = np.ascontiguousarray(
        w1p.reshape(2, NDT, 128, 2, NJ, 128).transpose(0, 4, 3, 2, 1, 5)
    ).reshape(2, NJ, 2, 128, 1024)
    g["w2p"] = w2p
    g["b1a_col"] = np.ascontiguousarray(b1p[:, :HIDP].reshape(2, NJ, 128).transpose(0, 2, 1))
    g["b1g_col"] = np.ascontiguousarray(b1p[:, HIDP:].reshape(2, NJ, 128).transpose(0, 2, 1))
    g["lng_col"] = np.ascontiguousarray(lng.reshape(2, NDT, 128).transpose(0, 2, 1))
    g["lnb_col"] = np.ascontiguousarray(lnb.reshape(2, NDT, 128).transpose(0, 2, 1))
    g["b2_col"] = np.ascontiguousarray(b2.reshape(2, NDT, 128).transpose(0, 2, 1))

    def gcol(a):
        return f32(np.concatenate([a, a])[:, None])

    qsr = np.concatenate([qs[:, 32:64], qs[:, 0:32]], axis=1)
    ksr = np.concatenate([ks[:, 32:64], ks[:, 0:32]], axis=1)
    g["qs_col"] = np.stack([gcol(qs[l]) for l in range(2)])
    g["qsr_col"] = np.stack([gcol(qsr[l]) for l in range(2)])
    g["ks_col"] = np.stack([gcol(ks[l]) for l in range(2)])
    g["ksr_col"] = np.stack([gcol(ksr[l]) for l in range(2)])
    g["Wq"] = f32(inputs["L0_attn_a_Wq"])
    g["Wk"] = f32(inputs["L0_attn_a_Wk"])
    g["Wv"] = f32(inputs["L0_attn_a_Wv"])
    g["bq"] = f32(inputs["L0_attn_a_bq"]).reshape(1, D)
    g["bk"] = f32(inputs["L0_attn_a_bk"]).reshape(1, D)
    g["bv"] = f32(inputs["L0_attn_a_bv"]).reshape(1, D)
    g["logA"] = f32(inputs["L0_attn_b_log_A_diag"])
    g["BwT"] = np.ascontiguousarray(f32(inputs["L0_attn_b_B_w"]).T)
    g["Cw"] = f32(inputs["L0_attn_b_C_w"])
    g["Dvec_col"] = np.ascontiguousarray(f32(inputs["L0_attn_b_D_vec"]).reshape(NDT, 128).T)
    return g



class MK:
    def __init__(self, consts, stages="afbf", dbg_taps=()):
        self.stages = [s for s in stages if s in "afb"]
        self.dbg_taps = list(dbg_taps)
        self.consts = consts
        self.stack = contextlib.ExitStack()
        nc = bass.Bass("TRN2", target_bir_lowering=False, debug=False)
        self.nc = nc
        self.t = Trk(nc, self.stack)
        self.din = {}
        self.dbg_out = {}
        self._declare_io()
        self._alloc()

    def _declare_io(self):
        nc = self.nc
        A = lambda name, shape: nc.dram_tensor(name, list(shape), F32, kind="ExternalInput")
        self.xin = A("xin", (D, T))
        for nm, shape in [
            ("wqkv", (2, D, 1536)), ("wo", (2, INNER, D)), ("w1dev", (2, NJ, 2, 128, 1024)),
            ("w2p", (2, HIDP, D)), ("b1a_col", (2, 128, NJ)), ("b1g_col", (2, 128, NJ)),
            ("lng_col", (2, 128, NDT)), ("lnb_col", (2, 128, NDT)), ("b2_col", (2, 128, NDT)),
            ("qs_col", (2, 128, 1)), ("qsr_col", (2, 128, 1)),
            ("ks_col", (2, 128, 1)), ("ksr_col", (2, 128, 1)),
            ("Wq", (D, D)), ("Wk", (D, D)), ("Wv", (D, D)),
            ("bq", (1, D)), ("bk", (1, D)), ("bv", (1, D)),
            ("logA", (D, S_ST)), ("BwT", (D, S_ST)), ("Cw", (D, S_ST)),
            ("Dvec_col", (128, NDT)),
        ]:
            self.din[nm] = A(nm, shape)
        for nm, arr in self.consts.items():
            self.din[nm] = A("c_" + nm, arr.shape)
        self.xout = nc.dram_tensor("xout", [D, T], F32, kind="ExternalOutput")
        I = lambda name, shape, dt=BF16: nc.dram_tensor(name, list(shape), dt, kind="Internal")
        self.d_xmod = self.t.reg(I("d_xmod", (NDT, 128, COLS)), "d_xmod", "D")
        self.d_ln3 = self.t.reg(I("d_ln3", (NDT, 128, COLS)), "d_ln3", "D")
        self.d_wf = self.t.reg(I("d_wf", (6, NDT, 128, NF)), "d_wf", "D")
        self.d_wT = self.t.reg(I("d_wT", (2, 5, 128, T)), "d_wT", "D")
        self.d_u = self.t.reg(I("d_u", (S_ST, ULEN)), "d_u", "D")
        self.d_k0 = self.t.reg(I("d_k0", (S_ST, 128, D)), "d_k0", "D")
        self.d_y = self.t.reg(I("d_y", (NDT, 128, T)), "d_y", "D")
        self.d_w1b = self.t.reg(I("d_w1b", (2, NJ, 2, 128, 1024)), "d_w1b", "D")
        self.d_w2b = self.t.reg(I("d_w2b", (2, NJ, 2, 128, 512)), "d_w2b", "D")
        self.d_tabsb = self.t.reg(I("d_tabsb", (4, 128, T)), "d_tabsb", "D")

    def _alloc(self):
        nc, t = self.nc, self.t
        SB = lambda nm, shape, dt: t.reg(nc.alloc_sbuf_tensor(nm, list(shape), dt), nm, "S")
        PS = lambda nm, shape, dt: t.reg(nc.alloc_psum_tensor(nm, list(shape), dt), nm, "P")
        self.XM = [SB(f"XM{i}", (128, COLS), BF16) for i in range(NDT)]
        self.XS = [SB(f"XS{i}", (128, COLS), BF16) for i in range(NDT)]
        self.G8 = [SB(f"G8_{i}", (128, 1536), BF16) for i in range(NDT)]
        self.WO8 = [SB(f"WO8_{i}", (64, D), BF16) for i in range(HEADS)]
        self.VT7 = SB("VT7", (128, 520), BF16)
        self.FS = SB("FS", (128, 1024), F32)
        self.WK = [SB(f"WK{i}", (128, T), BF16) for i in range(2)]
        self.SC = [SB(f"SC{i}", (128, 64), F32) for i in range(4)]
        self.RS = [SB(f"RS{i}", (1, 512), F32) for i in range(2)]
        self.NMR = SB("NMR", (1, 512), BF16)
        self.BIASA = SB("BIASA", (128, NJ), F32)
        self.BIASG = SB("BIASG", (128, NJ), F32)
        self.C16B = SB("C16B", (16, 512), BF16)
        self.ident_bf = SB("ident_bf", (128, 128), BF16)
        self.ident_f = SB("ident_f", (128, 128), F32)
        self.l1_bf = SB("l1_bf", (128, 128), BF16)
        self.rotm_bf = SB("rotm_bf", (128, 128), BF16)
        self.ones_col_bf = SB("ones_col_bf", (128, 1), BF16)
        self.ones_row_f = SB("ones_row_f", (1, 128), F32)
        self.ones128_f = SB("ones128_f", (128, 128), F32)
        self.jrev_row = SB("jrev_row", (1, 128), F32)
        self.eps_ln = SB("eps_ln", (128, 1), F32)
        self.eps_hrr = SB("eps_hrr", (128, 1), F32)
        self.masks = {}
        self.masks["mask_mid"] = SB("mask_mid", (128, NF), BF16)
        self.negbig_bf = SB("negbig_bf", (128, 1024), BF16)
        self.BCOLW = SB("BCOLW", (128, 4), BF16)
        self.BCOLM = SB("BCOLM", (128, 4), BF16)
        self.sel = {}
        for i in range(4):
            for p in ("sel_q", "sel_k"):
                self.sel[f"{p}{i}"] = SB(f"{p}{i}", (128, 16), BF16)
            for p in ("rsel_q", "rsel_k"):
                self.sel[f"{p}{i}"] = SB(f"{p}{i}", (16, 128), BF16)
        self.qs_col = SB("sb_qs_col", (128, 1), F32)
        self.qsr_col = SB("sb_qsr_col", (128, 1), F32)
        self.ks_col = SB("sb_ks_col", (128, 1), F32)
        self.ksr_col = SB("sb_ksr_col", (128, 1), F32)
        self.lng_col = SB("sb_lng_col", (128, NDT), F32)
        self.lnb_col = SB("sb_lnb_col", (128, NDT), F32)
        self.lgb_bf = SB("lgb_bf", (128, 16), BF16)
        self.b2_col = SB("sb_b2_col", (128, NDT), F32)
        self.b1a_col = SB("sb_b1a_col", (128, NJ), F32)
        self.b1g_col = SB("sb_b1g_col", (128, NJ), F32)
        self.Dvec_col = SB("sb_Dvec_col", (128, NDT), F32)
        self.bhat_col = SB("bhat_col", (128, 32), F32)
        self.bcol_bf = SB("bcol_bf", (128, 8), BF16)
        self.PB = [PS(f"PB{i}", (128, 512), F32) for i in range(5)]
        self.PW = PS("PW", (128, NF), F32)
        self.PT = PS("PT", (128, 1024), BF16)

    def es_slot(self, it):
        return [(self.XM[4], 0), (self.XM[6], 2048), (self.XM[4], 1024),
                (self.WK[0], 0)][it]

    def qbuf(self, h):
        return self.XM[0], h * 512
    def kbuf(self, h):
        return (self.XM[1] if h % 2 == 0 else self.XM[2]), (h // 2) * 1024
    def vtbuf(self, h):
        if h < 7:
            return self.XM[3], h * 520
        return self.VT7, 0
    def attnT_buf(self):
        return self.XM[4]
    def qraw_buf(self):
        return self.XM[4]
    def ofin_buf(self, h):
        return self.XM[5], h * 512
    def q2buf(self):
        return self.XM[7]
    def tabq(self, k):
        return self.XM[6], k * 512
    def maskT(self, jt):
        if jt < 4:
            return self.masks[f"maskT_{jt}"], 0
        return self.XM[6], 2048 + (jt - 4) * 512
    def mst(self, j):
        return self.XM[j // 8], (j % 8) * 512
    def prbuf(self):
        return self.XM[6]
    def w2buf(self, r):
        return self.XM[7], r * 1024
    def lgra(self):
        return self.XM[3]
    def lgrg(self):
        return self.XM[4]

    def cast_load(self, dram_ap, sb_f32_tmp, out_bf, shape_cols, eng="SP"):
        t = self.t
        t.dma(eng, lambda h: h.dma_start(sb_f32_tmp[:, 0:shape_cols], dram_ap), writes=[sb_f32_tmp])
        t.op("DVE", lambda h: h.tensor_copy(out_bf[:, 0:shape_cols], sb_f32_tmp[:, 0:shape_cols]),
             reads=[sb_f32_tmp], writes=[out_bf])

    def mm(self, psum_ap, pairs, psum_buf, reads, start=True, stop=True):
        t = self.t
        n = len(pairs)
        fns = []
        for k, (lt, rh) in enumerate(pairs):
            fns.append(lambda h, lt=lt, rh=rh, k=k: h.matmul(
                psum_ap, lt, rh, start=(start and k == 0), stop=(stop and k == n - 1)))
        t.group("PE", fns, reads=list(reads), writes=[psum_buf])

    def tap(self, name, bufs, cols=COLS, dt=None):
        nc, t = self.nc, self.t
        n = len(bufs)
        dtt = dt or BF16
        k = 0
        while f"dbg_{name}{k}" in self.dbg_out:
            k += 1
        nm = f"dbg_{name}{k}"
        o = nc.dram_tensor(nm, [n, 128, cols], dtt, kind="ExternalOutput")
        self.dbg_out[nm] = o
        for i, b in enumerate(bufs):
            t.dma("POOL", lambda h, i=i, b=b: h.dma_start(o[i, :, 0:cols], b[:, 0:cols]),
                  reads=[b])

    def build(self):
        t = self.t
        self.ffn_converted = False
        self.load_consts()
        if "f" in self.stages:
            for k, nm in enumerate(("cs_q", "ss_q", "cs_k", "ss_k")):
                t.dma("POOL", lambda h, k=k, nm=nm: h.dma_start(
                    self.d_tabsb[k], self.din[nm][:, :]), writes=[self.d_tabsb])
        self.load_x()
        for st in self.stages:
            if st == "a":
                self.attn_a()
            elif st == "f":
                self.ffn()
            elif st == "b":
                self.attn_b()
        self.write_out()
        t.finish("SP")
        block = self.stack.enter_context(self.nc.Block())
        t.flush(block)
        self.stack.close()
        return self.nc

    def load_consts(self):
        t, din = self.t, self.din

        def ldb(name, dst, rows, cols):
            t.dma("POOL", lambda h: h.dma_start(dst[0:rows, 0:cols], din[name][0:rows, 0:cols]),
                  writes=[dst])

        def ldf(name, dst, rows, cols):
            t.dma("SP", lambda h: h.dma_start(dst[0:rows, 0:cols], din[name][0:rows, 0:cols]),
                  writes=[dst])

        ldb("ident", self.ident_bf, 128, 128)
        ldf("ident", self.ident_f, 128, 128)
        ldb("l1", self.l1_bf, 128, 128)
        ldb("rotm", self.rotm_bf, 128, 128)
        ldb("ones_col", self.ones_col_bf, 128, 1)
        ldf("ones_row", self.ones_row_f, 1, 128)
        ldf("ones128", self.ones128_f, 128, 128)
        ldf("jrev_row", self.jrev_row, 1, 128)
        ldf("eps_ln", self.eps_ln, 128, 1)
        ldf("eps_hrr", self.eps_hrr, 128, 1)
        ldb("mask_mid", self.masks["mask_mid"], 128, NF)
        ldb("negbig", self.negbig_bf, 128, 1024)
        for nm, dst in self.sel.items():
            if nm.startswith("rsel"):
                ldb(nm, dst, 16, 128)
            else:
                ldb(nm, dst, 128, 16)
        t.dma("SP", lambda h: h.dma_start(self.Dvec_col[:, :], din["Dvec_col"][:, :]),
              writes=[self.Dvec_col])

    def load_layer_smalls(self, l):
        t, din = self.t, self.din
        for nm, dst in [("qs_col", self.qs_col), ("qsr_col", self.qsr_col),
                        ("ks_col", self.ks_col), ("ksr_col", self.ksr_col),
                        ("lng_col", self.lng_col), ("lnb_col", self.lnb_col),
                        ("b2_col", self.b2_col), ("b1a_col", self.b1a_col),
                        ("b1g_col", self.b1g_col)]:
            t.dma("SP", lambda h, nm=nm, dst=dst: h.dma_start(
                dst[:, :], din[nm][l]), writes=[dst])
        for dst in (self.qs_col, self.qsr_col):
            t.op("DVE", lambda h, dst=dst: h.tensor_scalar_mul(dst[:, :], dst[:, :], 8.0),
                 reads=[dst], writes=[dst])

    def load_x(self):
        t = self.t
        for dt_ in range(NDT):
            t.dma("POOL", lambda h, dt_=dt_: h.dma_start(
                self.XM[dt_][:, 1:T + 1], self.xin[dt_ * 128:(dt_ + 1) * 128, 0:T]),
                writes=[self.XM[dt_]])
            t.op("DVE", lambda h, dt_=dt_: h.memset(self.XM[dt_][:, 0:1], 0.0),
                 reads=[], writes=[self.XM[dt_]])
            t.op("DVE", lambda h, dt_=dt_: h.memset(self.XM[dt_][:, T + 1:COLS], 0.0),
                 reads=[], writes=[self.XM[dt_]])

    def row_to_col(self, row_f32, col_out, psum, ncols=32):
        t = self.t
        fns = []
        for ch in range(ncols):
            fns.append(lambda h, ch=ch: h.transpose(
                psum[:, ch:ch + 1], row_f32[0:1, ch * 128:(ch + 1) * 128], self.ident_f[:, :]))
        t.group("PE", fns, reads=[row_f32, self.ident_f], writes=[psum])
        t.op("DVE", lambda h: h.tensor_copy(col_out[:, 0:ncols], psum[:, 0:ncols]),
             reads=[psum], writes=[col_out])

    def col_to_row(self, col_f32, row_out, psum, ncols=32):
        t = self.t
        for blk in range(0, ncols, 4):
            nb = min(4, ncols - blk)
            fns = []
            for ch in range(blk, blk + nb):
                fns.append(lambda h, ch=ch: h.transpose(
                    psum[0:1, (ch % 4) * 128:(ch % 4) * 128 + 128],
                    col_f32[:, ch:ch + 1], self.ident_f[:, :]))
            t.group("PE", fns, reads=[col_f32, self.ident_f], writes=[psum])
            t.op("ACT", lambda h, blk=blk, nb=nb: h.copy(
                row_out[0:1, blk * 128:(blk + nb) * 128], psum[0:1, 0:nb * 128]),
                reads=[psum], writes=[row_out])

    def ln_stats(self, XG, shifted=False, sq_scratch=None):
        t = self.t
        PBa = self.PB[0]
        sumc, sqc, muc, rstdc = self.SC[0], self.SC[1], self.SC[2], self.SC[3]
        sq_tiles = sq_scratch

        def src_ap(dt_, c0, n):
            off = -1 if (shifted and dt_ >= 4) else 0
            return XG[dt_][:, c0 + off:c0 + off + n]

        for dt_ in range(NDT):
            t.op("ACT", lambda h, dt_=dt_: h.activation(
                sq_tiles[dt_][:, 0:T], src_ap(dt_, 1, T), AF.Square),
                reads=[XG[dt_]], writes=[sq_tiles[dt_]])
        for tc in range(NW):
            c0 = 1 + tc * 512
            self.mm(PBa[0:1, 0:512],
                    [(self.ones_col_bf[:, :], src_ap(dt_, c0, 512)) for dt_ in range(NDT)],
                    PBa, reads=list(XG) + [self.ones_col_bf])
            t.op("ACT", lambda h: h.copy(self.RS[0][0:1, 0:512], PBa[0:1, 0:512]),
                 reads=[PBa], writes=[self.RS[0]])
            self.row_to_col4(self.RS[0], sumc, tc * 4, self.PB[1])
            self.mm(PBa[0:1, 0:512],
                    [(self.ones_col_bf[:, :], sq_tiles[dt_][:, tc * 512:tc * 512 + 512])
                     for dt_ in range(NDT)],
                    PBa, reads=list(sq_tiles) + [self.ones_col_bf])
            t.op("ACT", lambda h: h.copy(self.RS[0][0:1, 0:512], PBa[0:1, 0:512]),
                 reads=[PBa], writes=[self.RS[0]])
            self.row_to_col4(self.RS[0], sqc, tc * 4, self.PB[1])
        t.op("DVE", lambda h: h.tensor_scalar_mul(muc[:, 0:32], sumc[:, 0:32], 1.0 / D),
             reads=[sumc], writes=[muc])
        t.op("DVE", lambda h: h.tensor_scalar_mul(sqc[:, 0:32], sqc[:, 0:32], 1.0 / D),
             reads=[sqc], writes=[sqc])
        t.op("DVE", lambda h: h.tensor_mul(sumc[:, 0:32], muc[:, 0:32], muc[:, 0:32]),
             reads=[muc], writes=[sumc])
        t.op("DVE", lambda h: h.tensor_sub(sqc[:, 0:32], sqc[:, 0:32], sumc[:, 0:32]),
             reads=[sqc, sumc], writes=[sqc])
        t.op("ACT", lambda h: h.activation(sqc[:, 0:32], sqc[:, 0:32], AF.Ln,
                                           bias=self.eps_ln[:, :]),
             reads=[sqc, self.eps_ln], writes=[sqc])
        t.op("ACT", lambda h: h.activation(rstdc[:, 0:32], sqc[:, 0:32], AF.Exp, scale=-0.5),
             reads=[sqc], writes=[rstdc])
        t.op("DVE", lambda h: h.tensor_mul(muc[:, 0:32], muc[:, 0:32], rstdc[:, 0:32]),
             reads=[muc, rstdc], writes=[muc])

    def row_to_col4(self, row_f32, col_out, cidx, psum):
        t = self.t
        fns = []
        for k in range(4):
            fns.append(lambda h, k=k: h.transpose(
                psum[:, k:k + 1], row_f32[0:1, k * 128:(k + 1) * 128], self.ident_f[0:1, 0:1]))
        t.group("PE", fns, reads=[row_f32, self.ident_f], writes=[psum])
        t.op("DVE", lambda h: h.tensor_copy(col_out[:, cidx:cidx + 4], psum[:, 0:4]),
             reads=[psum], writes=[col_out])

    def ln_apply(self, XG, OUT):
        t = self.t
        arep, brep = self.WK[0], self.WK[1]
        for tc in range(NW):
            c0 = tc * 512
            if tc % 2 == 0:
                ra, roa, rb, rob = self.RS[0], 0, self.RS[1], 0
                pca, pcb, pba, pbb = self.PB[2], self.PB[3], self.PB[0], self.PB[1]
            else:
                ra, roa, rb, rob = self.FS, 0, self.FS, 512
                pca, pcb, pba, pbb = self.PB[4], self.PW, self.PB[4], self.PW
            self.cols_to_row2(self.SC[3], tc * 4, 4, ra, roa, pca)
            self.mm(pba[0:128, 0:512],
                    [(self.ones_row_f[0:1, 0:128], ra[0:1, roa:roa + 512])],
                    pba, reads=[self.ones_row_f, ra])
            t.op("DVE", lambda h, c0=c0, pba=pba: h.tensor_copy(
                arep[:, c0:c0 + 512], pba[:, 0:512]),
                reads=[pba], writes=[arep])
            self.cols_to_row2(self.SC[2], tc * 4, 4, rb, rob, pcb)
            t.op("DVE", lambda h, rb=rb, rob=rob: h.tensor_scalar_mul(
                rb[0:1, rob:rob + 512], rb[0:1, rob:rob + 512], -1.0),
                reads=[rb], writes=[rb])
            self.mm(pbb[0:128, 0:512],
                    [(self.ones_row_f[0:1, 0:128], rb[0:1, rob:rob + 512])],
                    pbb, reads=[self.ones_row_f, rb])
            t.op("DVE", lambda h, c0=c0, pbb=pbb: h.tensor_copy(
                brep[:, c0:c0 + 512], pbb[:, 0:512]),
                reads=[pbb], writes=[brep])
        for dt_ in range(NDT):
            t.op("DVE", lambda h, dt_=dt_: h.tensor_mul(
                OUT[dt_][:, 1:T + 1], XG[dt_][:, 1:T + 1], arep[:, 0:T]),
                reads=[XG[dt_], arep], writes=[OUT[dt_]])
            t.op("DVE", lambda h, dt_=dt_: h.tensor_add(
                OUT[dt_][:, 1:T + 1], OUT[dt_][:, 1:T + 1], brep[:, 0:T]),
                reads=[OUT[dt_], brep], writes=[OUT[dt_]])
            t.op("DVE", lambda h, dt_=dt_: h.memset(OUT[dt_][:, 0:1], 0.0),
                 reads=[], writes=[OUT[dt_]])
            t.op("DVE", lambda h, dt_=dt_: h.memset(OUT[dt_][:, T + 1:COLS], 0.0),
                 reads=[], writes=[OUT[dt_]])

    def spill(self, XG, dram):
        t = self.t
        for dt_ in range(NDT):
            t.dma("ACT", lambda h, dt_=dt_: h.dma_start(
                dram[dt_, :, :], XG[dt_][:, :]), reads=[XG[dt_]], writes=[dram])

    def restore_add(self, XG, dram, OTHER):
        t = self.t
        f = self.WK[1]
        for dt_ in range(NDT):
            t.dma("ACT", lambda h, dt_=dt_: h.dma_start(
                f[:, 0:T], dram[dt_, :, 1:T + 1]), reads=[dram], writes=[f])
            t.op("DVE", lambda h, dt_=dt_: h.tensor_add(
                XG[dt_][:, 1:T + 1], OTHER[dt_][:, 1:T + 1], f[:, 0:T]),
                reads=[OTHER[dt_], f], writes=[XG[dt_]])
            t.op("DVE", lambda h, dt_=dt_: h.memset(XG[dt_][:, 0:1], 0.0),
                 reads=[], writes=[XG[dt_]])
            t.op("DVE", lambda h, dt_=dt_: h.memset(XG[dt_][:, T + 1:COLS], 0.0),
                 reads=[], writes=[XG[dt_]])

    def cols_to_row2(self, colbuf, cidx, n, rowt, rowoff, rowpsum):
        t = self.t
        fns = []
        for k in range(n):
            fns.append(lambda h, k=k: h.transpose(
                rowpsum[0:1, k * 128:(k + 1) * 128], colbuf[:, cidx + k:cidx + k + 1],
                self.ident_f[:, :]))
        t.group("PE", fns, reads=[colbuf, self.ident_f], writes=[rowpsum])
        t.op("ACT", lambda h, n=n: h.copy(rowt[0:1, rowoff:rowoff + n * 128],
                                          rowpsum[0:1, 0:n * 128]),
             reads=[rowpsum], writes=[rowt])

    def cols_to_row(self, colbuf, cidx, n, row_out, rowpsum):
        t = self.t
        fns = []
        for k in range(n):
            fns.append(lambda h, k=k: h.transpose(
                rowpsum[0:1, k * 128:(k + 1) * 128], colbuf[:, cidx + k:cidx + k + 1],
                self.ident_f[:, :]))
        t.group("PE", fns, reads=[colbuf, self.ident_f], writes=[rowpsum])
        t.op("ACT", lambda h, n=n: h.copy(row_out[0:1, 0:n * 128], rowpsum[0:1, 0:n * 128]),
             reads=[rowpsum], writes=[row_out])

    def bcast_row(self, row_f32, ncols, psum_out, po=128):
        self.mm(psum_out[0:po, 0:ncols],
                [(self.ones_row_f[0:1, 0:po], row_f32[0:1, 0:ncols])],
                psum_out, reads=[self.ones_row_f, row_f32])

    def ffn(self, first=True):
        t, din = self.t, self.din
        taps = os.environ.get("MK_TAPS", "")
        if not self.ffn_converted:
            self.ffn_converted = True
            for l in range(2):
                for j in range(NJ):
                    for ag in range(2):
                        t.dma("POOL", lambda h, l=l, j=j, ag=ag: h.dma_start(
                            self.d_w1b[l, j, ag], self.din["w1dev"][l, j, ag]),
                            writes=[self.d_w1b])
                    for grp in range(2):
                        t.dma("POOL", lambda h, l=l, j=j, grp=grp: h.dma_start(
                            self.d_w2b[l, j, grp],
                            self.din["w2p"][l, j * 128:(j + 1) * 128,
                                            grp * 512:(grp + 1) * 512]),
                            writes=[self.d_w2b])
        self.ln_stats(self.XM, shifted=False, sq_scratch=self.XS)
        self.ln_apply(self.XM, self.XS)
        if "ln" in taps:
            self.tap("ln", self.XS)
        self.spill(self.XM, self.d_xmod)
        for l in range(2):
            self.load_layer_smalls(l)
            self.ffn_layer(l, taps if l == 0 else "")
        self.restore_add(self.XM, self.d_xmod, self.XS)

    def ffn_layer(self, l, taps=""):
        t, din = self.t, self.din
        for dt_ in range(NDT):
            t.dma("POOL", lambda h, dt_=dt_: h.dma_start(
                self.G8[dt_][:, 0:1536], din["wqkv"][l, dt_ * 128:(dt_ + 1) * 128, :]),
                writes=[self.G8[dt_]])
        for hh in range(HEADS):
            t.dma("POOL", lambda h, hh=hh: h.dma_start(
                self.WO8[hh][0:64, :], din["wo"][l, hh * 64:(hh + 1) * 64, :]),
                writes=[self.WO8[hh]])
        for xb in (self.XM[0], self.XM[1], self.XM[2], self.XM[3], self.VT7):
            t.op("DVE", lambda h, xb=xb: h.memset(xb[:, :], 0.0), writes=[xb])
        for hh in range(HEADS):
            vb, vc = self.vtbuf(hh)
            for half in range(2):
                for jt in range(4):
                    c = vc + half * 260 + jt * 65 + 64
                    t.op("DVE", lambda h, vb=vb, c=c: h.memset(vb[:, c:c + 1], 1.0),
                         writes=[vb])
        for w in range(NW):
            self.ffn_window(l, w, taps if w == 0 else "")
        if "attn" in taps:
            self.tap("attn", self.XS)
        self.ffn_mlp(l)
        if "mlp" in taps:
            self.tap("mlp", self.XS)

    def ffn_window(self, l, w, taps=""):
        t, din = self.t, self.din
        c0 = 1 + w * WIN
        jts = list(range(4, 8)) if w == 0 else list(range(8))
        for k in range(4):
            tb, tcc = self.tabq(k)
            t.dma("SP" if k % 2 == 0 else "ACT",
                  lambda h, k=k, tb=tb, tcc=tcc: h.dma_start(
                      tb[:, tcc:tcc + 512], self.d_tabsb[k, :, w * WIN:(w + 1) * WIN]),
                  reads=[self.d_tabsb], writes=[tb])
        if w > 0:
            for hh in range(HEADS):
                kb, kc = self.kbuf(hh)
                t.op("DVE", lambda h, kb=kb, kc=kc: h.tensor_copy(
                    kb[:, kc:kc + 512], kb[:, kc + 512:kc + 1024]), reads=[kb], writes=[kb])
                vb, vc = self.vtbuf(hh)
                t.op("DVE", lambda h, vb=vb, vc=vc: h.tensor_copy(
                    vb[:, vc:vc + 260], vb[:, vc + 260:vc + 520]), reads=[vb], writes=[vb])
        qraw = self.qraw_buf()
        q2 = self.q2buf()
        vraw = self.WK[1]
        for ct in range(12):
            ps = self.PB[ct % 2]
            if w == 0:
                pairs = []
                for dt_ in range(NDT):
                    cs = c0 if dt_ < 4 else c0 - 1
                    pairs.append((self.G8[dt_][:, ct * 128:(ct + 1) * 128],
                                  self.XS[dt_][:, cs:cs + 512]))
                self.mm(ps[:, 0:512], pairs, ps, reads=self.G8 + self.XS)
            else:
                fns = []
                for k, dt_ in enumerate((0, 1, 2)):
                    fns.append(lambda h, ct=ct, dt_=dt_, ps=ps, k=k: h.matmul(
                        ps[:, 0:512], self.G8[dt_][:, ct * 128:(ct + 1) * 128],
                        self.XS[dt_][:, c0:c0 + 512], start=(k == 0), stop=False))
                for dt_ in range(4, 8):
                    fns.append(lambda h, ct=ct, dt_=dt_, ps=ps: h.matmul(
                        ps[:, 0:1], self.G8[dt_][:, ct * 128:(ct + 1) * 128],
                        self.BCOLW[:, dt_ - 4:dt_ - 3], start=False, stop=False))
                    fns.append(lambda h, ct=ct, dt_=dt_, ps=ps: h.matmul(
                        ps[:, 1:512], self.G8[dt_][:, ct * 128:(ct + 1) * 128],
                        self.XS[dt_][:, c0:c0 + 511], start=False, stop=False))
                fns.append(lambda h, ct=ct, ps=ps: h.matmul(
                    ps[:, 0:512], self.G8[3][:, ct * 128:(ct + 1) * 128],
                    self.XS[3][:, c0:c0 + 512], start=False, stop=True))
                t.group("PE", fns, reads=self.G8 + self.XS + [self.BCOLW], writes=[ps])
            if ct < 8:
                t.op("ACT", lambda h, ct=ct, ps=ps: h.copy(
                    qraw[:, ct * 512:(ct + 1) * 512], ps[:, 0:512]),
                    reads=[ps], writes=[qraw])
                t.op("ACT", lambda h, ct=ct, ps=ps: h.activation(
                    q2[:, ct * 512:(ct + 1) * 512], ps[:, 0:512],
                    AF.Square), reads=[ps], writes=[q2])
            else:
                t.op("ACT", lambda h, ct=ct, ps=ps: h.copy(
                    vraw[:, 2048 + (ct - 8) * 512:2048 + (ct - 7) * 512], ps[:, 0:512]),
                    reads=[ps], writes=[vraw])
        pairs = []
        rds = []
        for i in range(4):
            pairs.append((self.sel[f"sel_q{i}"][:, :], q2[:, i * 512:(i + 1) * 512]))
            rds.append(self.sel[f"sel_q{i}"])
        for i in range(4):
            pairs.append((self.sel[f"sel_k{i}"][:, :], q2[:, (4 + i) * 512:(5 + i) * 512]))
            rds.append(self.sel[f"sel_k{i}"])
        self.mm(self.PB[2][0:16, 0:512], pairs, self.PB[2], reads=rds + [q2])
        C16 = self.FS
        t.op("DVE", lambda h: h.tensor_scalar_max(C16[0:16, 0:512],
                                                  self.PB[2][0:16, 0:512], 1e-24),
             reads=[self.PB[2]], writes=[C16])
        t.op("ACT", lambda h: h.activation(C16[0:16, 0:512], C16[0:16, 0:512],
                                           AF.Ln), reads=[C16], writes=[C16])
        t.op("ACT", lambda h: h.activation(self.C16B[0:16, 0:512], C16[0:16, 0:512],
                                           AF.Exp, scale=-0.5), reads=[C16], writes=[self.C16B])
        for ct in range(8):
            isq = ct < 4
            i4 = ct % 4
            rsel = self.sel[("rsel_q%d" if isq else "rsel_k%d") % i4]
            prr = self.PB[3] if ct % 2 == 0 else self.PB[0]
            prot = self.PB[4] if ct % 2 == 0 else self.PB[1]
            buf = self.WK[0] if ct % 2 == 0 else self.XM[7]
            self.mm(prr[:, 0:512], [(rsel[:, :], self.C16B[0:16, 0:512])],
                    prr, reads=[rsel, self.C16B])
            qn1 = buf
            t.op("DVE", lambda h, ct=ct, qn1=qn1, prr=prr: h.tensor_mul(
                qn1[:, 0:512], qraw[:, ct * 512:(ct + 1) * 512], prr[:, 0:512]),
                reads=[qraw, prr], writes=[qn1])
            self.mm(prot[:, 0:512], [(self.rotm_bf[:, :], qn1[:, 0:512])],
                    prot, reads=[self.rotm_bf, qn1])
            tb_c, tcc_c = self.tabq(0 if isq else 2)
            tb_s, tcc_s = self.tabq(1 if isq else 3)
            t1, t2 = buf, buf
            gc = self.qs_col if isq else self.ks_col
            gcr = self.qsr_col if isq else self.ksr_col
            t.op("DVE", lambda h, tb_c=tb_c, tcc_c=tcc_c, gc=gc, qn1=qn1, t1=t1:
                 h.scalar_tensor_tensor(
                     t1[:, 512:1024], qn1[:, 0:512], gc[:, :], tb_c[:, tcc_c:tcc_c + 512],
                     ALU.mult, ALU.mult),
                 reads=[qn1, gc, tb_c], writes=[t1])
            t.op("DVE", lambda h, tb_s=tb_s, tcc_s=tcc_s, gcr=gcr, t2=t2, prot=prot:
                 h.scalar_tensor_tensor(
                     t2[:, 1024:1536], prot[:, 0:512], gcr[:, :], tb_s[:, tcc_s:tcc_s + 512],
                     ALU.mult, ALU.mult),
                 reads=[prot, gcr, tb_s], writes=[t2])
            for half in range(2):
                hh = 2 * i4 + half
                r0 = half * 64
                eng = "DVE" if half == 0 else "POOL"
                if isq:
                    qb, qc = self.qbuf(hh)
                    t.op(eng, lambda h, qb=qb, qc=qc, r0=r0, t1=t1, t2=t2: h.tensor_add(
                        qb[r0:r0 + 64, qc:qc + 512], t1[r0:r0 + 64, 512:1024],
                        t2[r0:r0 + 64, 1024:1536]), reads=[t1, t2], writes=[qb])
                else:
                    kb, kc = self.kbuf(hh)
                    t.op(eng, lambda h, kb=kb, kc=kc, r0=r0, t1=t1, t2=t2: h.tensor_add(
                        kb[r0:r0 + 64, kc + 512:kc + 1024], t1[r0:r0 + 64, 512:1024],
                        t2[r0:r0 + 64, 1024:1536]), reads=[t1, t2], writes=[kb])
        for hh in range(HEADS):
            vb, vc = self.vtbuf(hh)
            vr0 = (hh % 2) * 64
            vcol = (hh // 2) * 512
            fns = []
            for jt in range(4):
                fns.append(lambda h, jt=jt, vr0=vr0, vcol=vcol: h.transpose(
                    self.PT[:, jt * 64:jt * 64 + 64],
                    vraw[vr0:vr0 + 64, 2048 + vcol + jt * 128:2048 + vcol + (jt + 1) * 128],
                    self.ident_bf[vr0:vr0 + 64, vr0:vr0 + 64]))
            t.group("PE", fns, reads=[vraw, self.ident_bf], writes=[self.PT])
            for jt in range(4):
                c = vc + 260 + jt * 65
                t.op("DVE", lambda h, jt=jt, vb=vb, c=c: h.tensor_copy(
                    vb[:, c:c + 64], self.PT[:, jt * 64:jt * 64 + 64]),
                    reads=[self.PT], writes=[vb])
        if "qkv" in taps:
            self.tap("qkv_qraw", [qraw], cols=T)
            self.tap("qkv_v", [self.WK[1]], cols=T)
        ptail = None
        for hh in range(HEADS):
            ptail = self.ffn_head(l, w, hh, jts, prev_tail=ptail)
        ptail()
        if "qkv" in taps:
            self.tap("qbuf", [self.XM[0]], cols=T)
            self.tap("kbuf", [self.XM[1], self.XM[2]], cols=T)
        if "ofin" in taps:
            self.tap("ofin", [self.XM[5]], cols=T)
        for dt_ in range(4, 8):
            t.op("DVE", lambda h, dt_=dt_: h.tensor_copy(
                self.BCOLW[:, dt_ - 4:dt_ - 3], self.XS[dt_][:, c0 + 511:c0 + 512]),
                reads=[self.XS[dt_]], writes=[self.BCOLW])
        ofin = self.XM[5]
        for dt_ in range(NDT):
            ps = self.PB[dt_ % 2]
            pairs = []
            for hh in range(HEADS):
                pairs.append((self.WO8[hh][0:64, dt_ * 128:(dt_ + 1) * 128],
                              ofin[0:64, hh * 512:(hh + 1) * 512]))
            self.mm(ps[:, 0:512], pairs, ps,
                    reads=[self.WO8[hh2] for hh2 in range(HEADS)] + [ofin])
            t.op("DVE", lambda h, dt_=dt_, ps=ps, c0=c0: h.tensor_add(
                self.XS[dt_][:, c0:c0 + 512], self.XS[dt_][:, c0:c0 + 512], ps[:, 0:512]),
                reads=[ps, self.XS[dt_]], writes=[self.XS[dt_]])

    def ffn_head(self, l, w, hh, jts, prev_tail=None):
        t, din = self.t, self.din
        qb, qc = self.qbuf(hh)
        kb, kc = self.kbuf(hh)
        vb, vc = self.vtbuf(hh)
        PO = self.PB[4]

        def emit_scores(it):
            mask, mo = self.masks["mask_mid"], 0
            nblank = max(0, 512 - it * 128) if w == 0 else 0
            if it % 2 == 0:
                pslo, plc, pshi, phc = self.PW, 0, self.PW, 512
            else:
                pslo, plc, pshi, phc = self.PB[0], 0, self.PB[1], 0
            negm = self.SC[it]
            fns = [
                lambda h, it=it, pslo=pslo, plc=plc: h.matmul(
                    pslo[:, plc:plc + 512],
                    qb[:, qc + it * 128:qc + (it + 1) * 128],
                    kb[:, kc + it * 128:kc + it * 128 + 512],
                    start=True, stop=False),
                lambda h, it=it, pshi=pshi, phc=phc: h.matmul(
                    pshi[:, phc:phc + 128],
                    qb[:, qc + it * 128:qc + (it + 1) * 128],
                    kb[:, kc + it * 128 + 512:kc + it * 128 + 640],
                    start=True, stop=False),
                lambda h, mask=mask, mo=mo, pslo=pslo, plc=plc, nblank=nblank: h.matmul(
                    pslo[:, plc:plc + 512], self.ident_bf[:, :],
                    mask[:, mo:mo + 512], start=False, stop=(nblank == 0)),
                lambda h, mask=mask, mo=mo, pshi=pshi, phc=phc: h.matmul(
                    pshi[:, phc:phc + 128], self.ident_bf[:, :],
                    mask[:, mo + 512:mo + 640], start=False, stop=True),
            ]
            if nblank > 0:
                fns.append(lambda h, pslo=pslo, plc=plc, nblank=nblank: h.matmul(
                    pslo[:, plc:plc + 512], self.ident_bf[:, :],
                    self.negbig_bf[:, 512 - nblank:1024 - nblank],
                    start=False, stop=True))
            wl = [self.PW] if it % 2 == 0 else [self.PB[0], self.PB[1]]
            rl = [qb, kb, self.ident_bf, mask] + ([self.negbig_bf] if nblank else [])
            t.group("PE", fns, reads=rl, writes=wl)
            ES, eb = self.es_slot(it)
            if it % 2 == 0:
                t.op("DVE", lambda h, it=it, negm=negm: h.reduce_max(
                    negm[:, 0:1], self.PW[:, 0:640], axis=AX.X, negate=True),
                    reads=[self.PW], writes=[negm])
                t.op("ACT", lambda h, it=it, ES=ES, eb=eb, negm=negm: h.activation(
                    ES[:, eb:eb + 640], self.PW[:, 0:640], AF.Exp, bias=negm[:, 0:1]),
                    reads=[self.PW, negm], writes=[ES])
            else:
                t.op("DVE", lambda h, it=it, negm=negm: h.reduce_max(
                    negm[:, 0:1], self.PB[0][:, 0:512], axis=AX.X, negate=True),
                    reads=[self.PB[0]], writes=[negm])
                t.op("DVE", lambda h, it=it, negm=negm: h.reduce_max(
                    negm[:, 1:2], self.PB[1][:, 0:128], axis=AX.X, negate=True),
                    reads=[self.PB[1]], writes=[negm])
                t.op("DVE", lambda h, it=it, negm=negm: h.tensor_tensor(
                    negm[:, 0:1], negm[:, 0:1], negm[:, 1:2],
                    ALU.min), reads=[negm], writes=[negm])
                t.op("ACT", lambda h, it=it, ES=ES, eb=eb, negm=negm: h.activation(
                    ES[:, eb:eb + 512], self.PB[0][:, 0:512], AF.Exp, bias=negm[:, 0:1]),
                    reads=[self.PB[0], negm], writes=[ES])
                t.op("ACT", lambda h, it=it, ES=ES, eb=eb, negm=negm: h.activation(
                    ES[:, eb + 512:eb + 640], self.PB[1][:, 0:128], AF.Exp,
                    bias=negm[:, 0:1]),
                    reads=[self.PB[1], negm], writes=[ES])

        AT = self.XM[7]

        def emit_tr(it):
            ES, eb = self.es_slot(it)
            fns = []
            for jt in range(5):
                fns.append(lambda h, jt=jt, ES=ES, eb=eb: h.transpose(
                    self.PT[:, jt * 128:(jt + 1) * 128],
                    ES[:, eb + jt * 128:eb + (jt + 1) * 128], self.ident_bf[:, :]))
            t.group("PE", fns, reads=[ES, self.ident_bf], writes=[self.PT])
            eng = "ACT" if it % 2 == 0 else "DVE"
            fn = (lambda h, it=it: h.copy(
                bass.AP(AT, it * 640, [[COLS, 128], [512, 5], [1, 128]]),
                self.PT[:, 0:640])) if it % 2 == 0 else (lambda h, it=it: h.tensor_copy(
                bass.AP(AT, it * 640, [[COLS, 128], [512, 5], [1, 128]]),
                self.PT[:, 0:640]))
            t.op(eng, fn, reads=[self.PT], writes=[AT])

        def emit_av():
            fns = [lambda h: h.matmul(PO[0:65, 0:512], vb[:, vc:vc + 65],
                                      self.negbig_bf[:, 512:1024],
                                      start=True, stop=False)]
            for jg in range(8):
                i0 = max(0, jg - 4)
                i1 = min(3, jg) + 1
                half = 0 if jg < 4 else 1
                cc = vc + half * 260 + (jg % 4) * 65
                fns.append(lambda h, jg=jg, i0=i0, i1=i1, cc=cc: h.matmul(
                    PO[0:65, i0 * 128:i1 * 128], vb[:, cc:cc + 65],
                    AT[:, jg * 512 + i0 * 128:jg * 512 + i1 * 128],
                    start=False, stop=False))
            fns.append(lambda h: h.matmul(PO[0:65, 0:512], vb[:, vc:vc + 65],
                                          self.negbig_bf[:, 512:1024],
                                          start=False, stop=True))
            t.group("PE", fns, reads=[vb, AT, self.negbig_bf], writes=[PO])

        emit_scores(0)
        emit_scores(1)
        emit_scores(2)
        emit_tr(0)
        emit_scores(3)
        if prev_tail is not None:
            prev_tail()
        emit_tr(1)
        emit_tr(2)
        emit_tr(3)
        emit_av()

        def tail():
            zrow = self.WK[0]
            t.op("ACT", lambda h: h.activation(self.FS[64:65, 0:512], PO[64:65, 0:512],
                                               AF.Ln), reads=[PO], writes=[self.FS])
            t.op("ACT", lambda h: h.activation(zrow[64:65, 3584:4096],
                                               self.FS[64:65, 0:512],
                                               AF.Exp, scale=-1.0),
                 reads=[self.FS], writes=[zrow])
            zp = self.PB[2]
            self.mm(zp[0:64, 0:512],
                    [(self.l1_bf[64:65, 64:128], zrow[64:65, 3584:4096])],
                    zp, reads=[self.l1_bf, zrow])
            zrep = self.WK[0]
            t.op("ACT", lambda h, zp=zp: h.copy(zrep[0:64, 3072:3584], zp[0:64, 0:512]),
                 reads=[zp], writes=[zrep])
            ob, oc = self.ofin_buf(hh)
            t.op("DVE", lambda h, ob=ob, oc=oc: h.tensor_mul(
                ob[0:64, oc:oc + 512], PO[0:64, 0:512], zrep[0:64, 3072:3584]),
                reads=[PO, zrep], writes=[ob])
        return tail

    def ffn_mlp(self, l):
        t, din = self.t, self.din
        self.ln_stats(self.XS, shifted=True, sq_scratch=[self.XM[i] for i in range(NDT)])
        t.op("DVE", lambda h: h.tensor_copy(
            self.lgb_bf[:, 0:16:2], self.lnb_col[:, 0:8]), reads=[self.lnb_col],
            writes=[self.lgb_bf])
        t.op("DVE", lambda h: h.tensor_copy(
            self.lgb_bf[:, 1:16:2], self.lng_col[:, 0:8]), reads=[self.lng_col],
            writes=[self.lgb_bf])
        for tcp in range(4):
            self.ffn_mlp_tcpair(l, 2 * tcp, 2 * tcp + 1)

    def mstB(self, j):
        return [self.XM[3], self.XM[4], self.XM[5]][j // 8], (j % 8) * 512

    def ffn_mlp_tcpair(self, l, ta, tb):
        t, din = self.t, self.din
        first_tc = ta == 0
        Z3 = [self.WK[0], self.WK[1]]
        rrepb = self.G8[6]
        mrow, rrow = self.RS[1], self.RS[0]
        for idx, tc in ((0, ta), (1, tb)):
            c0 = 1 + tc * WIN
            z3 = Z3[idx]
            self.cols_to_row(self.SC[3], tc * 4, 4, rrow, self.PB[0])
            self.bcast_row(rrow, 512, self.PB[1])
            t.op("DVE", lambda h, idx=idx: h.tensor_copy(
                rrepb[:, idx * 512:idx * 512 + 512], self.PB[1][:, 0:512]),
                reads=[self.PB[1]], writes=[rrepb])
            self.cols_to_row(self.SC[2], tc * 4, 4, mrow, self.PB[0])
            t.op("DVE", lambda h: h.tensor_scalar_mul(mrow[0:1, 0:512],
                                                      mrow[0:1, 0:512], -1.0),
                 reads=[mrow], writes=[mrow])
            self.bcast_row(mrow, 512, self.PW)
            for dt_ in range(NDT):
                if dt_ < 4 or idx == 1 or tc == 0:
                    cs = c0 if dt_ < 4 else c0 - 1
                    t.op("DVE", lambda h, dt_=dt_, cs=cs, z3=z3, idx=idx:
                         h.scalar_tensor_tensor(
                             z3[:, dt_ * 512:(dt_ + 1) * 512], self.XS[dt_][:, cs:cs + 512],
                             self.lng_col[:, dt_:dt_ + 1],
                             rrepb[:, idx * 512:idx * 512 + 512], ALU.mult, ALU.mult),
                         reads=[self.XS[dt_], self.lng_col, rrepb], writes=[z3])
                else:
                    t.op("DVE", lambda h, dt_=dt_, z3=z3, idx=idx: h.scalar_tensor_tensor(
                        z3[:, dt_ * 512:dt_ * 512 + 1], self.BCOLM[:, dt_ - 4:dt_ - 3],
                        self.lng_col[:, dt_:dt_ + 1],
                        rrepb[:, idx * 512:idx * 512 + 1], ALU.mult, ALU.mult),
                        reads=[self.BCOLM, self.lng_col, rrepb], writes=[z3])
                    t.op("DVE", lambda h, dt_=dt_, z3=z3, idx=idx, c0=c0:
                         h.scalar_tensor_tensor(
                             z3[:, dt_ * 512 + 1:(dt_ + 1) * 512],
                             self.XS[dt_][:, c0:c0 + 511],
                             self.lng_col[:, dt_:dt_ + 1],
                             rrepb[:, idx * 512 + 1:idx * 512 + 512], ALU.mult, ALU.mult),
                         reads=[self.XS[dt_], self.lng_col, rrepb], writes=[z3])
                t.op("DVE", lambda h, dt_=dt_, z3=z3: h.scalar_tensor_tensor(
                    z3[:, dt_ * 512:(dt_ + 1) * 512], self.PW[:, 0:512],
                    self.lng_col[:, dt_:dt_ + 1], z3[:, dt_ * 512:(dt_ + 1) * 512],
                    ALU.mult, ALU.add),
                    reads=[self.PW, self.lng_col, z3], writes=[z3])
        GG = self.G8[7]
        for j in range(NJ):
            PR = self.XM[6] if j % 2 == 0 else self.XM[7]
            for ag in range(2):
                t.dma("POOL" if ag == 0 else "SP",
                      lambda h, j=j, ag=ag, PR=PR: h.dma_start(
                          PR[:, ag * 1024:(ag + 1) * 1024], self.d_w1b[l, j, ag]),
                      reads=[self.d_w1b], writes=[PR])
            if first_tc:
                for ag, (bias, b1c) in enumerate(((self.BIASA, self.b1a_col),
                                                  (self.BIASG, self.b1g_col))):
                    pc = self.PB[4]
                    pairs = [(PR[:, ag * 1024 + dt_ * 128:ag * 1024 + (dt_ + 1) * 128],
                              self.lgb_bf[:, 2 * dt_:2 * dt_ + 1])
                             for dt_ in range(NDT)]
                    self.mm(pc[0:128, 1:2], pairs, pc, reads=[self.lgb_bf, PR])
                    t.op("DVE", lambda h, j=j, bias=bias, b1c=b1c: h.scalar_tensor_tensor(
                        bias[:, j:j + 1], pc[0:128, 1:2], 1.0, b1c[:, j:j + 1],
                        ALU.mult, ALU.add), reads=[pc, b1c], writes=[bias])
            for ag in range(2):
                psa, psb = (self.PB[0], self.PB[1]) if ag == 0 else (self.PB[2], self.PB[3])
                fns = []
                for dt_ in range(NDT):
                    for k, ps in ((0, psa), (1, psb)):
                        fns.append(lambda h, ag=ag, dt_=dt_, ps=ps, PR=PR, k=k: h.matmul(
                            ps[:, 0:512],
                            PR[:, ag * 1024 + dt_ * 128:ag * 1024 + (dt_ + 1) * 128],
                            Z3[k][:, dt_ * 512:(dt_ + 1) * 512],
                            start=(dt_ == 0), stop=(dt_ == NDT - 1)))
                t.group("PE", fns, reads=[PR, Z3[0], Z3[1]], writes=[psa, psb])
            for k in range(2):
                psa = self.PB[0] if k == 0 else self.PB[1]
                psg = self.PB[2] if k == 0 else self.PB[3]
                t.op("ACT", lambda h, j=j, psg=psg, k=k: h.activation(
                    GG[:, k * 512:(k + 1) * 512], psg[:, 0:512], AF.Gelu,
                    bias=self.BIASG[:, j:j + 1]),
                    reads=[psg, self.BIASG], writes=[GG])
                mb, mc = self.mst(j) if k == 0 else self.mstB(j)
                t.op("DVE", lambda h, j=j, mb=mb, mc=mc, psa=psa, k=k:
                     h.scalar_tensor_tensor(
                         mb[:, mc:mc + 512], psa[:, 0:512], self.BIASA[:, j:j + 1],
                         GG[:, k * 512:(k + 1) * 512], ALU.add, ALU.mult),
                     reads=[psa, self.BIASA, GG], writes=[mb])
        for idx, tc in ((0, ta), (1, tb)):
            c0 = 1 + tc * WIN
            for grp in range(2):
                for j in range(NJ):
                    wb = self.XM[7] if j % 2 == 0 else self.XM[6]
                    t.dma("POOL" if j % 2 == 0 else "SP",
                          lambda h, j=j, grp=grp, wb=wb: h.dma_start(
                              wb[:, 0:512], self.d_w2b[l, j, grp]),
                          reads=[self.d_w2b], writes=[wb])
                    mb, mc = self.mst(j) if idx == 0 else self.mstB(j)
                    fns = []
                    pss = []
                    for k, dt_ in enumerate(range(grp * 4, grp * 4 + 4)):
                        ps = self.PB[k]
                        pss.append(ps)
                        fns.append(lambda h, j=j, dt_=dt_, ps=ps, wb=wb, mb=mb, mc=mc,
                                   grp=grp:
                                   h.matmul(ps[:, 0:512],
                                            wb[:, (dt_ - grp * 4) * 128:
                                               (dt_ - grp * 4 + 1) * 128],
                                            mb[:, mc:mc + 512],
                                            start=(j == 0), stop=(j == NJ - 1)))
                    t.group("PE", fns, reads=[wb, mb], writes=pss)
                if grp == 1 and idx == 1:
                    for dt_ in range(4, 8):
                        t.op("DVE", lambda h, dt_=dt_, c0=c0: h.tensor_copy(
                            self.BCOLM[:, dt_ - 4:dt_ - 3],
                            self.XS[dt_][:, c0 + 511:c0 + 512]),
                            reads=[self.XS[dt_]], writes=[self.BCOLM])
                for k, dt_ in enumerate(range(grp * 4, grp * 4 + 4)):
                    ps = self.PB[k]
                    t.op("DVE", lambda h, dt_=dt_, ps=ps, c0=c0: h.scalar_tensor_tensor(
                        self.XS[dt_][:, c0:c0 + 512], ps[:, 0:512],
                        self.b2_col[:, dt_:dt_ + 1], self.XS[dt_][:, c0:c0 + 512],
                        ALU.add, ALU.add), reads=[ps, self.b2_col, self.XS[dt_]],
                        writes=[self.XS[dt_]])

    def attn_a(self):
        t, din = self.t, self.din
        taps = os.environ.get("MK_TAPS", "")
        self.ln_stats(self.XM, shifted=False, sq_scratch=self.XS)
        self.ln_apply(self.XM, self.XS)
        if "aln" in taps:
            self.tap("aln", self.XS)
        self.spill(self.XM, self.d_xmod)
        self.attn_a_fold()
        for ft in range(5):
            self.attn_a_ft(ft, taps if ft == 0 else "")
        self.attn_a_ifft(taps)
        for dt_ in range(NDT):
            t.op("DVE", lambda h, dt_=dt_: h.tensor_copy(
                self.XM[dt_][:, 0:COLS], self.XS[dt_][:, 0:COLS]),
                reads=[self.XS[dt_]], writes=[self.XM[dt_]])

    def attn_a_fold(self):
        t, din = self.t, self.din
        bcol = self.bcol_bf
        def dslot(ci, ec):
            return self.XM[ci * 4 + ec // 2], (ec % 2) * 640
        for ci, dnm in enumerate(("dft_cr", "dft_ci")):
            for ec in range(NDT):
                xb, xc = dslot(ci, ec)
                t.op("DVE", lambda h, xb=xb, xc=xc: h.memset(
                    xb[:, xc + 513:xc + 640], 0.0), writes=[xb])
                t.dma("POOL", lambda h, dnm=dnm, ec=ec, xb=xb, xc=xc: h.dma_start(
                    xb[:, xc:xc + 513], din[dnm][ec * 128:(ec + 1) * 128, :]),
                    writes=[xb])
        for wnm, bnm, cbase in (("Wq", "bq", 0), ("Wk", "bk", 2), ("Wv", "bv", 4)):
            for ec in range(NDT):
                t.dma("POOL", lambda h, wnm=wnm, ec=ec: h.dma_start(
                    self.G8[ec][:, 0:1024], din[wnm][ec * 128:(ec + 1) * 128, :]),
                    writes=[self.G8[ec]])
                t.dma("POOL", lambda h, bnm=bnm, ec=ec: h.dma_start(
                    bcol[:, ec:ec + 1],
                    bass.AP(din[bnm], ec * 128, [[1, 128], [1, 1]])),
                    writes=[bcol])
            for ci in range(2):
                comp = cbase + ci
                dnm = "dft_cr" if ci == 0 else "dft_ci"
                for half in range(4):
                    dts = (2 * half, 2 * half + 1)
                    for ec in range(NDT):
                        xb, xc = dslot(ci, ec)
                        if half == 0:
                            self.mm(self.PW[0:1, 0:512],
                                    [(bcol[:, ec:ec + 1], xb[:, xc:xc + 512])], self.PW,
                                    reads=[bcol, xb], start=(ec == 0), stop=(ec == 7))
                            self.mm(self.PW[0:1, 512:640],
                                    [(bcol[:, ec:ec + 1], xb[:, xc + 512:xc + 640])],
                                    self.PW,
                                    reads=[bcol, xb], start=(ec == 0), stop=(ec == 7))
                        for k, dt_ in enumerate(dts):
                            ps = self.PB[k]
                            self.mm(ps[:, 0:512],
                                    [(self.G8[ec][:, dt_ * 128:(dt_ + 1) * 128],
                                      xb[:, xc:xc + 512])],
                                    ps, reads=[self.G8[ec], xb], start=(ec == 0),
                                    stop=(ec == 7))
                            ps2 = self.PB[2 + k]
                            self.mm(ps2[:, 0:128],
                                    [(self.G8[ec][:, dt_ * 128:(dt_ + 1) * 128],
                                      xb[:, xc + 512:xc + 640])],
                                    ps2, reads=[self.G8[ec], xb], start=(ec == 0),
                                    stop=(ec == 7))
                    for k, dt_ in enumerate(dts):
                        out = self.WK[0]
                        t.op("DVE", lambda h, k=k: h.tensor_copy(
                            out[:, 0:512], self.PB[k][:, 0:512]),
                            reads=[self.PB[k]], writes=[out])
                        t.op("DVE", lambda h, k=k: h.tensor_copy(
                            out[:, 512:640], self.PB[2 + k][:, 0:128]),
                            reads=[self.PB[2 + k]], writes=[out])
                        t.dma("ACT", lambda h, comp=comp, dt_=dt_: h.dma_start(
                            self.d_wf[comp, dt_, :, :], out[:, 0:640]), reads=[out], writes=[self.d_wf])
                t.op("ACT", lambda h: h.copy(self.RS[0][0:1, 0:512], self.PW[0:1, 0:512]),
                     reads=[self.PW], writes=[self.RS[0]])
                t.op("ACT", lambda h: h.copy(self.RS[1][0:1, 0:128], self.PW[0:1, 512:640]),
                     reads=[self.PW], writes=[self.RS[1]])
                fns = []
                for ftk in range(4):
                    fns.append(lambda h, ftk=ftk: h.transpose(
                        self.PB[4][:, ftk:ftk + 1],
                        self.RS[0][0:1, ftk * 128:(ftk + 1) * 128],
                        self.ident_f[0:1, 0:1]))
                fns.append(lambda h: h.transpose(
                    self.PB[4][:, 4:5], self.RS[1][0:1, 0:128], self.ident_f[0:1, 0:1]))
                t.group("PE", fns, reads=[self.RS[0], self.RS[1], self.ident_f],
                        writes=[self.PB[4]])
                t.op("DVE", lambda h, comp=comp: h.tensor_copy(
                    self.bhat_col[:, comp * 5:comp * 5 + 5], self.PB[4][:, 0:5]),
                    reads=[self.PB[4]], writes=[self.bhat_col])

    def attn_a_ft(self, ft, taps=""):
        t, din = self.t, self.din
        XM = self.XM
        for comp in range(6):
            for dt_ in range(NDT):
                t.dma("POOL", lambda h, comp=comp, dt_=dt_, ft=ft: h.dma_start(
                    self.G8[dt_][:, comp * 128:(comp + 1) * 128],
                    self.d_wf[comp, dt_, :, ft * 128:(ft + 1) * 128]),
                    reads=[self.d_wf], writes=[self.G8[dt_]])
        for comp in range(6):
            for tc in range(NW):
                ps = self.PB[(comp + tc) % 2]
                pairs = [(self.G8[dt_][:, comp * 128:(comp + 1) * 128],
                          self.XS[dt_][:, 1 + tc * 512:1 + (tc + 1) * 512])
                         for dt_ in range(NDT)]
                self.mm(ps[:, 0:512], pairs, ps, reads=self.G8 + self.XS)
                t.op("DVE", lambda h, comp=comp, tc=tc, ps=ps, ft=ft: h.tensor_scalar_add(
                    XM[comp][:, tc * 512:(tc + 1) * 512], ps[:, 0:512],
                    self.bhat_col[:, comp * 5 + ft:comp * 5 + ft + 1]),
                    reads=[ps, self.bhat_col], writes=[XM[comp]])
        if "proj" in taps:
            self.tap("proj", [XM[i] for i in range(6)], cols=T)
        m2s = [self.WK[0], self.WK[1], XM[6]]
        TMP = XM[7]
        for s in range(3):
            r_, i_ = XM[2 * s], XM[2 * s + 1]
            m2 = m2s[s]
            t.op("ACT", lambda h, r_=r_, m2=m2: h.activation(
                m2[:, 0:T], r_[:, 0:T], AF.Square), reads=[r_], writes=[m2])
            t.op("ACT", lambda h, i_=i_: h.activation(
                TMP[:, 0:T], i_[:, 0:T], AF.Square), reads=[i_], writes=[TMP])
            t.op("DVE", lambda h, m2=m2: h.tensor_add(
                m2[:, 0:T], m2[:, 0:T], TMP[:, 0:T]), reads=[m2, TMP], writes=[m2])
            t.op("DVE", lambda h, m2=m2: h.tensor_scalar_max(
                m2[:, 0:T], m2[:, 0:T], 1e-38), reads=[m2], writes=[m2])
        for s in range(3):
            t.op("ACT", lambda h, s=s: h.activation(
                m2s[s][:, 0:T], m2s[s][:, 0:T], AF.Sqrt), reads=[m2s[s]], writes=[m2s[s]])
        for s in range(3):
            t.op("ACT", lambda h, s=s: h.activation(
                m2s[s][:, 0:T], m2s[s][:, 0:T], AF.Ln, bias=self.eps_hrr[:, :]),
                reads=[m2s[s], self.eps_hrr], writes=[m2s[s]])
        for s in range(3):
            t.op("ACT", lambda h, s=s: h.activation(
                m2s[s][:, 0:T], m2s[s][:, 0:T], AF.Exp, scale=-1.0),
                reads=[m2s[s]], writes=[m2s[s]])
        for s in range(3):
            for c in range(2):
                b = XM[2 * s + c]
                t.op("DVE", lambda h, b=b, s=s: h.tensor_mul(
                    b[:, 0:T], b[:, 0:T], m2s[s][:, 0:T]),
                    reads=[b, m2s[s]], writes=[b])
        if "unorm" in taps:
            self.tap("unorm", [XM[i] for i in range(6)], cols=T)
        t.op("DVE", lambda h: h.tensor_mul(TMP[:, 0:T], XM[2][:, 0:T], XM[4][:, 0:T]),
             reads=[XM[2], XM[4]], writes=[TMP])
        t.op("DVE", lambda h: h.tensor_mul(self.WK[0][:, 0:T], XM[3][:, 0:T], XM[5][:, 0:T]),
             reads=[XM[3], XM[5]], writes=[self.WK[0]])
        t.op("DVE", lambda h: h.tensor_mul(self.WK[1][:, 0:T], XM[2][:, 0:T], XM[5][:, 0:T]),
             reads=[XM[2], XM[5]], writes=[self.WK[1]])
        t.op("DVE", lambda h: h.tensor_sub(XM[6][:, 0:T], TMP[:, 0:T], self.WK[0][:, 0:T]),
             reads=[TMP, self.WK[0]], writes=[XM[6]])
        t.op("DVE", lambda h: h.tensor_mul(TMP[:, 0:T], XM[3][:, 0:T], XM[4][:, 0:T]),
             reads=[XM[3], XM[4]], writes=[TMP])
        t.op("DVE", lambda h: h.tensor_add(XM[7][:, 0:T], self.WK[1][:, 0:T], TMP[:, 0:T]),
             reads=[self.WK[1], TMP], writes=[XM[7]])
        if "bind" in taps:
            self.tap("bind", [XM[6], XM[7]], cols=T)
        for srcb, dstb in ((XM[6], XM[2]), (XM[7], XM[3]), (XM[0], XM[4]), (XM[1], XM[5])):
            self.transpose_full(srcb, dstb)
        CROW = self.RS[0]
        t.op("ACT", lambda h: h.memzero(CROW[0:1, 0:256]), writes=[CROW])
        psI = self.PB[0]
        psC = self.PB[1]
        for tt in range(32):
            fns = [
                lambda h, tt=tt: h.matmul(psI[:, 0:128], self.l1_bf[:, :],
                                          XM[2][:, tt * 128:(tt + 1) * 128],
                                          start=True, stop=False),
                lambda h, tt=tt: h.matmul(psI[:, 128:256], self.l1_bf[:, :],
                                          XM[3][:, tt * 128:(tt + 1) * 128],
                                          start=False, stop=False),
                lambda h: h.matmul(psI[:, 0:256], self.ones_row_f[0:1, :],
                                   CROW[0:1, 0:256], start=False, stop=True),
            ]
            t.group("PE", fns, reads=[self.l1_bf, XM[2], XM[3], self.ones_row_f, CROW],
                    writes=[psI])
            fns = [
                lambda h, tt=tt: h.matmul(psC[0:1, 0:128], self.ones_col_bf[:, :],
                                          XM[2][:, tt * 128:(tt + 1) * 128],
                                          start=True, stop=False),
                lambda h, tt=tt: h.matmul(psC[0:1, 128:256], self.ones_col_bf[:, :],
                                          XM[3][:, tt * 128:(tt + 1) * 128],
                                          start=False, stop=True),
            ]
            t.group("PE", fns, reads=[self.ones_col_bf, XM[2], XM[3]], writes=[psC])
            t.op("DVE", lambda h, tt=tt: h.tensor_copy(
                XM[2][:, tt * 128:(tt + 1) * 128], psI[:, 0:128]),
                reads=[psI], writes=[XM[2]])
            t.op("DVE", lambda h, tt=tt: h.tensor_copy(
                XM[3][:, tt * 128:(tt + 1) * 128], psI[:, 128:256]),
                reads=[psI], writes=[XM[3]])
            t.op("DVE", lambda h: h.tensor_add(CROW[0:1, 0:256], CROW[0:1, 0:256],
                                               psC[0:1, 0:256]),
                 reads=[psC, CROW], writes=[CROW])
        if "cums" in taps:
            self.tap("cums", [XM[2], XM[3]], cols=T)
        t.op("DVE", lambda h: h.tensor_mul(self.WK[0][:, 0:T], XM[2][:, 0:T], XM[4][:, 0:T]),
             reads=[XM[2], XM[4]], writes=[self.WK[0]])
        t.op("DVE", lambda h: h.tensor_mul(self.WK[1][:, 0:T], XM[3][:, 0:T], XM[5][:, 0:T]),
             reads=[XM[3], XM[5]], writes=[self.WK[1]])
        t.op("DVE", lambda h: h.tensor_mul(TMP[:, 0:T], XM[3][:, 0:T], XM[4][:, 0:T]),
             reads=[XM[3], XM[4]], writes=[TMP])
        t.op("DVE", lambda h: h.tensor_add(self.WK[0][:, 0:T], self.WK[0][:, 0:T],
                                           self.WK[1][:, 0:T]),
             reads=[self.WK[0], self.WK[1]], writes=[self.WK[0]])
        t.op("DVE", lambda h: h.tensor_mul(self.WK[1][:, 0:T], XM[2][:, 0:T], XM[5][:, 0:T]),
             reads=[XM[2], XM[5]], writes=[self.WK[1]])
        t.op("DVE", lambda h: h.tensor_sub(TMP[:, 0:T], TMP[:, 0:T], self.WK[1][:, 0:T]),
             reads=[TMP, self.WK[1]], writes=[TMP])
        self.transpose_full(self.WK[0], XM[0])
        self.transpose_full(TMP, XM[1])
        t.dma("ACT", lambda h, ft=ft: h.dma_start(self.d_wT[0, ft, :, :], XM[0][:, 0:T]),
              reads=[XM[0]], writes=[self.d_wT])
        t.dma("ACT", lambda h, ft=ft: h.dma_start(self.d_wT[1, ft, :, :], XM[1][:, 0:T]),
              reads=[XM[1]], writes=[self.d_wT])

    def transpose_full(self, srcb, dstb):
        t = self.t
        for g in range(4):
            fns = []
            for k in range(8):
                tt = g * 8 + k
                fns.append(lambda h, tt=tt, k=k: h.transpose(
                    self.PT[:, k * 128:(k + 1) * 128],
                    srcb[:, tt * 128:(tt + 1) * 128], self.ident_bf[:, :]))
            t.group("PE", fns, reads=[srcb, self.ident_bf], writes=[self.PT])
            t.op("DVE", lambda h, g=g: h.tensor_copy(
                dstb[:, g * 1024:(g + 1) * 1024], self.PT[:, 0:1024]),
                reads=[self.PT], writes=[dstb])

    def attn_a_ifft(self, taps=""):
        t, din = self.t, self.din
        XM = self.XM
        for fc in range(8):
            t.dma("POOL", lambda h, fc=fc: h.dma_start(
                self.G8[fc][:, 0:1024], din["ifft_b"][fc * 128:(fc + 1) * 128, :]),
                writes=[self.G8[fc]])
        for fc in (8, 9):
            for dt_ in range(NDT):
                t.dma("POOL", lambda h, fc=fc, dt_=dt_: h.dma_start(
                    self.G8[dt_][:, 1024 + (fc - 8) * 128:1024 + (fc - 7) * 128],
                    din["ifft_b"][fc * 128:(fc + 1) * 128, dt_ * 128:(dt_ + 1) * 128]),
                    writes=[self.G8[dt_]])
        for comp in range(2):
            for ft in range(5):
                fc = comp * 5 + ft
                dst = XM[fc] if fc < 8 else (self.WK[0] if fc == 8 else self.WK[1])
                t.dma("POOL", lambda h, comp=comp, ft=ft, dst=dst: h.dma_start(
                    dst[:, 0:T], self.d_wT[comp, ft, :, :]), reads=[self.d_wT], writes=[dst])
        for dt_ in range(NDT):
            t.dma("ACT", lambda h, dt_=dt_: h.dma_start(
                self.XS[dt_][:, 0:COLS], self.d_xmod[dt_, :, :]), reads=[self.d_xmod], writes=[self.XS[dt_]])
        wt = [XM[0], XM[1], XM[2], XM[3], XM[4], XM[5], XM[6], XM[7], self.WK[0], self.WK[1]]
        for dt_ in range(NDT):
            for tc in range(NW):
                ps = self.PB[(dt_ + tc) % 3]
                fns = []
                for fc in range(10):
                    if fc < 8:
                        lt = self.G8[fc][:, dt_ * 128:(dt_ + 1) * 128]
                    else:
                        lt = self.G8[dt_][:, 1024 + (fc - 8) * 128:1024 + (fc - 7) * 128]
                    fns.append(lambda h, fc=fc, lt=lt, tc=tc, ps=ps: h.matmul(
                        ps[:, 0:512], lt, wt[fc][:, tc * 512:(tc + 1) * 512],
                        start=(fc == 0), stop=(fc == 9)))
                t.group("PE", fns, reads=self.G8 + wt, writes=[ps])
                t.op("DVE", lambda h, dt_=dt_, tc=tc, ps=ps: h.tensor_add(
                    self.XS[dt_][:, 1 + tc * 512:1 + (tc + 1) * 512],
                    self.XS[dt_][:, 1 + tc * 512:1 + (tc + 1) * 512], ps[:, 0:512]),
                    reads=[ps, self.XS[dt_]], writes=[self.XS[dt_]])


    def attn_b(self):
        t, din = self.t, self.din
        taps = os.environ.get("MK_TAPS", "")
        XM = self.XM
        self.ln_stats(self.XM, shifted=False, sq_scratch=self.XS)
        self.ln_apply(self.XM, self.XS)
        if "bln" in taps:
            self.tap("bln", self.XS)
        self.spill(self.XM, self.d_xmod)
        for dt_ in range(NDT):
            t.dma("POOL", lambda h, dt_=dt_: h.dma_start(
                self.G8[dt_][:, 0:16], din["BwT"][dt_ * 128:(dt_ + 1) * 128, :]),
                writes=[self.G8[dt_]])
        t.op("DVE", lambda h: h.memset(self.C16B[0:16, 0:256], 0.0), writes=[self.C16B])
        t.dma("ACT", lambda h: h.dma_start(self.d_u[:, 0:256], self.C16B[0:16, 0:256]),
              reads=[self.C16B], writes=[self.d_u])
        for tc in range(NW):
            ps = self.PB[0]
            pairs = [(self.G8[dt_][:, 0:16], self.XS[dt_][:, 1 + tc * 512:1 + (tc + 1) * 512])
                     for dt_ in range(NDT)]
            self.mm(ps[0:16, 0:512], pairs, ps, reads=self.G8 + self.XS)
            t.op("DVE", lambda h, ps=ps: h.tensor_copy(self.C16B[0:16, 0:512],
                                                       ps[0:16, 0:512]),
                 reads=[ps], writes=[self.C16B])
            t.dma("ACT", lambda h, tc=tc: h.dma_start(
                self.d_u[:, 256 + tc * 512:256 + (tc + 1) * 512], self.C16B[0:16, 0:512]),
                reads=[self.C16B], writes=[self.d_u])
        if "bu" in taps:
            o = self.nc.dram_tensor("dbg_bu0", [S_ST, ULEN], BF16, kind="ExternalOutput")
            self.dbg_out["dbg_bu0"] = o
            t.op("DVE", lambda h: h.memset(self.C16B[0:16, 0:256], 0.0), writes=[self.C16B])
            t.dma("POOL", lambda h: h.dma_start(o[:, :], self.d_u[:, :]), reads=[self.d_u])
        self.spill(self.XS, self.d_ln3)
        snT = self.FS
        CT = self.WK[0]
        for dt_ in range(NDT):
            t.dma("SP", lambda h, dt_=dt_: h.dma_start(
                self.SC[1][:, 0:16], din["logA"][dt_ * 128:(dt_ + 1) * 128, :]),
                writes=[self.SC[1]])
            t.op("ACT", lambda h: h.activation(self.SC[1][:, 16:32], self.SC[1][:, 0:16],
                                               AF.Exp), reads=[self.SC[1]], writes=[self.SC[1]])
            t.op("DVE", lambda h: h.tensor_scalar_mul(self.SC[1][:, 16:32],
                                                      self.SC[1][:, 16:32], -1.0),
                 reads=[self.SC[1]], writes=[self.SC[1]])
            t.group("PE", [lambda h: h.transpose(
                self.PB[1][0:16, 0:128], self.SC[1][:, 16:32], self.ident_f[:, :])],
                reads=[self.SC[1], self.ident_f], writes=[self.PB[1]])
            t.op("ACT", lambda h, dt_=dt_: h.copy(
                snT[0:16, dt_ * 128:(dt_ + 1) * 128], self.PB[1][0:16, 0:128]),
                reads=[self.PB[1]], writes=[snT])
            t.dma("SP", lambda h, dt_=dt_: h.dma_start(
                self.SC[1][:, 32:48], din["Cw"][dt_ * 128:(dt_ + 1) * 128, :]),
                writes=[self.SC[1]])
            t.group("PE", [lambda h: h.transpose(
                self.PB[2][0:16, 0:128], self.SC[1][:, 32:48], self.ident_f[:, :])],
                reads=[self.SC[1], self.ident_f], writes=[self.PB[2]])
            t.op("ACT", lambda h, dt_=dt_: h.copy(
                CT[0:16, dt_ * 128:(dt_ + 1) * 128], self.PB[2][0:16, 0:128]),
                reads=[self.PB[2]], writes=[CT])
        AQ = lambda s: (XM[4 + s // 4], (s % 4) * 1024)
        for s in range(S_ST):
            for half in range(2):
                t.dma("SP", lambda h, s=s, half=half: h.dma_start(
                    self.RS[0][0:1, 0:512], snT[s:s + 1, half * 512:(half + 1) * 512]),
                    reads=[snT], writes=[self.RS[0]])
                t.dma("POOL", lambda h, s=s, half=half: h.dma_start(
                    self.RS[1][0:1, 0:512], CT[s:s + 1, half * 512:(half + 1) * 512]),
                    reads=[CT], writes=[self.RS[1]])
                self.mm(self.PB[3][:, 0:512],
                        [(self.jrev_row[0:1, :], self.RS[0][0:1, 0:512])],
                        self.PB[3], reads=[self.jrev_row, self.RS[0]])
                t.op("ACT", lambda h, half=half: h.activation(
                    CT[:, 1024 + half * 512:1024 + (half + 1) * 512], self.PB[3][:, 0:512],
                    AF.Exp), reads=[self.PB[3]], writes=[CT])
                self.bcast_row(self.RS[1], 512, self.PB[4])
                t.op("DVE", lambda h, half=half: h.tensor_mul(
                    self.WK[1][:, half * 512:(half + 1) * 512],
                    CT[:, 1024 + half * 512:1024 + (half + 1) * 512], self.PB[4][:, 0:512]),
                    reads=[CT, self.PB[4]], writes=[self.WK[1]])
                aqb, aqc = AQ(s)
                self.bcast_row(self.RS[0], 512, self.PB[2])
                t.op("ACT", lambda h, aqb=aqb, aqc=aqc, half=half: h.activation(
                    aqb[:, aqc + half * 512:aqc + (half + 1) * 512], self.PB[2][:, 0:512],
                    AF.Exp, scale=128.0), reads=[self.PB[2]], writes=[aqb])
            t.dma("ACT", lambda h, s=s: h.dma_start(self.d_k0[s, :, :],
                                                    self.WK[1][:, 0:1024]),
                  reads=[self.WK[1]], writes=[self.d_k0])
        if "bk0" in taps:
            self.tap("bk0", [self.WK[1]], cols=1024)
        def k0slot(s):
            if s < 12:
                return self.XS[4 + s // 3], (s % 3) * 1024
            return self.G8[s - 12], 0
        for s in range(S_ST):
            kb, kc = k0slot(s)
            t.dma("POOL" if s % 2 == 0 else "SP",
                  lambda h, s=s, kb=kb, kc=kc: h.dma_start(
                      kb[:, kc:kc + 1024], self.d_k0[s, :, :]),
                  reads=[self.d_k0], writes=[kb])
        for xb in (XM[0], XM[1], XM[2], XM[3]):
            t.op("DVE", lambda h, xb=xb: h.memset(xb[:, 0:T], 0.0), writes=[xb])
        OS = [self.G8[4], self.G8[5], self.G8[7]]
        YBUF = self.G8[6]
        YST = self.WK[1]
        for tg in range(NW):
            for s in range(S_ST):
                ub, uc = self.XS[s // 4], (s % 4) * 512
                t.dma("POOL" if s % 2 == 0 else "SP",
                      lambda h, s=s, tg=tg, ub=ub, uc=uc: h.dma_start(
                          ub[:, uc:uc + 512],
                          bass.AP(self.d_u, s * ULEN + 129 + tg * 512, [[1, 128], [1, 512]])),
                      reads=[self.d_u], writes=[ub])
            for crel in range(4):
                c = tg * 4 + crel
                for s in range(S_ST):
                    zb, zc = XM[s // 4], (s % 4) * 1024
                    kb, kc = k0slot(s)
                    ub, uc = self.XS[s // 4], (s % 4) * 512
                    psA = self.PB[2 * (s % 2)]
                    psB = self.PB[2 * (s % 2) + 1]
                    osl = OS[s % 3]
                    fns = []
                    for half, ps in ((0, psA), (1, psB)):
                        fns.append(lambda h, ps=ps, half=half, ub=ub, uc=uc, crel=crel,
                                   kb=kb, kc=kc: h.matmul(
                            ps[:, 0:512],
                            ub[:, uc + crel * 128:uc + crel * 128 + 128],
                            kb[:, kc + half * 512:kc + (half + 1) * 512],
                            start=True, stop=False))
                    for half, ps in ((0, psA), (1, psB)):
                        fns.append(lambda h, ps=ps, zb=zb, zc=zc, half=half: h.matmul(
                            ps[:, 0:512],
                            self.ident_bf[:, :],
                            zb[:, zc + half * 512:zc + (half + 1) * 512],
                            start=False, stop=True))
                    t.group("PE", fns, reads=[ub, kb, zb, self.ident_bf],
                            writes=[psA, psB])
                    t.op("ACT", lambda h, psA=psA, osl=osl: h.copy(
                        osl[:, 0:512], psA[:, 0:512]), reads=[psA], writes=[osl])
                    t.op("DVE", lambda h, psB=psB, osl=osl: h.tensor_copy(
                        osl[:, 512:1024], psB[:, 0:512]), reads=[psB], writes=[osl])
                    aqb, aqc = AQ(s)
                    t.op("DVE", lambda h, zb=zb, zc=zc, aqb=aqb, aqc=aqc, psA=psA:
                         h.tensor_mul(zb[:, zc:zc + 512], psA[:, 0:512],
                                      aqb[:, aqc:aqc + 512]),
                         reads=[psA, aqb], writes=[zb])
                    t.op("POOL", lambda h, zb=zb, zc=zc, aqb=aqb, aqc=aqc, osl=osl:
                         h.tensor_mul(zb[:, zc + 512:zc + 1024], osl[:, 512:1024],
                                      aqb[:, aqc + 512:aqc + 1024]),
                         reads=[osl, aqb], writes=[zb])
                    if s > 1:
                        sp = s - 2
                        osp = OS[sp % 3]
                        t.group("PE", [
                            lambda h, sp=sp, osp=osp: h.matmul(
                                self.PB[4][:, 0:512], self.ident_bf[:, :],
                                osp[:, 0:512], start=(sp == 0), stop=False),
                            lambda h, sp=sp, osp=osp: h.matmul(
                                self.PW[:, 0:512], self.ident_bf[:, :],
                                osp[:, 512:1024], start=(sp == 0), stop=False)],
                            reads=[osp, self.ident_bf], writes=[self.PB[4], self.PW])
                for sp in (S_ST - 2, S_ST - 1):
                    osp = OS[sp % 3]
                    t.group("PE", [
                        lambda h, sp=sp, osp=osp: h.matmul(
                            self.PB[4][:, 0:512], self.ident_bf[:, :],
                            osp[:, 0:512], start=False, stop=(sp == S_ST - 1)),
                        lambda h, sp=sp, osp=osp: h.matmul(
                            self.PW[:, 0:512], self.ident_bf[:, :],
                            osp[:, 512:1024], start=False, stop=(sp == S_ST - 1))],
                        reads=[osp, self.ident_bf], writes=[self.PB[4], self.PW])
                t.op("ACT", lambda h: h.copy(YBUF[:, 0:512], self.PB[4][:, 0:512]),
                     reads=[self.PB[4]], writes=[YBUF])
                t.op("ACT", lambda h: h.copy(YBUF[:, 512:1024], self.PW[:, 0:512]),
                     reads=[self.PW], writes=[YBUF])
                fns = []
                for dsl in range(NDT):
                    fns.append(lambda h, dsl=dsl: h.transpose(
                        self.PT[:, dsl * 128:(dsl + 1) * 128],
                        YBUF[:, dsl * 128:(dsl + 1) * 128],
                        self.ident_bf[:, :]))
                t.group("PE", fns, reads=[YBUF, self.ident_bf], writes=[self.PT])
                for dsl in range(NDT):
                    t.op("ACT", lambda h, dsl=dsl, crel=crel: h.copy(
                        YST[:, dsl * 512 + crel * 128:dsl * 512 + crel * 128 + 128],
                        self.PT[:, dsl * 128:(dsl + 1) * 128]),
                        reads=[self.PT], writes=[YST])
            for dsl in range(NDT):
                t.dma("ACT", lambda h, dsl=dsl, tg=tg: h.dma_start(
                    self.d_y[dsl, :, tg * 512:(tg + 1) * 512],
                    YST[:, dsl * 512:(dsl + 1) * 512]),
                    reads=[YST], writes=[self.d_y])
        for dt_ in range(NDT):
            t.dma("ACT", lambda h, dt_=dt_: h.dma_start(
                self.WK[1][:, 0:T], self.d_y[dt_, :, :]), reads=[self.d_y],
                writes=[self.WK[1]])
            t.dma("POOL", lambda h, dt_=dt_: h.dma_start(
                self.XS[dt_][:, 0:COLS], self.d_ln3[dt_, :, :]), reads=[self.d_ln3],
                writes=[self.XS[dt_]])
            t.op("DVE", lambda h, dt_=dt_: h.scalar_tensor_tensor(
                self.WK[1][:, 0:T], self.XS[dt_][:, 1:T + 1],
                self.Dvec_col[:, dt_:dt_ + 1], self.WK[1][:, 0:T],
                ALU.mult, ALU.add), reads=[self.XS[dt_], self.Dvec_col, self.WK[1]],
                writes=[self.WK[1]])
            t.dma("ACT", lambda h, dt_=dt_: h.dma_start(
                self.WK[0][:, 0:T], self.d_xmod[dt_, :, 1:T + 1]), reads=[self.d_xmod],
                writes=[self.WK[0]])
            t.op("DVE", lambda h, dt_=dt_: h.tensor_add(
                self.XM[dt_][:, 1:T + 1], self.WK[0][:, 0:T], self.WK[1][:, 0:T]),
                reads=[self.WK[0], self.WK[1]], writes=[self.XM[dt_]])
            t.op("DVE", lambda h, dt_=dt_: h.memset(self.XM[dt_][:, 0:1], 0.0),
                 writes=[self.XM[dt_]])
            t.op("DVE", lambda h, dt_=dt_: h.memset(self.XM[dt_][:, T + 1:COLS], 0.0),
                 writes=[self.XM[dt_]])


    def write_out(self):
        t = self.t
        for dt_ in range(NDT):
            for cc in range(4):
                t.op("DVE", lambda h, dt_=dt_, cc=cc: h.tensor_copy(
                    self.FS[:, 0:1024], self.XM[dt_][:, 1 + cc * 1024:1 + (cc + 1) * 1024]),
                    reads=[self.XM[dt_]], writes=[self.FS])
                t.dma("POOL", lambda h, dt_=dt_, cc=cc: h.dma_start(
                    self.xout[dt_ * 128:(dt_ + 1) * 128, cc * 1024:(cc + 1) * 1024],
                    self.FS[:, 0:1024]), reads=[self.FS])


def kernel(**inputs):
    consts = build_constants()
    g = host_layout(inputs)
    x = np.asarray(inputs["L0_attn_a_x"], np.float32)
    B = x.shape[0]
    mk = MK(consts, stages=os.environ.get("MK_STAGES", "afbf"))
    nc = mk.build()
    in_maps = []
    for b in range(B):
        m = {"xin": np.ascontiguousarray(x[b].T)}
        for nm, arr in g.items():
            m[nm] = arr
        for nm, arr in consts.items():
            m["c_" + nm] = arr
        in_maps.append(m)
    res = run_bass_kernel_spmd(nc, in_maps, core_ids=list(range(B)))
    global _LAST_DBG
    _LAST_DBG = [{k: np.asarray(v) for k, v in r.items() if k.startswith("dbg_")}
                 for r in res.results]
    out = np.stack([np.asarray(r["xout"], np.float32).T for r in res.results])
    return out


_LAST_DBG = None
